# version 04
# speedup vs baseline: 1.4533x; 1.0330x over previous
import numpy as np
import ml_dtypes
import concourse.bass as bass
import concourse.mybir as mybir
from concourse.bass_utils import run_bass_kernel_spmd
from contextlib import ExitStack

F32 = mybir.dt.float32
BF16 = mybir.dt.bfloat16
I32 = mybir.dt.int32
OP = mybir.AluOpType
AF = mybir.ActivationFunctionType
AX = mybir.AxisListType
BFDT = ml_dtypes.bfloat16

D = 1024
T = 2048
NCH = 8
LOGK = float(np.log(np.float32(np.float32(1152.0))))
SIMPROOF = True
STRICT = False

ENGS = ("pe", "act", "dve", "sp", "gp")
DMAS = ("S_X", "S_WA", "S_WB", "S_B", "S_O", "S_Q", "S_G", "S_C")
SERIAL_DMAS = {"S_X", "S_B", "S_O", "S_Q", "S_C"}


class Prog:
    def __init__(self, nc):
        self.nc = nc
        self.strict = STRICT
        self.th = {e: [] for e in ENGS}
        self.cnt = {e: 0 for e in ENGS}
        self.cnt.update({q: 0 for q in DMAS})
        self.waited = {e: {} for e in ENGS}
        self.sems = {}

    def alloc_sems(self, ctx):
        for name in ENGS + DMAS:
            self.sems[name] = ctx.enter_context(self.nc.semaphore(name))

    def _deps(self, eng, deps):
        out = []
        for d in deps:
            if d is None:
                continue
            se, sv = d
            if se == eng:
                continue
            if sv <= self.waited[eng].get(se, 0):
                continue
            self.waited[eng][se] = sv
            out.append((se, sv))
        return out

    def emit(self, eng, fn, deps=(), sem=None, inc=1):
        semname = sem or eng
        if self.strict and getattr(self, "last_tok", None) is not None:
            deps = list(deps) + [self.last_tok]
        deps2 = self._deps(eng, deps)
        self.cnt[semname] += inc
        target = self.cnt[semname]

        def thunk(e, deps2=deps2, fn=fn, semname=semname, inc=inc):
            for (se, sv) in deps2:
                e.wait_ge(self.sems[se], sv)
            last = fn(e)
            if inc and last is not None:
                last.then_inc(self.sems[semname], inc)
        self.th[eng].append(thunk)
        if inc:
            self.last_tok = (semname, target)
        return (semname, target)

    def dma(self, eng, fn, deps=(), sem="S_B", n=1):
        if self.strict and getattr(self, "last_tok", None) is not None:
            deps = list(deps) + [self.last_tok]
        self_wait = None
        if sem in SERIAL_DMAS and self.cnt[sem] > 0:
            self_wait = (sem, self.cnt[sem])
        deps2 = self._deps(eng, list(deps))
        if self_wait is not None and self_wait[1] > self.waited[eng].get(sem, 0):
            self.waited[eng][sem] = self_wait[1]
            deps2.append(self_wait)
        self.cnt[sem] += 16 * n
        target = self.cnt[sem]

        def thunk(e, deps2=deps2, fn=fn, sem=sem, n=n):
            for (se, sv) in deps2:
                e.wait_ge(self.sems[se], sv)
            ins = fn(e)
            assert len(ins) == n, f"dma task expected {n} got {len(ins)}"
            for i in ins:
                i.then_inc(self.sems[sem], 16)
        self.th[eng].append(thunk)
        self.last_tok = (sem, target)
        return (sem, target)


_DTSZ = {F32: 4, BF16: 2, I32: 4}


class VProxy:
    TRACK = 3

    def __init__(self, e):
        object.__setattr__(self, "_e", e)
        object.__setattr__(self, "_hist", [])

    def __getattr__(self, k):
        return getattr(self._e, k)

    def _rng(self, ap):
        try:
            esz = _DTSZ.get(ap.dtype, 4)
            lo = int(ap.offset) * esz
            fb = ap.free_nbytes
            if callable(fb):
                fb = fb()
            return (ap.tensor.name, lo, lo + int(fb))
        except Exception:
            return None

    def _chk(self, ins):
        for ap in ins:
            if ap is None or isinstance(ap, (int, float)):
                continue
            r = self._rng(ap)
            if r is None:
                continue
            for (n, lo, hi) in self._hist:
                if n == r[0] and r[1] < hi and lo < r[2]:
                    self._e.drain()
                    object.__setattr__(self, "_hist", [])
                    return

    def _rec(self, out):
        r = self._rng(out)
        h = ([r] if r else []) + self._hist[:self.TRACK - 1]
        object.__setattr__(self, "_hist", h)

    def tensor_tensor(self, out, in0, in1, op):
        self._chk([in0, in1])
        r = self._e.tensor_tensor(out=out, in0=in0, in1=in1, op=op)
        self._rec(out)
        return r

    def tensor_scalar(self, out, in0, scalar1, scalar2=None, op0=None, op1=None):
        self._chk([in0, scalar1, scalar2])
        kw = dict(out=out, in0=in0, scalar1=scalar1, scalar2=scalar2, op0=op0)
        if op1 is not None:
            kw["op1"] = op1
        r = self._e.tensor_scalar(**kw)
        self._rec(out)
        return r

    def scalar_tensor_tensor(self, out, in0, scalar, in1, op0, op1):
        self._chk([in0, scalar, in1])
        r = self._e.scalar_tensor_tensor(out=out, in0=in0, scalar=scalar, in1=in1,
                                         op0=op0, op1=op1)
        self._rec(out)
        return r

    def tensor_copy(self, out, in_):
        self._chk([in_])
        r = self._e.tensor_copy(out=out, in_=in_)
        self._rec(out)
        return r

    def tensor_reduce(self, out, in_, axis, op, apply_absolute_value=None):
        self._chk([in_])
        kw = dict(out=out, in_=in_, axis=axis, op=op)
        if apply_absolute_value is not None:
            kw["apply_absolute_value"] = apply_absolute_value
        r = self._e.tensor_reduce(**kw)
        self._rec(out)
        return r

    def reciprocal(self, out, in_):
        self._chk([in_])
        self._e.reciprocal(out=out, in_=in_)
        r = self._e.drain()
        object.__setattr__(self, "_hist", [])
        return r

    def tensor_tensor_scan(self, out, data0, data1, initial, op0, op1):
        self._chk([data0, data1])
        r = self._e.tensor_tensor_scan(out=out, data0=data0, data1=data1,
                                       initial=initial, op0=op0, op1=op1)
        self._rec(out)
        self._e.drain()
        object.__setattr__(self, "_hist", [])
        return r

    def memset(self, ap, v):
        r = self._e.memset(ap, v)
        self._rec(ap)
        return r

    def select(self, out, mask, on_true, on_false):
        self._chk([mask, on_true, on_false])
        r = self._e.select(out=out, mask=mask, on_true=on_true, on_false=on_false)
        self._rec(out)
        return r

    def activation(self, out, in_, func, bias=None, scale=None, accum_out=None):
        self._chk([in_, bias])
        kw = dict(out=out, in_=in_, func=func)
        if bias is not None:
            kw["bias"] = bias
        if scale is not None:
            kw["scale"] = scale
        if accum_out is not None:
            kw["accum_out"] = accum_out
        r = self._e.activation(**kw)
        self._rec(out)
        if accum_out is not None:
            self._rec(accum_out)
        return r


def recip_d(e, out, in_):
    e.reciprocal(out=out, in_=in_)
    return e.drain()


def build(nl=4):
    nc = bass.Bass("TRN2", target_bir_lowering=False)
    dt = {}

    def din(name, shape, dtype=F32):
        dt[name] = nc.dram_tensor(name, list(shape), dtype, kind="ExternalInput").ap()

    din("xT", (D, T))
    din("ident", (128, 128), BF16)
    din("tri", (128, 128), BF16)
    din("l0_wq", (D, D)); din("l0_wk", (D, D)); din("l0_wv", (D, D)); din("l0_wo", (D, D))
    din("l0_wgT", (D, 4096)); din("l0_wuT", (D, 4096)); din("l0_wdT", (4096, D))
    if nl >= 2:
        din("l1_wq_a", (D, D)); din("l1_wk_a", (D, D)); din("l1_wv_a", (D, D))
        din("l1_wo", (D, D))
        din("l1_ln_g", (128, 8)); din("l1_ln_b", (128, 8))
        din("l1_wqkv_l", (D, 1536)); din("l1_wout_l", (512, D))
        din("l1_null_q", (128, 8)); din("l1_rt_q", (128, 8))
        din("l1_rms_g", (128, 8))
        din("l1_wq_h", (D, 512)); din("l1_wkv_h", (D, 1024))
        din("l1_nkd", (128, 8), BF16); din("l1_nvaug", (1, 520), BF16)
        din("l1_wout_h", (512, D))
        din("lmask", (128, 256), BF16)

    if nl >= 3:
        din("l2_wq_a", (D, D)); din("l2_wk_a", (D, D)); din("l2_wv_a", (D, D))
        din("l2_wproj", (D, D)); din("l2_bqkv", (128, 24)); din("l2_bproj", (128, 8))
        din("l2_gwT", (128, 64)); din("iott", (128, 128))
        for ex in range(8):
            din("l2_wgT" + str(ex), (D, 512), BF16)
            din("l2_wuT" + str(ex), (D, 512), BF16)
            din("l2_wdT" + str(ex), (512, D), BF16)
    if nl >= 4:
        din("l3_wq", (D, D)); din("l3_wk", (D, D)); din("l3_wo", (D, D))
        din("l3_bo", (128, 8)); din("l3_rwT", (128, 64)); din("l3_rbt", (1, 128))
        din("prior_mask", (8, 1))
        for ex in range(8):
            din("l3_W" + str(ex), (D, 2048), BF16)
            din("l3_V" + str(ex), (D, 2048), BF16)
            din("l3_Wout" + str(ex), (2048, D), BF16)
        coll_in = nc.dram_tensor("coll_in", [1, 8], F32, kind="Internal").ap()
        coll_out = nc.dram_tensor("coll_out", [8, 8], F32, kind="Internal",
                                  addr_space="Shared").ap()

    xout = nc.dram_tensor("xout", [D, T], F32, kind="ExternalOutput").ap()

    dscr = nc.dram_tensor("dscr", [34, T], F32, kind="Internal").ap()
    gscr = nc.dram_tensor("gscr", [128, NCH * T], BF16, kind="Internal").ap()
    scr_wgq = nc.dram_tensor("scr_wgq", [D, 4096], BF16, kind="Internal").ap()
    scr_wuq = nc.dram_tensor("scr_wuq", [D, 4096], BF16, kind="Internal").ap()
    scr_wdq = nc.dram_tensor("scr_wdq", [4096, D], BF16, kind="Internal").ap()

    p = Prog(nc)
    with ExitStack() as ctx:
        E = ctx.enter_context
        p.alloc_sems(ctx)
        XF = E(nc.sbuf_tensor("XF", [128, NCH * T], F32))
        FA = E(nc.sbuf_tensor("FA", [128, 9984], F32))
        BA = E(nc.sbuf_tensor("BA", [128, 53248], BF16))
        SID = E(nc.sbuf_tensor("SID", [128, 128], BF16))
        STRI = E(nc.sbuf_tensor("STRI", [128, 128], BF16))
        ONES = E(nc.sbuf_tensor("ONES", [128, 1], F32))
        ONESB = E(nc.sbuf_tensor("ONESB", [128, 1], BF16))
        ROWS = E(nc.sbuf_tensor("ROWS", [128, 64], F32))
        PS = [E(nc.psum_tensor(f"PS{i}", [128, 512], F32)) for i in range(8)]

        BC_R = FA[:, 0:2048]
        BC_M = FA[:, 2048:4096]
        FS = FA[:, 4096:6144]
        FLEX = FA[:, 6144:9984]

        XN = BA[:, 0:16384]
        SLOTA = BA[:, 16384:32768]
        SLOT2 = BA[:, 32768:49152]
        WB = [BA[:, 49152:51200], BA[:, 51200:53248]]
        SAF = SLOTA.bitcast(F32)
        SAI = SLOTA.bitcast(I32)
        S2F = SLOT2.bitcast(F32)
        S2I = SLOT2.bitcast(I32)

        def xf(c):
            return XF[:, c * T:(c + 1) * T]

        def xn(c):
            return XN[:, c * T:(c + 1) * T]

        def f_load0(e):
            outs = [e.dma_start(out=SID[:], in_=dt["ident"]),
                    e.dma_start(out=STRI[:], in_=dt["tri"]),
                    e.dma_start(out=XF[:].rearrange("p (c t) -> p c t", c=8),
                                in_=dt["xT"].rearrange("(c p) t -> p c t", p=128))]
            return outs
        tok_x = p.dma("sp", f_load0, n=3, sem="S_X")
        tok_ones = p.emit("dve", lambda e: (e.memset(ONES[:], 1.0),
                          e.memset(ONESB[:], 1.0))[-1])
        SIDF = FA[:, 4096:4224]
        SIDF2 = FA[:, 4096:4224]

        bounce_row = [0]

        def bounce(srcs_dsts, deps, tag=""):
            t = None
            for src, dst0, npart in srcs_dsts:
                r = bounce_row[0] % 24
                bounce_row[0] += 1
                ncols = src.shape[-1] * (src.shape[0] if src.shape[0] <= 4 else 1)
                if src.shape[0] == 4:
                    t_o = p.dma("sp", lambda e, src=src, r=r: [e.dma_start(
                        out=dscr[r:r + 1, 0:src.shape[0] * src.shape[-1]].rearrange(
                            "o (p f) -> (o p) f", p=4), in_=src)],
                        deps=deps + ([t] if t else []), sem="S_B")
                else:
                    t_o = p.dma("sp", lambda e, src=src, r=r: [e.dma_start(
                        out=dscr[r:r + 1, 0:src.shape[-1]], in_=src)],
                        deps=deps + ([t] if t else []), sem="S_B")
                t = p.dma("sp", lambda e, dst0=dst0, r=r, ncols=ncols, npart=npart:
                          [e.dma_start(out=dst0, in_=dscr[r:r + 1, 0:ncols]
                                       .partition_broadcast(npart))],
                          deps=[t_o], sem="S_B")
            return t

        def emit_rsqrt_newton(e, v, r, t1):
            recip_d(e, t1, r)
            e.tensor_copy(out=r, in_=t1)
            for _ in range(2):
                e.tensor_tensor(out=t1, in0=r, in1=r, op=OP.mult)
                e.tensor_tensor(out=t1, in0=t1, in1=v, op=OP.mult)
                e.tensor_scalar(out=t1, in0=t1, scalar1=-0.5, scalar2=1.5,
                                op0=OP.mult, op1=OP.add)
                e.tensor_tensor(out=r, in0=r, in1=t1, op=OP.mult)

        def emit_round(e, z, out_bf, si, f1, m):
            e.tensor_copy(out=si, in_=z)
            e.tensor_copy(out=f1, in_=si)
            if SIMPROOF:
                e.tensor_tensor(out=z, in0=z, in1=f1, op=OP.subtract)
                e.tensor_scalar(out=m, in0=z, scalar1=0.5, scalar2=None, op0=OP.is_gt)
                e.tensor_scalar(out=z, in0=z, scalar1=-0.5, scalar2=None, op0=OP.is_lt)
                e.tensor_tensor(out=m, in0=m, in1=z, op=OP.subtract)
                return e.tensor_tensor(out=out_bf, in0=f1, in1=m, op=OP.add)
            return e.tensor_copy(out=out_bf, in_=f1)

        def emit_stats(src_fn, deps, psA=None, psB=None):
            t_mm = None
            for c in range(NCH):
                t_sq = p.emit("act", lambda e, c=c: e.activation(
                    out=FS[:], in_=src_fn(c), func=AF.Square),
                    deps=list(deps) + ([t_mm] if t_mm else []))

                def f_mm(e, c=c):
                    r = None
                    for qb in range(4):
                        r = e.matmul(PS[4 + qb][0:1, :], ONESB[:],
                                     src_fn(c)[:, qb * 512:(qb + 1) * 512],
                                     start=(c == 0), stop=(c == NCH - 1))
                    for qb in range(4):
                        r = e.matmul(PS[4 + qb][32:33, :], ONES[:],
                                     FS[:, qb * 512:(qb + 1) * 512],
                                     start=(c == 0), stop=(c == NCH - 1))
                    return r
                t_mm = p.emit("pe", f_mm, deps=[t_sq, tok_ones] + list(deps))
            return t_mm

        def emit_ln(deps_x):
            t_mm = None
            for c in range(NCH):
                t_sq = p.emit("act", lambda e, c=c: e.activation(
                    out=FS[:], in_=xf(c), func=AF.Square),
                    deps=list(deps_x) + ([t_mm] if t_mm else []))

                def f_mm(e, c=c):
                    r = None
                    for qb in range(4):
                        r = e.matmul(PS[4 + qb][0:1, :], ONES[:],
                                     xf(c)[:, qb * 512:(qb + 1) * 512],
                                     start=(c == 0), stop=(c == NCH - 1))
                    for qb in range(4):
                        r = e.matmul(PS[4 + qb][32:33, :], ONES[:],
                                     FS[:, qb * 512:(qb + 1) * 512],
                                     start=(c == 0), stop=(c == NCH - 1))
                    return r
                t_mm = p.emit("pe", f_mm, deps=[t_sq, tok_ones] + list(deps_x))

            R_ALL = FA[0:1, 6144:8192]
            MR_ALL = FA[0:1, 4096:6144]
            TM = FA[0:1, 0:512]
            TV = FA[0:1, 512:1024]
            TT1 = FA[0:1, 1024:1536]

            t_nm = None
            for qb in range(4):
                def f_q(e, qb=qb):
                    rr = R_ALL[:, qb * 512:(qb + 1) * 512]
                    mr = MR_ALL[:, qb * 512:(qb + 1) * 512]
                    e.tensor_scalar(out=TM, in0=PS[4 + qb][0:1, :],
                                    scalar1=1.0 / 1024.0, scalar2=None, op0=OP.mult)
                    e.tensor_scalar(out=TV, in0=PS[4 + qb][32:33, :],
                                    scalar1=1.0 / 1024.0, scalar2=None, op0=OP.mult)
                    e.tensor_tensor(out=TT1, in0=TM, in1=TM, op=OP.mult)
                    e.tensor_tensor(out=TV, in0=TV, in1=TT1, op=OP.subtract)
                    return e.tensor_scalar(out=TV, in0=TV, scalar1=1e-5,
                                           scalar2=None, op0=OP.add)
                t_a = p.emit("dve", f_q, deps=[t_mm] + ([t_nm] if t_nm else []))
                t_b = p.emit("act", lambda e, qb=qb: e.activation(
                    out=R_ALL[:, qb * 512:(qb + 1) * 512], in_=TV, func=AF.Sqrt),
                    deps=[t_a])

                def f_n(e, qb=qb):
                    rr = R_ALL[:, qb * 512:(qb + 1) * 512]
                    mr = MR_ALL[:, qb * 512:(qb + 1) * 512]
                    emit_rsqrt_newton(e, TV, rr, TT1)
                    return e.tensor_tensor(out=mr, in0=TM, in1=rr, op=OP.mult)
                t_nm = p.emit("dve", f_n, deps=[t_b])
            return bounce([(R_ALL, BC_R, 128), (MR_ALL, BC_M, 128)], [t_nm])

        def emit_xn(t_bc, deps_x, per_chunk=None):
            last_users = []
            out_toks = []
            for c in range(NCH):
                def f_xc(e, c=c):
                    e.tensor_tensor(out=FS[:], in0=xf(c), in1=BC_R[:], op=OP.mult)
                    return e.tensor_tensor(out=FS[:], in0=FS[:], in1=BC_M[:],
                                           op=OP.subtract)
                t_fs = p.emit("dve", f_xc, deps=[t_bc] + list(deps_x) + last_users)
                t_cp = p.emit("dve", lambda e, c=c: e.tensor_copy(out=xn(c), in_=FS[:]),
                              deps=[t_fs])
                last_users = [t_cp]
                if per_chunk is not None:
                    last_users = last_users + per_chunk(c, t_fs)
                out_toks.append(t_cp)
            return out_toks, last_users

        def l0_ffn_and_ln():
            t_bc = emit_ln([tok_x])
            AMAX = FLEX[:, 0:2048]

            AMIN = SAF[:, 0:2048]

            def per_chunk(c, t_fs):
                def f_am(e, c=c):
                    if c == 0:
                        e.tensor_copy(out=AMAX, in_=FS[:])
                        return e.tensor_copy(out=AMIN, in_=FS[:])
                    e.tensor_tensor(out=AMAX, in0=AMAX, in1=FS[:], op=OP.max)
                    return e.tensor_tensor(out=AMIN, in0=AMIN, in1=FS[:], op=OP.min)
                return [p.emit("dve", f_am, deps=[t_fs])]
            xn_toks, last_fs = emit_xn(t_bc, [tok_x], per_chunk)

            winv_col = {"wgq": ROWS[:, 4:5], "wuq": ROWS[:, 5:6],
                        "wdq": ROWS[:, 6:7]}
            wq_last = None
            for nm, src, dst, rows, cols in (
                    ("wgq", dt["l0_wgT"], scr_wgq, D, 4096),
                    ("wuq", dt["l0_wuT"], scr_wuq, D, 4096),
                    ("wdq", dt["l0_wdT"], scr_wdq, 4096, D)):
                nchunk = rows // 128
                nh = 2 if cols == 4096 else 1
                hc = cols // nh
                stg = [SAF[:, 2560:2560 + hc], SAF[:, 4608:4608 + hc]]
                qb_ = [SLOT2[:, 0:hc], SLOT2[:, hc:2 * hc]]
                m1s = S2F[:, 2048:2048 + hc]
                accc = ROWS[:, 0:1]
                tmpc = ROWS[:, 1:2]
                t_use = None
                i = 0
                for c in range(nchunk):
                    for h in range(nh):
                        b = i % 2
                        t_l = p.dma("gp", lambda e, c=c, h=h, b=b, hc=hc, src=src,
                                    stg=stg:
                                    [e.dma_start(out=stg[b],
                                     in_=src[c * 128:(c + 1) * 128, h * hc:(h + 1) * hc])],
                                    deps=[tok_x] + ([t_use] if t_use else []), sem="S_Q")

                        def f_abs(e, b=b, first=(i == 0), stg=stg):
                            e.tensor_reduce(out=tmpc, in_=stg[b], axis=AX.X, op=OP.add,
                                            apply_absolute_value=True)
                            if first:
                                return e.tensor_copy(out=accc, in_=tmpc)
                            return e.tensor_tensor(out=accc, in0=accc, in1=tmpc,
                                                   op=OP.add)
                        t_use = p.emit("dve", f_abs, deps=[t_l])
                        i += 1

                t_summ = p.emit("pe", lambda e: e.matmul(
                    PS[0][0:1, 0:1], ONES[:], accc, start=True, stop=True),
                    deps=[t_use, tok_ones])

                def f_tree(e, rows=rows, cols=cols):
                    recip_d(e, tmpc[0:1, :], PS[0][0:1, 0:1])
                    e.tensor_scalar(out=accc[0:1, :], in0=tmpc[0:1, :],
                                    scalar1=float(rows * cols), scalar2=None, op0=OP.mult)
                    return recip_d(e, tmpc[0:1, :], accc[0:1, :])
                t_tree = p.emit("dve", f_tree, deps=[t_summ])
                t_bcs = bounce([(accc[0:1, :], ROWS[:, 2:3], 128),
                                (tmpc[0:1, :], winv_col[nm], 128)], [t_tree],
                               tag=nm)
                wscol = ROWS[:, 2:3]
                t_q = None
                i = 0
                for c in range(nchunk):
                    for h in range(nh):
                        b = i % 2
                        t_l = p.dma("gp", lambda e, c=c, h=h, b=b, hc=hc, src=src,
                                    stg=stg:
                                    [e.dma_start(out=stg[b],
                                     in_=src[c * 128:(c + 1) * 128, h * hc:(h + 1) * hc])],
                                    deps=[t_bcs] + ([t_q] if t_q else []), sem="S_Q")

                        def f_q(e, b=b, hc=hc, stg=stg, m1s=m1s, qb_=qb_):
                            e.tensor_scalar(out=m1s[:, 0:hc], in0=stg[b], scalar1=wscol,
                                            scalar2=0.5, op0=OP.mult, op1=OP.is_ge)
                            e.tensor_scalar(out=stg[b], in0=stg[b], scalar1=wscol,
                                            scalar2=-0.5, op0=OP.mult, op1=OP.is_le)
                            return e.tensor_tensor(out=qb_[b], in0=m1s[:, 0:hc],
                                                   in1=stg[b], op=OP.subtract)
                        t_qv = p.emit("dve", f_q, deps=[t_l])
                        t_q = p.dma("gp", lambda e, c=c, h=h, b=b, hc=hc, dst=dst,
                                    qb_=qb_:
                                    [e.dma_start(out=dst[c * 128:(c + 1) * 128,
                                                        h * hc:(h + 1) * hc],
                                     in_=qb_[b])], deps=[t_qv], sem="S_Q")
                        i += 1
                wq_last = t_q

            SCB = FA[:, 7936:9984]

            t_sidf = p.emit("dve", lambda e: e.tensor_copy(out=SIDF, in_=SID[:]),
                            deps=last_fs + xn_toks)

            def f_amc(e):
                e.tensor_scalar(out=AMIN, in0=AMIN, scalar1=-1.0, scalar2=None,
                                op0=OP.mult)
                return e.tensor_tensor(out=AMAX, in0=AMAX, in1=AMIN, op=OP.max)
            t_amc = p.emit("dve", f_amc, deps=last_fs + xn_toks)

            AMT = ROWS[:, 8:24]
            ASC = ROWS[:, 24:40]
            AIV = ROWS[:, 40:56]
            amb_need = SLOTA.bitcast(BF16)
            t_amt = []
            for tt in range(16):
                t_tr = p.emit("pe", lambda e, tt=tt: e.transpose(
                    PS[0][:, 0:128], AMAX[:, tt * 128:(tt + 1) * 128], SIDF),
                    deps=([t_sidf, t_amc] + last_fs + xn_toks) if tt == 0 else
                    [t_sidf, t_amt[-1]])
                t_rd = p.emit("dve", lambda e, tt=tt: e.tensor_reduce(
                    out=AMT[:, tt:tt + 1], in_=PS[0][:, 0:128], axis=AX.X,
                    op=OP.max, apply_absolute_value=True), deps=[t_tr])
                t_amt.append(t_rd)

            def f_sc(e):
                recip_d(e, ASC, AMT)
                e.tensor_scalar(out=ASC, in0=ASC, scalar1=127.0, scalar2=None,
                                op0=OP.mult)
                return e.tensor_scalar(out=AIV, in0=AMT, scalar1=1.0 / 127.0,
                                       scalar2=None, op0=OP.mult)
            t_sc = p.emit("dve", f_sc, deps=[t_amt[-1]])
            R16A = FA[0:16, 7296:7424]
            R16B = FA[0:16, 7424:7552]
            t_tA = p.emit("pe", lambda e: e.transpose(
                PS[1][0:16, 0:128], ASC, SIDF), deps=[t_sc, t_sidf])
            t_cA = p.emit("dve", lambda e: e.tensor_copy(
                out=R16A, in_=PS[1][0:16, 0:128]), deps=[t_tA])
            t_tB = p.emit("pe", lambda e: e.transpose(
                PS[1][0:16, 128:256], AIV, SIDF), deps=[t_sc, t_cA])
            t_cB = p.emit("dve", lambda e: e.tensor_copy(
                out=R16B, in_=PS[1][0:16, 128:256]), deps=[t_tB])
            t_sco = p.dma("sp", lambda e: [
                e.dma_start(out=dscr[25:26, :].rearrange("o (t p) -> (o t) p", t=16),
                            in_=R16A),
                e.dma_start(out=dscr[26:27, :].rearrange("o (t p) -> (o t) p", t=16),
                            in_=R16B)], deps=[t_cA, t_cB], sem="S_B", n=2)
            t_sci = p.dma("sp", lambda e: [e.dma_start(
                out=SCB, in_=dscr[25:26, :].partition_broadcast(128))],
                deps=[t_sco], sem="S_B")
            t_invo = t_sco

            def xq(c):
                return SLOT2[:, c * T:(c + 1) * T]
            xq_toks = []
            t_prev = None
            for c in range(NCH):
                def f_z(e, c=c):
                    e.tensor_tensor(out=FS[:], in0=xf(c), in1=BC_R[:], op=OP.mult)
                    e.tensor_tensor(out=FS[:], in0=FS[:], in1=BC_M[:], op=OP.subtract)
                    return e.tensor_tensor(out=FS[:], in0=FS[:], in1=SCB, op=OP.mult)
                deps = [t_sci, tok_x, wq_last] + ([t_prev] if t_prev else [])
                t_z = p.emit("dve", f_z, deps=deps)
                t_r = p.emit("dve", lambda e, c=c: emit_round(
                    e, FS[:], xq(c), SAI[:, 0:2048], SAF[:, 2048:4096],
                    SAF[:, 4096:6144]),
                    deps=[t_z])
                t_prev = t_r
                xq_toks.append(t_r)

            HF = FA[:, 0:4096]
            IASB = FLEX[:, 0:2048]
            t_ias = p.dma("sp", lambda e: [e.dma_start(
                out=IASB, in_=dscr[26:27, :].partition_broadcast(128))],
                deps=[t_invo, t_sc] + xq_toks, sem="S_B")
            wg_inv = winv_col["wgq"]
            wu_inv = winv_col["wuq"]
            wd_inv = winv_col["wdq"]
            TB = 128
            G1 = SAF[:, 0 * TB:1 * TB]
            U1 = SAF[:, 1 * TB:2 * TB]
            HAM = SAF[:, 2 * TB:3 * TB]
            HT1 = SAF[:, 3 * TB:4 * TB]
            HSCB = SAF[:, 4 * TB:5 * TB]
            HIAB = SAF[:, 5 * TB:6 * TB]
            ZZ = SAF[:, 6 * TB:7 * TB]
            F1 = SAF[:, 7 * TB:8 * TB]
            SIV = SAI[:, 8 * TB:9 * TB]
            HAMN = SAF[:, 9 * TB:10 * TB]
            MRND = SAF[:, 10 * TB:11 * TB]

            def hqap(oc):
                return SLOTA[:, 8192 + oc * TB:8192 + (oc + 1) * TB]

            t_sidf2 = p.emit("dve", lambda e: e.tensor_copy(out=SIDF2, in_=SID[:]),
                             deps=xq_toks + [t_ias])
            g_last = None
            t_wprev = [None, None]
            t_ev_prev = None
            t_gv_prev = [None, None]
            for tb in range(T // TB):
                t0 = tb * TB
                t_h = []
                for oc in range(32):
                    b = oc % 2

                    def f_wl(e, oc=oc, b=b):
                        return [e.dma_start(
                                out=WB[b][:, 0:1024].rearrange("p (c o) -> p c o", c=8),
                                in_=scr_wgq[:, oc * 128:(oc + 1) * 128]
                                .rearrange("(c p) o -> p c o", p=128)),
                                e.dma_start(
                                out=WB[b][:, 1024:2048].rearrange("p (c o) -> p c o", c=8),
                                in_=scr_wuq[:, oc * 128:(oc + 1) * 128]
                                .rearrange("(c p) o -> p c o", p=128))]
                    dpw = [wq_last]
                    if t_wprev[b] is not None:
                        dpw.append(t_wprev[b])
                    t_wl = p.dma("gp", f_wl, deps=dpw,
                                 sem=("S_WA" if b == 0 else "S_WB"), n=2)

                    def f_gmm(e, oc=oc, b=b, t0=t0):
                        r = None
                        for c in range(NCH):
                            r = e.matmul(PS[0][:, 0:TB], WB[b][:, c * 128:(c + 1) * 128],
                                         xq(c)[:, t0:t0 + TB], start=(c == 0),
                                         stop=(c == NCH - 1))
                        for c in range(NCH):
                            r = e.matmul(PS[1][:, 0:TB],
                                         WB[b][:, 1024 + c * 128:1024 + (c + 1) * 128],
                                         xq(c)[:, t0:t0 + TB], start=(c == 0),
                                         stop=(c == NCH - 1))
                        return r
                    dmm = [t_wl] + xq_toks
                    if t_ev_prev is not None:
                        dmm.append(t_ev_prev)
                    t_mm = p.emit("pe", f_gmm, deps=dmm)
                    t_wprev[b] = t_mm

                    def f_ev1(e, t0=t0):
                        e.tensor_scalar(out=G1, in0=PS[0][:, 0:TB], scalar1=wg_inv,
                                        scalar2=None, op0=OP.mult)
                        e.tensor_tensor(out=G1, in0=G1, in1=IASB[:, t0:t0 + TB],
                                        op=OP.mult)
                        e.tensor_scalar(out=U1, in0=PS[1][:, 0:TB], scalar1=wu_inv,
                                        scalar2=None, op0=OP.mult)
                        return e.tensor_tensor(out=U1, in0=U1, in1=IASB[:, t0:t0 + TB],
                                               op=OP.mult)
                    t_ev1 = p.emit("dve", f_ev1, deps=[t_mm, t_ias])
                    t_ev_prev = t_ev1
                    t_sil = p.emit("act", lambda e: e.activation(
                        out=G1, in_=G1, func=AF.Silu), deps=[t_ev1])

                    def f_h(e, oc=oc):
                        hsl = HF[:, oc * TB:(oc + 1) * TB]
                        e.tensor_tensor(out=hsl, in0=G1, in1=U1, op=OP.mult)
                        if oc == 0:
                            e.tensor_copy(out=HAM, in_=hsl)
                            return e.tensor_copy(out=HAMN, in_=hsl)
                        e.tensor_tensor(out=HAM, in0=HAM, in1=hsl, op=OP.max)
                        return e.tensor_tensor(out=HAMN, in0=HAMN, in1=hsl, op=OP.min)
                    t_hh = p.emit("dve", f_h, deps=[t_sil])
                    t_ev_prev = t_hh
                    t_h.append(t_hh)

                NHT = TB // 128
                HMT = ROWS[:, 56:56 + NHT]
                HAS = ROWS[:, 58:58 + NHT]
                HAI = ROWS[:, 60:60 + NHT]
                def f_hamc(e):
                    e.tensor_scalar(out=HAMN, in0=HAMN, scalar1=-1.0, scalar2=None,
                                    op0=OP.mult)
                    return e.tensor_tensor(out=HAM, in0=HAM, in1=HAMN, op=OP.max)
                t_hamc = p.emit("dve", f_hamc, deps=[t_h[-1]])
                t_htr = []
                for tt in range(NHT):
                    t_tr = p.emit("pe", lambda e, tt=tt: e.transpose(
                        PS[4][:, 0:128], HAM[:, tt * 128:(tt + 1) * 128], SIDF2),
                        deps=[t_sidf2, t_hamc] if tt == 0 else [t_htr[-1]])
                    t_rd = p.emit("dve", lambda e, tt=tt: e.tensor_reduce(
                        out=HMT[:, tt:tt + 1], in_=PS[4][:, 0:128], axis=AX.X,
                        op=OP.max, apply_absolute_value=True), deps=[t_tr])
                    t_htr.append(t_rd)

                def f_hsc(e):
                    recip_d(e, HAS, HMT)
                    e.tensor_scalar(out=HAS, in0=HAS, scalar1=127.0, scalar2=None,
                                    op0=OP.mult)
                    return e.tensor_scalar(out=HAI, in0=HMT, scalar1=1.0 / 127.0,
                                           scalar2=None, op0=OP.mult)
                t_hsc = p.emit("dve", f_hsc, deps=[t_htr[-1]])
                R2A = FA[0:NHT, 8192:8192 + 128]
                R2B = FA[0:NHT, 8320:8320 + 128]
                t_t2 = p.emit("pe", lambda e: e.transpose(
                    PS[5][0:NHT, 0:128], HAS, SIDF2), deps=[t_hsc, t_sidf2])
                t_c2 = p.emit("dve", lambda e: e.tensor_copy(
                    out=R2A, in_=PS[5][0:NHT, 0:128]), deps=[t_t2])
                t_t3 = p.emit("pe", lambda e: e.transpose(
                    PS[5][0:NHT, 128:256], HAI, SIDF2), deps=[t_hsc, t_c2])
                t_c3 = p.emit("dve", lambda e: e.tensor_copy(
                    out=R2B, in_=PS[5][0:NHT, 128:256]), deps=[t_t3])
                r0 = 27 + (tb % 2)
                t_ho = p.dma("sp", lambda e, r0=r0: [
                    e.dma_start(out=dscr[r0:r0 + 1, 0:TB]
                                .rearrange("o (t p) -> (o t) p", t=NHT), in_=R2A),
                    e.dma_start(out=dscr[r0:r0 + 1, 256:256 + TB]
                                .rearrange("o (t p) -> (o t) p", t=NHT), in_=R2B)],
                    deps=[t_c2, t_c3], sem="S_B", n=2)
                t_hb = p.dma("sp", lambda e, r0=r0: [
                    e.dma_start(out=HSCB, in_=dscr[r0:r0 + 1, 0:TB]
                                .partition_broadcast(128)),
                    e.dma_start(out=HIAB, in_=dscr[r0:r0 + 1, 256:256 + TB]
                                .partition_broadcast(128))],
                    deps=[t_ho], sem="S_B", n=2)
                hq_toks = []
                for oc in range(32):
                    def f_hq(e, oc=oc):
                        hsl = HF[:, oc * TB:(oc + 1) * TB]
                        e.tensor_tensor(out=ZZ, in0=hsl, in1=HSCB, op=OP.mult)
                        return emit_round(e, ZZ, hqap(oc), SIV, F1, MRND)
                    hq_toks.append(p.emit("dve", f_hq, deps=[t_hb, t_h[oc]]))

                for dchunk in range(NCH):
                    par = dchunk % 2

                    def f_wdl(e, dchunk=dchunk):
                        ap = scr_wdq[:, dchunk * 128:(dchunk + 1) * 128].rearrange(
                            "(c p) o -> p c o", p=128)
                        return [e.dma_start(
                            out=WB[0][:, 0:2048].rearrange("p (c o) -> p c o", c=16),
                            in_=ap[:, 0:16]),
                            e.dma_start(
                            out=WB[1][:, 0:2048].rearrange("p (c o) -> p c o", c=16),
                            in_=ap[:, 16:32])]
                    dpw = [wq_last] + hq_toks
                    for tt in (t_wprev[0], t_wprev[1]):
                        if tt is not None:
                            dpw.append(tt)
                    t_wl = p.dma("gp", f_wdl, deps=dpw, sem="S_WA", n=2)

                    def f_dmm(e, par=par):
                        r = None
                        for oc in range(32):
                            w = WB[oc // 16][:, (oc % 16) * 128:(oc % 16 + 1) * 128]
                            r = e.matmul(PS[2 + par][:, 0:TB], w, hqap(oc),
                                         start=(oc == 0), stop=(oc == 31))
                        return r
                    dmm = [t_wl]
                    if t_gv_prev[par] is not None:
                        dmm.append(t_gv_prev[par])
                    t_dm = p.emit("pe", f_dmm, deps=dmm)
                    t_wprev[0] = t_dm
                    t_wprev[1] = t_dm

                    def f_gv(e, par=par):
                        e.tensor_scalar(out=ZZ, in0=PS[2 + par][:, 0:TB],
                                        scalar1=wd_inv, scalar2=None, op0=OP.mult)
                        return e.tensor_tensor(out=ZZ, in0=ZZ, in1=HIAB, op=OP.mult)
                    t_gv = p.emit("dve", f_gv, deps=[t_dm, t_hb])
                    gstg = SLOTA[:, 4608 + par * TB:4608 + (par + 1) * TB]
                    t_gs = p.emit("act", lambda e, gstg=gstg: e.activation(
                        out=gstg, in_=ZZ, func=AF.Silu), deps=[t_gv])
                    t_gv_prev[par] = t_gs
                    g_last = p.dma("gp", lambda e, dchunk=dchunk, t0=t0, gstg=gstg:
                                   [e.dma_start(out=gscr[:, dchunk * T + t0:
                                                         dchunk * T + t0 + TB],
                                     in_=gstg)], deps=[t_gs], sem="S_G")
            return xn_toks, g_last

        def l0_attn(xn_toks, g_last):
            OFm = SLOTA
            DEN = FA[:, 7936:9984]

            def stb(h):
                pb = 64 * (h % 2)
                off = 15528 + (h // 2) * 65
                return SLOT2[pb:pb + 64, off:off + 65]

            def stf(h):
                pb = 64 * (h % 2)
                off = 6144 + (h // 2) * 65
                return FA[pb:pb + 64, off:off + 65]

            Q = SLOT2[:, 0:2048]
            K = SLOT2[:, 2048:4096]
            KT = SLOT2[:, 4096:6144]
            VST = SLOT2[:, 6144:8192]
            VAUG = SLOT2[:, 8192:10280]
            PT = SLOT2[:, 10280:10408]
            FT1 = SLOT2[:, 10408:10920]
            FT2 = SLOT2[:, 10920:11432]
            GST = [SLOT2[:, 11432:13480], SLOT2[:, 13480:15528]]

            t_lw = {"S_WA": None, "S_WB": None}
            t_ofm = []
            den_dump = []
            ft_guard = [None]
            fs_guard = [None]
            t_attn_ps_free = [None] * 8
            prev_vaug_users = []
            for g2 in range(8):
                def f_wq(e, g2=g2):
                    return [e.dma_start(
                            out=WB[0][:, 0:1024].rearrange("p (c f) -> p c f", c=8),
                            in_=dt["l0_wq"][:, g2 * 128:(g2 + 1) * 128]
                            .rearrange("(c p) f -> p c f", p=128))]
                dq = [tok_x, g_last] + ([t_lw["S_WA"]] if t_lw["S_WA"] else [])
                t_wq = p.dma("gp", f_wq, deps=dq, sem="S_WA")

                def f_wkv(e, g2=g2):
                    return [e.dma_start(
                            out=WB[1][:, 0:1024].rearrange("p (c f) -> p c f", c=8),
                            in_=dt["l0_wk"][:, g2 * 128:(g2 + 1) * 128]
                            .rearrange("(c p) f -> p c f", p=128)),
                            e.dma_start(
                            out=WB[1][:, 1024:2048].rearrange("p (c f) -> p c f", c=8),
                            in_=dt["l0_wv"][:, g2 * 128:(g2 + 1) * 128]
                            .rearrange("(c p) f -> p c f", p=128))]
                dk = [tok_x, g_last] + ([t_lw["S_WB"]] if t_lw["S_WB"] else [])
                t_wkv = p.dma("gp", f_wkv, deps=dk, sem="S_WB", n=2)

                def proj_feat(woff, wbuf, dst, t_w, prev_scan):
                    toks = []
                    for qb in range(4):
                        def f_mm(e, qb=qb, woff=woff, wbuf=wbuf):
                            r = None
                            for c in range(NCH):
                                r = e.matmul(PS[qb][:, :],
                                             wbuf[:, woff + c * 128:woff + (c + 1) * 128],
                                             xn(c)[:, qb * 512:(qb + 1) * 512],
                                             start=(c == 0), stop=(c == NCH - 1))
                            return r
                        dmm = [t_w] + xn_toks
                        if t_attn_ps_free[qb] is not None:
                            dmm.append(t_attn_ps_free[qb])
                        t_mm = p.emit("pe", f_mm, deps=dmm)
                        t_r = p.emit("act", lambda e, qb=qb: e.activation(
                            out=FT1, in_=PS[qb][:, :], func=AF.Relu),
                            deps=[t_mm] + ft_guard[0:1])
                        t_m = p.emit("dve", lambda e, qb=qb: e.tensor_scalar(
                            out=FS[:, 0:512], in0=PS[qb][:, :], scalar1=0.0,
                            scalar2=None, op0=OP.min),
                            deps=[t_mm, t_r] + fs_guard[0:1])
                        t_attn_ps_free[qb] = t_m
                        t_e = p.emit("act", lambda e: e.activation(
                            out=FT2, in_=FS[:, 0:512], func=AF.Exp), deps=[t_m])
                        fs_guard[0] = t_e
                        t_a = p.emit("dve", lambda e, qb=qb, dst=dst: e.tensor_tensor(
                            out=dst[:, qb * 512:(qb + 1) * 512], in0=FT1, in1=FT2,
                            op=OP.add), deps=[t_e] + prev_scan)
                        ft_guard[0] = t_a
                        toks.append(t_a)
                    return toks
                tq = proj_feat(0, WB[0], Q, t_wq, prev_vaug_users)
                tk_ = proj_feat(0, WB[1], K, t_wkv, prev_vaug_users)
                t_lw["S_WA"] = tq[-1]
                tv = []
                for qb in range(4):
                    def f_mm(e, qb=qb):
                        r = None
                        for c in range(NCH):
                            r = e.matmul(PS[4 + qb % 2][:, :],
                                         WB[1][:, 1024 + c * 128:1024 + (c + 1) * 128],
                                         xn(c)[:, qb * 512:(qb + 1) * 512],
                                         start=(c == 0), stop=(c == NCH - 1))
                        return r
                    dmm = [t_wkv] + xn_toks
                    if t_attn_ps_free[4 + qb % 2] is not None:
                        dmm.append(t_attn_ps_free[4 + qb % 2])
                    t_mm = p.emit("pe", f_mm, deps=dmm)
                    t_v = p.emit("dve", lambda e, qb=qb: e.tensor_copy(
                        out=VST[:, qb * 512:(qb + 1) * 512], in_=PS[4 + qb % 2][:, :]),
                        deps=[t_mm])
                    t_attn_ps_free[4 + qb % 2] = t_v
                    tv.append(t_v)
                t_lw["S_WB"] = tv[-1]

                t_ms = p.emit("dve", lambda e: e.memset(VAUG[:], 1.0),
                              deps=prev_vaug_users)
                ttr = []
                for ct in range(16):
                    def f_tr(e, ct=ct):
                        e.transpose(PS[6].bitcast(BF16)[:, 0:128],
                                    VST[:, ct * 128:(ct + 1) * 128], SID[:])
                        return e.transpose(PS[7].bitcast(BF16)[:, 0:128],
                                           K[:, ct * 128:(ct + 1) * 128], SID[:])
                    dtr = [tv[ct // 4], tk_[ct // 4]]
                    if ttr:
                        dtr.append(ttr[-1])
                    if t_attn_ps_free[6] is not None:
                        dtr.append(t_attn_ps_free[6])
                    t_tr = p.emit("pe", f_tr, deps=dtr)

                    def f_ev(e, ct=ct):
                        e.tensor_copy(out=VAUG[:, ct * 130:ct * 130 + 64],
                                      in_=PS[6].bitcast(BF16)[:, 0:64])
                        e.tensor_copy(out=VAUG[:, ct * 130 + 65:ct * 130 + 129],
                                      in_=PS[6].bitcast(BF16)[:, 64:128])
                        return e.tensor_copy(out=KT[:, ct * 128:(ct + 1) * 128],
                                             in_=PS[7].bitcast(BF16)[:, 0:128])
                    t_ev = p.emit("dve", f_ev, deps=[t_tr, t_ms])
                    t_attn_ps_free[6] = t_ev
                    ttr.append(t_ev)

                t_pt_prev = None
                t_oe_prev = None
                t_av_prev = None
                scan_tail = []
                for hl in range(2):
                    h = 2 * g2 + hl
                    t_prev_state = None
                    for c in range(16):
                        def f_sc(e, hl=hl, c=c):
                            return e.matmul(
                                PS[0][:, 0:128],
                                K[hl * 64:(hl + 1) * 64, c * 128:(c + 1) * 128],
                                Q[hl * 64:(hl + 1) * 64, c * 128:(c + 1) * 128],
                                start=True, stop=True)
                        dsc = [tq[c // 4], tk_[c // 4]]
                        if t_pt_prev is not None:
                            dsc.append(t_pt_prev)
                        if t_attn_ps_free[0] is not None:
                            dsc.append(t_attn_ps_free[0])
                            t_attn_ps_free[0] = None
                        t_s = p.emit("pe", f_sc, deps=dsc)
                        t_pt = p.emit("dve", lambda e: e.tensor_tensor(
                            out=PT, in0=PS[0][:, 0:128], in1=STRI[:], op=OP.mult),
                            deps=[t_s] + ([t_av_prev] if t_av_prev else []))
                        t_pt_prev = t_pt

                        def f_av(e, hl=hl, h=h, c=c):
                            if c > 0:
                                e.matmul(PS[1][0:65, 0:128], stb(h),
                                         Q[hl * 64:(hl + 1) * 64, c * 128:(c + 1) * 128],
                                         start=True, stop=False)
                                return e.matmul(
                                    PS[1][0:65, 0:128],
                                    VAUG[:, c * 130 + hl * 65:c * 130 + (hl + 1) * 65],
                                    PT, start=False, stop=True)
                            return e.matmul(
                                PS[1][0:65, 0:128],
                                VAUG[:, c * 130 + hl * 65:c * 130 + (hl + 1) * 65],
                                PT, start=True, stop=True)
                        davd = [t_pt, ttr[c]]
                        if t_prev_state is not None:
                            davd.append(t_prev_state)
                        if t_oe_prev is not None:
                            davd.append(t_oe_prev)
                        t_av = p.emit("pe", f_av, deps=davd)
                        t_av_prev = t_av

                        def f_su(e, hl=hl, c=c):
                            return e.matmul(
                                PS[2][0:64, 0:65],
                                KT[:, c * 128 + hl * 64:c * 128 + (hl + 1) * 64],
                                VAUG[:, c * 130 + hl * 65:c * 130 + (hl + 1) * 65],
                                start=True, stop=True)
                        dsu = [ttr[c], t_av]
                        if t_prev_state is not None:
                            dsu.append(t_prev_state)
                        t_su = p.emit("pe", f_su, deps=dsu)

                        def f_st(e, h=h, c=c):
                            if c == 0:
                                e.tensor_copy(out=stf(h), in_=PS[2][0:64, 0:65])
                            else:
                                e.tensor_tensor(out=stf(h), in0=stf(h),
                                                in1=PS[2][0:64, 0:65], op=OP.add)
                            return e.tensor_copy(out=stb(h), in_=stf(h))
                        t_st = p.emit("dve", f_st, deps=[t_su])
                        t_prev_state = t_st

                        def f_oe(e, g2=g2, hl=hl, h=h, c=c):
                            e.tensor_copy(
                                out=OFm[hl * 64:(hl + 1) * 64,
                                        g2 * 2048 + c * 128:g2 * 2048 + (c + 1) * 128],
                                in_=PS[1][0:64, 0:128])
                            return e.tensor_copy(
                                out=DEN[hl * 64:hl * 64 + 1, c * 128:(c + 1) * 128],
                                in_=PS[1][64:65, 0:128])
                        t_oe = p.emit("dve", f_oe, deps=[t_av])
                        t_oe_prev = t_oe
                        t_ofm.append(t_oe)
                    scan_tail.append(t_prev_state)
                t_dd = p.dma("sp", lambda e, g2=g2: [
                    e.dma_start(out=dscr[2 * g2:2 * g2 + 1, :], in_=DEN[0:1, :]),
                    e.dma_start(out=dscr[2 * g2 + 1:2 * g2 + 2, :], in_=DEN[64:65, :])],
                    deps=[t_oe_prev] + scan_tail, sem="S_B", n=2)
                den_dump.append(t_dd)
                prev_vaug_users = scan_tail + [t_oe_prev]
                t_attn_ps_free[0] = t_pt_prev
                t_attn_ps_free[1] = t_oe_prev
                t_attn_ps_free[2] = scan_tail[-1]

            t_norm = []
            DBC = FA[:, 0:2048]
            for g2 in range(8):
                t_di = p.dma("sp", lambda e, g2=g2: [
                    e.dma_start(out=DBC[0:64, :],
                                in_=dscr[2 * g2:2 * g2 + 1, :].partition_broadcast(64)),
                    e.dma_start(out=DBC[64:128, :],
                                in_=dscr[2 * g2 + 1:2 * g2 + 2, :]
                                .partition_broadcast(64))],
                    deps=[den_dump[g2]] + (t_norm[-1:] if t_norm else []),
                    sem="S_B", n=2)

                def f_nm(e, g2=g2):
                    e.tensor_scalar(out=DBC, in0=DBC, scalar1=1e-6, scalar2=None,
                                    op0=OP.add)
                    recip_d(e, DBC, DBC)
                    return e.tensor_tensor(
                        out=OFm[:, g2 * 2048:(g2 + 1) * 2048],
                        in0=OFm[:, g2 * 2048:(g2 + 1) * 2048], in1=DBC, op=OP.mult)
                t_nm = p.emit("dve", f_nm, deps=[t_di] + t_ofm[-1:])
                t_norm.append(t_nm)

            comb_toks = []
            t_prev = None
            t_gld_prev = [None, None]
            for dchunk in range(NCH):
                b = dchunk % 2

                def f_wl(e, dchunk=dchunk, b=b):
                    return [e.dma_start(
                            out=WB[b][:, 0:1024].rearrange("p (c f) -> p c f", c=8),
                            in_=dt["l0_wo"][:, dchunk * 128:(dchunk + 1) * 128]
                            .rearrange("(c p) f -> p c f", p=128))]
                dps = [tok_x]
                if t_prev is not None:
                    dps.append(t_prev)
                t_wl = p.dma("gp", f_wl, deps=dps,
                             sem=("S_WA" if b == 0 else "S_WB"))
                dgl = [g_last]
                if t_gld_prev[b] is not None:
                    dgl.append(t_gld_prev[b])
                t_gld = p.dma("gp", lambda e, dchunk=dchunk, b=b: [
                    e.dma_start(out=GST[b], in_=gscr[:, dchunk * T:(dchunk + 1) * T])],
                    deps=dgl, sem="S_G")
                qtoks = []
                for qb in range(4):
                    def f_mm(e, qb=qb, b=b):
                        r = None
                        for ec in range(NCH):
                            r = e.matmul(PS[qb][:, :], WB[b][:, ec * 128:(ec + 1) * 128],
                                         OFm[:, ec * 2048 + qb * 512:
                                             ec * 2048 + (qb + 1) * 512],
                                         start=(ec == 0), stop=(ec == NCH - 1))
                        return r
                    dmm = [t_wl] + t_norm
                    if comb_toks:
                        dmm.append(comb_toks[-4] if len(comb_toks) >= 4
                                   else comb_toks[-1])
                    t_mm = p.emit("pe", f_mm, deps=dmm)

                    def f_cmb(e, qb=qb, dchunk=dchunk, b=b):
                        e.tensor_tensor(out=FS[:, 0:512], in0=PS[qb][:, :],
                                        in1=GST[b][:, qb * 512:(qb + 1) * 512],
                                        op=OP.mult)
                        return e.tensor_tensor(
                            out=xf(dchunk)[:, qb * 512:(qb + 1) * 512],
                            in0=xf(dchunk)[:, qb * 512:(qb + 1) * 512],
                            in1=FS[:, 0:512], op=OP.add)
                    t_cb = p.emit("dve", f_cmb, deps=[t_mm, t_gld])
                    qtoks.append(t_cb)
                    comb_toks.append(t_cb)
                t_gld_prev[b] = qtoks[-1]
                t_prev = qtoks[-1]
            return comb_toks

        xn_toks, g_last = l0_ffn_and_ln()
        comb = l0_attn(xn_toks, g_last)

        def emit_mha(deps0, xn_toks, g_last, wq_ap, wk_ap, wv_ap, wo_ap,
                     qscale, causal=False, v_from_k=False,
                     bq_col=None, bk_col=None, bv_col=None, bo_col=None):
            OFm = SLOTA
            DEN = FA[:, 7936:9984]
            Q = SLOT2[:, 0:2048]
            K = SLOT2[:, 2048:4096]
            VST = SLOT2[:, 6144:8192]
            VAUG = SLOT2[:, 8192:10280]
            PT = SLOT2[:, 10280:10792]
            GST = [SLOT2[:, 11432:13480], SLOT2[:, 13480:15528]]
            t_lw = {"S_WA": None, "S_WB": None}
            den_dump = []
            prev_scan = list(deps0)
            ps_free = [None] * 8
            for g2 in range(8):
                def f_wq(e, g2=g2):
                    return [e.dma_start(
                        out=WB[0][:, 0:1024].rearrange("p (c f) -> p c f", c=8),
                        in_=wq_ap[:, g2 * 128:(g2 + 1) * 128]
                        .rearrange("(c p) f -> p c f", p=128))]
                dq = list(deps0) + ([t_lw["S_WA"]] if t_lw["S_WA"] else [])
                t_wq = p.dma("gp", f_wq, deps=dq, sem="S_WA")

                def f_wkv(e, g2=g2):
                    outs = [e.dma_start(
                        out=WB[1][:, 0:1024].rearrange("p (c f) -> p c f", c=8),
                        in_=wk_ap[:, g2 * 128:(g2 + 1) * 128]
                        .rearrange("(c p) f -> p c f", p=128))]
                    if not v_from_k:
                        outs.append(e.dma_start(
                            out=WB[1][:, 1024:2048].rearrange("p (c f) -> p c f", c=8),
                            in_=wv_ap[:, g2 * 128:(g2 + 1) * 128]
                            .rearrange("(c p) f -> p c f", p=128)))
                    return outs
                dk = list(deps0) + ([t_lw["S_WB"]] if t_lw["S_WB"] else [])
                t_wkv = p.dma("gp", f_wkv, deps=dk, sem="S_WB",
                              n=(1 if v_from_k else 2))

                def proj(woff, wbuf, dst, t_w, scale, bcol):
                    toks = []
                    for qb in range(4):
                        def f_mm(e, qb=qb, woff=woff, wbuf=wbuf):
                            r = None
                            for c in range(NCH):
                                r = e.matmul(PS[4 + qb % 2][:, :],
                                             wbuf[:, woff + c * 128:woff + (c + 1) * 128],
                                             xn(c)[:, qb * 512:(qb + 1) * 512],
                                             start=(c == 0), stop=(c == NCH - 1))
                            return r
                        dmm = [t_w] + xn_toks + prev_scan
                        if ps_free[4 + qb % 2] is not None:
                            dmm.append(ps_free[4 + qb % 2])
                        t_mm = p.emit("pe", f_mm, deps=dmm)

                        def f_ev(e, qb=qb, dst=dst, scale=scale, bcol=bcol, g2=g2):
                            ps = PS[4 + qb % 2][:, :]
                            d = dst[:, qb * 512:(qb + 1) * 512]
                            if bcol is not None:
                                return e.tensor_scalar(out=d, in0=ps,
                                                       scalar1=bcol(g2),
                                                       scalar2=scale, op0=OP.add,
                                                       op1=OP.mult)
                            if scale != 1.0:
                                return e.tensor_scalar(out=d, in0=ps, scalar1=scale,
                                                       scalar2=None, op0=OP.mult)
                            return e.tensor_copy(out=d, in_=ps)
                        t_ev = p.emit("dve", f_ev, deps=[t_mm])
                        ps_free[4 + qb % 2] = t_ev
                        toks.append(t_ev)
                    return toks
                tq = proj(0, WB[0], Q, t_wq, qscale, bq_col)
                tk_ = proj(0, WB[1], K, t_wkv, 1.0, bk_col)
                t_lw["S_WA"] = tq[-1]
                if v_from_k:
                    tv = tk_
                    vsrc = K
                else:
                    tv = proj(1024, WB[1], VST, t_wkv, 1.0, bv_col)
                    vsrc = VST
                t_lw["S_WB"] = tv[-1]

                t_ms = p.emit("dve", lambda e: e.memset(VAUG[:], 1.0), deps=prev_scan)
                ttr = []
                for ct in range(16):
                    def f_tr(e, ct=ct, vsrc=vsrc):
                        return e.transpose(PS[6].bitcast(BF16)[:, 0:128],
                                           vsrc[:, ct * 128:(ct + 1) * 128], SID[:])
                    dtr = [tv[ct // 4]]
                    if ttr:
                        dtr.append(ttr[-1])
                    if ps_free[6] is not None:
                        dtr.append(ps_free[6])
                    t_tr = p.emit("pe", f_tr, deps=dtr)

                    def f_ev(e, ct=ct):
                        e.tensor_copy(out=VAUG[:, ct * 130:ct * 130 + 64],
                                      in_=PS[6].bitcast(BF16)[:, 0:64])
                        return e.tensor_copy(out=VAUG[:, ct * 130 + 65:ct * 130 + 129],
                                             in_=PS[6].bitcast(BF16)[:, 64:128])
                    t_ev = p.emit("dve", f_ev, deps=[t_tr, t_ms])
                    ps_free[6] = t_ev
                    ttr.append(t_ev)

                t_oe_prev = [None, None]
                t_pt_prev = None
                scan_tail = []
                for hl in range(2):
                    for qseg in range(4):
                        kt_hi = (qseg * 4 + 4) if causal else 16
                        t_av = None
                        for kt in range(kt_hi):
                            def f_s(e, hl=hl, kt=kt, qseg=qseg):
                                return e.matmul(
                                    PS[0][:, :],
                                    K[hl * 64:(hl + 1) * 64, kt * 128:(kt + 1) * 128],
                                    Q[hl * 64:(hl + 1) * 64,
                                      qseg * 512:(qseg + 1) * 512],
                                    start=True, stop=True)
                            ds = [tq[qseg], tk_[kt // 4]]
                            if t_pt_prev is not None:
                                ds.append(t_pt_prev)
                            t_s = p.emit("pe", f_s, deps=ds)
                            t_pt = p.emit("act", lambda e: e.activation(
                                out=PT, in_=PS[0][:, :], func=AF.Exp), deps=[t_s])
                            if causal and qseg * 512 <= kt * 128 < (qseg + 1) * 512:
                                off = kt * 128 - qseg * 512

                                def f_mask(e, off=off):
                                    r = e.tensor_tensor(
                                        out=PT[:, off:off + 128],
                                        in0=PT[:, off:off + 128], in1=STRI[:],
                                        op=OP.mult)
                                    if off > 0:
                                        r = e.memset(PT[:, 0:off], 0.0)
                                    return r
                                t_msk = p.emit("dve", f_mask, deps=[t_pt])
                            else:
                                t_msk = t_pt
                            t_pt_prev = t_msk

                            def f_av(e, hl=hl, kt=kt, first=(kt == 0),
                                     last=(kt == kt_hi - 1)):
                                return e.matmul(
                                    PS[2 + hl][0:65, :],
                                    VAUG[:, kt * 130 + hl * 65:kt * 130 + (hl + 1) * 65],
                                    PT, start=first, stop=last)
                            da = [t_msk, ttr[kt]]
                            if t_av is not None:
                                da.append(t_av)
                            if kt == 0 and t_oe_prev[hl] is not None:
                                da.append(t_oe_prev[hl])
                            t_av = p.emit("pe", f_av, deps=da)

                        def f_oe(e, g2=g2, hl=hl, qseg=qseg):
                            e.tensor_copy(
                                out=OFm[hl * 64:(hl + 1) * 64,
                                        g2 * 2048 + qseg * 512:
                                        g2 * 2048 + (qseg + 1) * 512],
                                in_=PS[2 + hl][0:64, :])
                            return e.tensor_copy(
                                out=DEN[hl * 64:hl * 64 + 1,
                                        qseg * 512:(qseg + 1) * 512],
                                in_=PS[2 + hl][64:65, :])
                        t_oe = p.emit("dve", f_oe, deps=[t_av])
                        t_oe_prev[hl] = t_oe
                        scan_tail.append(t_oe)
                t_dd = p.dma("sp", lambda e, g2=g2: [
                    e.dma_start(out=dscr[2 * g2:2 * g2 + 1, :], in_=DEN[0:1, :]),
                    e.dma_start(out=dscr[2 * g2 + 1:2 * g2 + 2, :],
                                in_=DEN[64:65, :])],
                    deps=scan_tail, sem="S_B", n=2)
                den_dump.append(t_dd)
                prev_scan = scan_tail + [t_dd]

            t_norm = []
            DBC = FA[:, 0:2048]
            for g2 in range(8):
                t_di = p.dma("sp", lambda e, g2=g2: [
                    e.dma_start(out=DBC[0:64, :],
                                in_=dscr[2 * g2:2 * g2 + 1, :].partition_broadcast(64)),
                    e.dma_start(out=DBC[64:128, :],
                                in_=dscr[2 * g2 + 1:2 * g2 + 2, :]
                                .partition_broadcast(64))],
                    deps=[den_dump[g2]] + (t_norm[-1:] if t_norm else []),
                    sem="S_B", n=2)

                def f_nm(e, g2=g2):
                    e.tensor_scalar(out=DBC, in0=DBC, scalar1=1e-6, scalar2=None,
                                    op0=OP.add)
                    recip_d(e, DBC, DBC)
                    return e.tensor_tensor(
                        out=OFm[:, g2 * 2048:(g2 + 1) * 2048],
                        in0=OFm[:, g2 * 2048:(g2 + 1) * 2048], in1=DBC, op=OP.mult)
                t_nm = p.emit("dve", f_nm, deps=[t_di] + prev_scan)
                t_norm.append(t_nm)

            comb_toks = []
            t_prev = None
            t_gld_prev = [None, None]
            for dchunk in range(NCH):
                b = dchunk % 2

                def f_wl(e, dchunk=dchunk, b=b):
                    return [e.dma_start(
                        out=WB[b][:, 0:1024].rearrange("p (c f) -> p c f", c=8),
                        in_=wo_ap[:, dchunk * 128:(dchunk + 1) * 128]
                        .rearrange("(c p) f -> p c f", p=128))]
                dps = list(deps0) + [t for t in
                                     (t_lw["S_WA"], t_lw["S_WB"]) if t]
                if t_prev is not None:
                    dps.append(t_prev)
                t_wl = p.dma("gp", f_wl, deps=dps,
                             sem=("S_WA" if b == 0 else "S_WB"))
                dgl = [g_last]
                if t_gld_prev[b] is not None:
                    dgl.append(t_gld_prev[b])
                t_gld = p.dma("gp", lambda e, dchunk=dchunk, b=b: [
                    e.dma_start(out=GST[b], in_=gscr[:, dchunk * T:(dchunk + 1) * T])],
                    deps=dgl, sem="S_G")
                qtoks = []
                for qb in range(4):
                    def f_mm(e, qb=qb, b=b):
                        r = None
                        for ec in range(NCH):
                            r = e.matmul(PS[qb][:, :], WB[b][:, ec * 128:(ec + 1) * 128],
                                         OFm[:, ec * 2048 + qb * 512:
                                             ec * 2048 + (qb + 1) * 512],
                                         start=(ec == 0), stop=(ec == NCH - 1))
                        return r
                    dmm = [t_wl] + t_norm
                    if comb_toks:
                        dmm.append(comb_toks[-1])
                    t_mm = p.emit("pe", f_mm, deps=dmm)

                    def f_cmb(e, qb=qb, dchunk=dchunk, b=b):
                        ps = PS[qb][:, :]
                        if bo_col is not None:
                            e.tensor_scalar(out=FS[:, 0:512], in0=ps,
                                            scalar1=bo_col(dchunk), scalar2=None,
                                            op0=OP.add)
                            e.tensor_tensor(out=FS[:, 0:512], in0=FS[:, 0:512],
                                            in1=GST[b][:, qb * 512:(qb + 1) * 512],
                                            op=OP.mult)
                        else:
                            e.tensor_tensor(out=FS[:, 0:512], in0=ps,
                                            in1=GST[b][:, qb * 512:(qb + 1) * 512],
                                            op=OP.mult)
                        return e.tensor_tensor(
                            out=xf(dchunk)[:, qb * 512:(qb + 1) * 512],
                            in0=xf(dchunk)[:, qb * 512:(qb + 1) * 512],
                            in1=FS[:, 0:512], op=OP.add)
                    t_cb = p.emit("dve", f_cmb, deps=[t_mm, t_gld])
                    qtoks.append(t_cb)
                    comb_toks.append(t_cb)
                t_gld_prev[b] = qtoks[-1]
                t_prev = qtoks[-1]
            return comb_toks

        def emit_l1(prev_comb):
            SMSK = WB[0][:, 1536:1792]
            t_smsk = p.dma("sp", lambda e: [e.dma_start(out=SMSK, in_=dt["lmask"])],
                           deps=prev_comb, sem="S_B")
            t_bc = emit_ln(prev_comb)
            RT = ROWS[:, 8:16]
            t_rt = p.dma("sp", lambda e: [e.dma_start(out=RT, in_=dt["l1_rt_q"])],
                deps=prev_comb, sem="S_B")
            s_ps = [(PS[4], 0), (PS[4], 32), (PS[4], 64), (PS[5], 0)]

            def per_chunk(c, t_fs):
                def f_s(e, c=c):
                    r = None
                    for qb in range(4):
                        tn, b0 = s_ps[qb]
                        r = e.matmul(tn[b0:b0 + 1, :], RT[:, c:c + 1],
                                     FS[:, qb * 512:(qb + 1) * 512],
                                     start=(c == 0), stop=(c == NCH - 1))
                    return r
                return [p.emit("pe", f_s, deps=[t_fs, t_rt])]
            xn_toks, last_fs = emit_xn(t_bc, prev_comb, per_chunk)
            SROW = FA[0:1, 6144:8192]

            def f_sev(e):
                r = None
                for qb in range(4):
                    tn, b0 = s_ps[qb]
                    r = e.tensor_copy(out=SROW[:, qb * 512:(qb + 1) * 512],
                                      in_=tn[b0:b0 + 1, :])
                return r
            t_sev = p.emit("dve", f_sev, deps=last_fs + xn_toks)

            A = FA[0:1, 4096:4097]
            NA = FA[0:1, 4097:4098]
            ES = FA[0:1, 4098:4099]
            L0R = FA[0:1, 4099:4100]
            E0R = FA[0:1, 4100:4101]
            U = FA[0:1, 0:2048]
            a1 = float(np.float32(np.float32(LOGK) -
                                  np.float32(np.log(np.float32(2048.0)))))

            def f_init(e):
                e.memset(A[:], a1)
                return e.memset(NA[:], -a1)
            t_it = p.emit("dve", f_init, deps=[t_sev])
            for it in range(49):
                t_u = p.emit("dve", lambda e: e.tensor_scalar(
                    out=U, in0=SROW, scalar1=NA, scalar2=None, op0=OP.min),
                    deps=[t_it])
                t_e = p.emit("act", lambda e: e.activation(
                    out=U, in_=U, func=AF.Exp), deps=[t_u])
                t_er = p.emit("dve", lambda e: e.tensor_reduce(
                    out=ES, in_=U, axis=AX.X, op=OP.add), deps=[t_e])
                t_ua = p.emit("act", lambda e: (
                    e.activation(out=L0R, in_=ES, func=AF.Ln),
                    e.activation(out=E0R, in_=L0R, func=AF.Exp, scale=-1.0))[-1],
                    deps=[t_er])

                def f_upd(e):
                    e.tensor_tensor(out=E0R, in0=E0R, in1=ES, op=OP.mult)
                    e.scalar_tensor_tensor(out=L0R, in0=E0R, scalar=-1.0, in1=L0R,
                                           op0=OP.add, op1=OP.add)
                    e.tensor_scalar(out=A, in0=L0R, scalar1=-1.0, scalar2=LOGK,
                                    op0=OP.mult, op1=OP.add)
                    return e.tensor_scalar(out=NA, in0=L0R, scalar1=LOGK,
                                           scalar2=None, op0=OP.subtract)
                t_it = p.emit("dve", f_upd, deps=[t_ua])

            SAT = FA[0:1, 2048:4096]
            PRE = SROW

            def f_sel(e):
                return e.tensor_scalar(out=SAT, in0=SROW, scalar1=A, scalar2=0.0,
                                       op0=OP.add, op1=OP.is_ge)
            t_sat = p.emit("dve", f_sel, deps=[t_it])

            def f_pref(e):
                e.memset(U, 0.0)
                e.tensor_tensor_scan(out=PRE, data0=SAT, data1=U,
                                     initial=0.0, op0=OP.add, op1=OP.add)
                e.tensor_scalar(out=PRE, in0=PRE, scalar1=1024.0, scalar2=None,
                                op0=OP.is_le)
                return e.tensor_tensor(out=SAT, in0=SAT, in1=PRE, op=OP.mult)
            t_selr = p.emit("dve", f_pref, deps=[t_sat])
            t_selo = p.dma("sp", lambda e: [e.dma_start(out=dscr[32:33, :],
                                                        in_=SAT)],
                           deps=[t_selr], sem="S_B")

            t_mm2 = emit_stats(xn, xn_toks + [t_selo], PS[6], PS[7])
            TM = FA[0:1, 4112:4624]
            TV = FA[0:1, 4624:5136]
            TT1 = FA[0:1, 5136:5648]
            R2_ALL = FA[0:1, 0:2048]
            M2R_ALL = FA[0:1, 2048:4096]
            CR_ALL = FA[0:1, 6144:8192]
            t_lr = None
            for qb in range(4):
                def f_q(e, qb=qb):
                    cr = CR_ALL[:, qb * 512:(qb + 1) * 512]
                    e.tensor_copy(out=cr, in_=PS[4 + qb][32:33, :])
                    e.tensor_scalar(out=TM, in0=PS[4 + qb][0:1, :],
                                    scalar1=1.0 / 1024.0, scalar2=None, op0=OP.mult)
                    e.tensor_scalar(out=TV, in0=PS[4 + qb][32:33, :],
                                    scalar1=1.0 / 1024.0, scalar2=None, op0=OP.mult)
                    e.tensor_tensor(out=TT1, in0=TM, in1=TM, op=OP.mult)
                    e.tensor_tensor(out=TV, in0=TV, in1=TT1, op=OP.subtract)
                    return e.tensor_scalar(out=TV, in0=TV, scalar1=1e-5,
                                           scalar2=None, op0=OP.add)
                t_a = p.emit("dve", f_q, deps=[t_mm2] + ([t_lr] if t_lr else []))
                t_b = p.emit("act", lambda e, qb=qb: e.activation(
                    out=R2_ALL[:, qb * 512:(qb + 1) * 512], in_=TV, func=AF.Sqrt),
                    deps=[t_a])

                def f_n(e, qb=qb):
                    rr = R2_ALL[:, qb * 512:(qb + 1) * 512]
                    mr = M2R_ALL[:, qb * 512:(qb + 1) * 512]
                    emit_rsqrt_newton(e, TV, rr, TT1)
                    return e.tensor_tensor(out=mr, in0=TM, in1=rr, op=OP.mult)
                t_c = p.emit("dve", f_n, deps=[t_b])
                t_d = p.emit("act", lambda e, qb=qb: e.activation(
                    out=TT1, in_=CR_ALL[:, qb * 512:(qb + 1) * 512], func=AF.Sqrt),
                    deps=[t_c])

                def f_cr(e, qb=qb):
                    cr = CR_ALL[:, qb * 512:(qb + 1) * 512]
                    recip_d(e, TM, TT1)
                    for _ in range(2):
                        e.tensor_tensor(out=TV, in0=TM, in1=TM, op=OP.mult)
                        e.tensor_tensor(out=TV, in0=TV, in1=cr, op=OP.mult)
                        e.tensor_scalar(out=TV, in0=TV, scalar1=-0.5, scalar2=1.5,
                                        op0=OP.mult, op1=OP.add)
                        e.tensor_tensor(out=TM, in0=TM, in1=TV, op=OP.mult)
                    return e.tensor_scalar(out=cr, in0=TM, scalar1=32.0,
                                           scalar2=None, op0=OP.mult)
                t_lr = p.emit("dve", f_cr, deps=[t_d])
            BC2R = FA[:, 0:2048]
            BC2M = FA[:, 2048:4096]
            t_bc2 = bounce([(R2_ALL, BC2R, 128), (M2R_ALL, BC2M, 128)], [t_lr])
            CBC = FA[:, 6144:8192]
            t_cbc = bounce([(CR_ALL, CBC, 128)], [t_bc2, t_lr])

            GCOL = ROWS[:, 16:24]
            BCOL = ROWS[:, 24:32]
            t_gb = p.dma("sp", lambda e: [
                e.dma_start(out=GCOL, in_=dt["l1_ln_g"]),
                e.dma_start(out=BCOL, in_=dt["l1_ln_b"])], deps=prev_comb,
                sem="S_B", n=2)

            def x2(c):
                return SLOT2[:, c * T:(c + 1) * T]
            x2_toks = []
            t_px = None
            for c in range(NCH):
                def f_x2(e, c=c):
                    e.tensor_tensor(out=FS[:], in0=xn(c), in1=BC2R[:], op=OP.mult)
                    e.tensor_tensor(out=FS[:], in0=FS[:], in1=BC2M[:], op=OP.subtract)
                    e.tensor_scalar(out=FS[:], in0=FS[:], scalar1=GCOL[:, c:c + 1],
                                    scalar2=None, op0=OP.mult)
                    e.tensor_scalar(out=FS[:], in0=FS[:], scalar1=BCOL[:, c:c + 1],
                                    scalar2=None, op0=OP.add)
                    return e.tensor_copy(out=x2(c), in_=FS[:])
                dps = [t_bc2, t_gb] + xn_toks + ([t_px] if t_px else [])
                t_px = p.emit("dve", f_x2, deps=dps)
                x2_toks.append(t_px)

            QL = SLOTA[:, 0:2048]
            KL = SLOTA[:, 2048:4096]
            VSTL = SLOTA[:, 4096:6144]
            VAUGL = SLOTA[:, 6144:8192]
            OL = SLOTA[:, 8192:16384]
            DEN = FA[:, 2048:4096]
            t_lwl = {"S_WA": None, "S_WB": None}
            den_dumpl = []
            prev_scanl = [t_cbc, t_smsk]
            ps_freel = [None] * 8
            for lg in range(4):
                def f_wq(e, lg=lg):
                    return [e.dma_start(
                        out=WB[0][:, 0:1024].rearrange("p (c f) -> p c f", c=8),
                        in_=dt["l1_wqkv_l"][:, lg * 128:(lg + 1) * 128]
                        .rearrange("(c p) f -> p c f", p=128))]
                dq = [t_gb] + ([t_lwl["S_WA"]] if t_lwl["S_WA"] else [])
                t_wq = p.dma("gp", f_wq, deps=dq, sem="S_WA")

                def f_wkv(e, lg=lg):
                    return [e.dma_start(
                        out=WB[1][:, 0:1024].rearrange("p (c f) -> p c f", c=8),
                        in_=dt["l1_wqkv_l"][:, 512 + lg * 128:512 + (lg + 1) * 128]
                        .rearrange("(c p) f -> p c f", p=128)),
                        e.dma_start(
                        out=WB[1][:, 1024:2048].rearrange("p (c f) -> p c f", c=8),
                        in_=dt["l1_wqkv_l"][:, 1024 + lg * 128:1024 + (lg + 1) * 128]
                        .rearrange("(c p) f -> p c f", p=128))]
                dk = [t_gb] + ([t_lwl["S_WB"]] if t_lwl["S_WB"] else [])
                t_wkv = p.dma("gp", f_wkv, deps=dk, sem="S_WB", n=2)

                def projl(woff, wbuf, dst, t_w, scale):
                    toks = []
                    for qb in range(4):
                        def f_mm(e, qb=qb, woff=woff, wbuf=wbuf):
                            r = None
                            for c in range(NCH):
                                r = e.matmul(PS[4 + qb % 2][:, :],
                                             wbuf[:, woff + c * 128:woff + (c + 1) * 128],
                                             x2(c)[:, qb * 512:(qb + 1) * 512],
                                             start=(c == 0), stop=(c == NCH - 1))
                            return r
                        dmm = [t_w] + x2_toks + prev_scanl
                        if ps_freel[4 + qb % 2] is not None:
                            dmm.append(ps_freel[4 + qb % 2])
                        t_mm = p.emit("pe", f_mm, deps=dmm)

                        def f_ev(e, qb=qb, dst=dst, scale=scale):
                            ps = PS[4 + qb % 2][:, :]
                            d = dst[:, qb * 512:(qb + 1) * 512]
                            if scale != 1.0:
                                return e.tensor_scalar(out=d, in0=ps, scalar1=scale,
                                                       scalar2=None, op0=OP.mult)
                            return e.tensor_copy(out=d, in_=ps)
                        t_ev = p.emit("dve", f_ev, deps=[t_mm])
                        ps_freel[4 + qb % 2] = t_ev
                        toks.append(t_ev)
                    return toks
                tql = projl(0, WB[0], QL, t_wq, 0.125)
                tkl = projl(0, WB[1], KL, t_wkv, 1.0)
                tvl = projl(1024, WB[1], VSTL, t_wkv, 1.0)
                t_lwl["S_WA"] = tql[-1]
                t_lwl["S_WB"] = tvl[-1]
                ttr = []
                for ct in range(16):
                    def f_tr(e, ct=ct):
                        return e.transpose(PS[6].bitcast(BF16)[:, 0:128],
                                           VSTL[:, ct * 128:(ct + 1) * 128], SID[:])
                    dtr = [tvl[ct // 4]] + prev_scanl
                    if ttr:
                        dtr.append(ttr[-1])
                    if ps_freel[6] is not None:
                        dtr.append(ps_freel[6])
                    t_tr = p.emit("pe", f_tr, deps=dtr)

                    def f_ev(e, ct=ct):
                        return e.tensor_copy(
                            out=VAUGL[:, ct * 128:(ct + 1) * 128],
                            in_=PS[6].bitcast(BF16)[:, 0:128])
                    t_ev = p.emit("dve", f_ev, deps=[t_tr])
                    ps_freel[6] = t_ev
                    ttr.append(t_ev)
                t_oe_prev = [None, None]
                t_pt_prev = None
                scan_tail = []
                PTT = WB[0][:, 1024:1152]
                for hl in range(2):
                    for m in range(16):
                        tiles = []
                        if m > 0:
                            tiles.append((m - 1, 0))
                        tiles.append((m, None))
                        if m < 15:
                            tiles.append((m + 1, 128))
                        t_av = None
                        for ti, (kt, moff) in enumerate(tiles):
                            def f_s(e, hl=hl, kt=kt, m=m):
                                return e.matmul(
                                    PS[0][:, 0:128],
                                    KL[hl * 64:(hl + 1) * 64, kt * 128:(kt + 1) * 128],
                                    QL[hl * 64:(hl + 1) * 64, m * 128:(m + 1) * 128],
                                    start=True, stop=True)
                            ds = [tql[m // 4], tkl[kt // 4]]
                            if t_pt_prev is not None:
                                ds.append(t_pt_prev)
                            t_s = p.emit("pe", f_s, deps=ds)
                            t_pt = p.emit("act", lambda e: e.activation(
                                out=PTT, in_=PS[0][:, 0:128], func=AF.Exp),
                                deps=[t_s])
                            if moff is not None:
                                t_mk = p.emit("dve", lambda e, moff=moff:
                                              e.tensor_tensor(
                                                  out=PTT, in0=PTT,
                                                  in1=SMSK[:, moff:moff + 128],
                                                  op=OP.mult), deps=[t_pt])
                            else:
                                t_mk = t_pt
                            t_pt_prev = t_mk

                            def f_av(e, hl=hl, kt=kt, first=(ti == 0),
                                     last=(ti == len(tiles) - 1)):
                                e.matmul(
                                    PS[2 + hl][0:64, 0:128],
                                    VAUGL[:, kt * 128 + hl * 64:
                                          kt * 128 + (hl + 1) * 64],
                                    PTT, start=first, stop=last)
                                return e.matmul(
                                    PS[5][hl * 64:hl * 64 + 1, 0:128],
                                    ONESB[:], PTT, start=first, stop=last)
                            da = [t_mk, ttr[kt], tok_ones]
                            if t_av is not None:
                                da.append(t_av)
                            if ti == 0 and t_oe_prev[hl] is not None:
                                da.append(t_oe_prev[hl])
                            t_av = p.emit("pe", f_av, deps=da)

                        def f_oe(e, lg=lg, hl=hl, m=m):
                            e.tensor_copy(
                                out=OL[hl * 64:(hl + 1) * 64,
                                       lg * 2048 + m * 128:lg * 2048 + (m + 1) * 128],
                                in_=PS[2 + hl][0:64, 0:128])
                            return e.tensor_copy(
                                out=DEN[hl * 64:hl * 64 + 1, m * 128:(m + 1) * 128],
                                in_=PS[5][hl * 64:hl * 64 + 1, 0:128])
                        t_oe = p.emit("dve", f_oe, deps=[t_av])
                        t_oe_prev[hl] = t_oe
                        scan_tail.append(t_oe)
                t_dd = p.dma("sp", lambda e, lg=lg: [
                    e.dma_start(out=dscr[16 + 2 * lg:16 + 2 * lg + 1, :],
                                in_=DEN[0:1, :]),
                    e.dma_start(out=dscr[16 + 2 * lg + 1:16 + 2 * lg + 2, :],
                                in_=DEN[64:65, :])],
                    deps=scan_tail, sem="S_B", n=2)
                den_dumpl.append(t_dd)
                prev_scanl = scan_tail + [t_dd]
            t_norml = []
            DBCL = FA[:, 4096:6144]
            for lg in range(4):
                t_di = p.dma("sp", lambda e, lg=lg: [
                    e.dma_start(out=DBCL[0:64, :],
                                in_=dscr[16 + 2 * lg:16 + 2 * lg + 1, :]
                                .partition_broadcast(64)),
                    e.dma_start(out=DBCL[64:128, :],
                                in_=dscr[16 + 2 * lg + 1:16 + 2 * lg + 2, :]
                                .partition_broadcast(64))],
                    deps=[den_dumpl[lg]] + (t_norml[-1:] if t_norml else []),
                    sem="S_B", n=2)

                def f_nm(e, lg=lg):
                    e.tensor_scalar(out=DBCL, in0=DBCL, scalar1=1e-6, scalar2=None,
                                    op0=OP.add)
                    recip_d(e, DBCL, DBCL)
                    return e.tensor_tensor(
                        out=OL[:, lg * 2048:(lg + 1) * 2048],
                        in0=OL[:, lg * 2048:(lg + 1) * 2048], in1=DBCL, op=OP.mult)
                t_nm = p.emit("dve", f_nm, deps=[t_di] + prev_scanl)
                t_norml.append(t_nm)
            LOUT = SLOT2
            lo_toks = []
            t_prev = None
            for dchunk in range(NCH):
                b = dchunk % 2

                def f_wl(e, dchunk=dchunk, b=b):
                    return [e.dma_start(
                        out=WB[b][:, 0:512].rearrange("p (c f) -> p c f", c=4),
                        in_=dt["l1_wout_l"][:, dchunk * 128:(dchunk + 1) * 128]
                        .rearrange("(c p) f -> p c f", p=128))]
                dps = [t_gb] + x2_toks + \
                    [t for t in (t_lwl["S_WA"], t_lwl["S_WB"]) if t]
                if t_prev is not None:
                    dps.append(t_prev)
                t_wl = p.dma("gp", f_wl, deps=dps,
                             sem=("S_WA" if b == 0 else "S_WB"))
                for qb in range(4):
                    def f_mm(e, qb=qb, b=b):
                        r = None
                        for ec in range(4):
                            r = e.matmul(PS[qb][:, :], WB[b][:, ec * 128:(ec + 1) * 128],
                                         OL[:, ec * 2048 + qb * 512:
                                            ec * 2048 + (qb + 1) * 512],
                                         start=(ec == 0), stop=(ec == 3))
                        return r
                    dmm = [t_wl] + t_norml
                    if lo_toks:
                        dmm.append(lo_toks[-1])
                    t_mm = p.emit("pe", f_mm, deps=dmm)
                    t_lo = p.emit("dve", lambda e, qb=qb, dchunk=dchunk:
                                  e.tensor_copy(
                                      out=LOUT[:, dchunk * 2048 + qb * 512:
                                               dchunk * 2048 + (qb + 1) * 512],
                                      in_=PS[qb][:, :]), deps=[t_mm])
                    lo_toks.append(t_lo)
                t_prev = lo_toks[-1]

            QH = SLOTA[:, 8192:10240]
            KH = SLOTA[:, 10240:12288]
            VAUGH = SLOTA[:, 12288:14368]
            NVAUG = SLOTA[0:1, 14368:14888]
            NKD = SLOTA[:, 14888:14896]
            GCOLR = ROWS[:, 32:40]
            NQCOL = ROWS[:, 40:48]
            t_gr = p.dma("sp", lambda e: [
                e.dma_start(out=GCOLR, in_=dt["l1_rms_g"]),
                e.dma_start(out=NQCOL, in_=dt["l1_null_q"]),
                e.dma_start(out=NKD, in_=dt["l1_nkd"]),
                e.dma_start(out=NVAUG, in_=dt["l1_nvaug"])],
                deps=lo_toks, sem="S_B", n=4)
            PNUL = WB[1][0:1, 0:2048]
            t_hw = {"S_WA": None, "S_WB": None}
            den_dumph = []
            prev_scanh = lo_toks[-1:]
            ps_freeh = [None] * 8
            for hg in range(4):
                def f_wq(e, hg=hg):
                    return [e.dma_start(
                        out=WB[0][:, 0:1024].rearrange("p (c f) -> p c f", c=8),
                        in_=dt["l1_wq_h"][:, hg * 128:(hg + 1) * 128]
                        .rearrange("(c p) f -> p c f", p=128))]
                dq = [t_gr] + ([t_hw["S_WA"]] if t_hw["S_WA"] else [])
                t_wq = p.dma("gp", f_wq, deps=dq, sem="S_WA")

                def f_wqs(e):
                    r = None
                    for cc in range(NCH):
                        r = e.tensor_scalar(out=WB[0][:, cc * 128:(cc + 1) * 128],
                                            in0=WB[0][:, cc * 128:(cc + 1) * 128],
                                            scalar1=GCOLR[:, cc:cc + 1],
                                            scalar2=None, op0=OP.mult)
                    return r
                t_wqs = p.emit("dve", f_wqs, deps=[t_wq])

                def f_wkv(e, hg=hg):
                    return [e.dma_start(
                        out=WB[1][:, 0:2048].rearrange("p (c f) -> p c f", c=8),
                        in_=dt["l1_wkv_h"][:, hg * 256:(hg + 1) * 256]
                        .rearrange("(c p) f -> p c f", p=128))]
                dk = [t_gr] + prev_scanh + ([t_hw["S_WB"]] if t_hw["S_WB"] else [])
                t_wkv = p.dma("gp", f_wkv, deps=dk, sem="S_WB")

                def f_wks(e):
                    r = None
                    for cc in range(NCH):
                        r = e.tensor_scalar(out=WB[1][:, cc * 256:(cc + 1) * 256],
                                            in0=WB[1][:, cc * 256:(cc + 1) * 256],
                                            scalar1=GCOLR[:, cc:cc + 1],
                                            scalar2=None, op0=OP.mult)
                    return r
                t_wks = p.emit("dve", f_wks, deps=[t_wkv])

                VSTH = SLOTA[:, hg * 2048:(hg + 1) * 2048]
                tqh = []
                for qb in range(4):
                    def f_mm(e, qb=qb):
                        r = None
                        for c in range(NCH):
                            r = e.matmul(PS[4 + qb % 2][:, :],
                                         WB[0][:, c * 128:(c + 1) * 128],
                                         xn(c)[:, qb * 512:(qb + 1) * 512],
                                         start=(c == 0), stop=(c == NCH - 1))
                        return r
                    dmm = [t_wqs] + xn_toks + prev_scanh
                    if ps_freeh[4 + qb % 2] is not None:
                        dmm.append(ps_freeh[4 + qb % 2])
                    t_mm = p.emit("pe", f_mm, deps=dmm)

                    def f_ev(e, qb=qb):
                        e.scalar_tensor_tensor(
                            out=FS[:, 0:512], in0=PS[4 + qb % 2][:, :], scalar=0.125,
                            in1=CBC[:, qb * 512:(qb + 1) * 512], op0=OP.mult,
                            op1=OP.mult)
                        return e.tensor_copy(out=QH[:, qb * 512:(qb + 1) * 512],
                                             in_=FS[:, 0:512])
                    t_ev = p.emit("dve", f_ev, deps=[t_mm, t_cbc])
                    ps_freeh[4 + qb % 2] = t_ev
                    tqh.append(t_ev)
                t_hw["S_WA"] = tqh[-1]
                tkh = []
                tvh = []
                for qb in range(4):
                    def f_mmk(e, qb=qb):
                        r = None
                        for c in range(NCH):
                            r = e.matmul(PS[4 + qb % 2][0:64, :],
                                         WB[1][:, c * 256:c * 256 + 64],
                                         xn(c)[:, qb * 512:(qb + 1) * 512],
                                         start=(c == 0), stop=(c == NCH - 1))
                        for c in range(NCH):
                            r = e.matmul(PS[4 + qb % 2][64:128, :],
                                         WB[1][:, c * 256 + 128:c * 256 + 192],
                                         xn(c)[:, qb * 512:(qb + 1) * 512],
                                         start=(c == 0), stop=(c == NCH - 1))
                        return r
                    dmm = [t_wks] + xn_toks
                    if ps_freeh[4 + qb % 2] is not None:
                        dmm.append(ps_freeh[4 + qb % 2])
                    t_mm = p.emit("pe", f_mmk, deps=dmm)

                    def f_ev(e, qb=qb):
                        e.tensor_tensor(out=FS[:, 0:512], in0=PS[4 + qb % 2][:, :],
                                        in1=CBC[:, qb * 512:(qb + 1) * 512],
                                        op=OP.mult)
                        return e.tensor_copy(out=KH[:, qb * 512:(qb + 1) * 512],
                                             in_=FS[:, 0:512])
                    t_ev = p.emit("dve", f_ev, deps=[t_mm, t_cbc])
                    ps_freeh[4 + qb % 2] = t_ev
                    tkh.append(t_ev)
                for qb in range(4):
                    def f_mmv(e, qb=qb):
                        r = None
                        for c in range(NCH):
                            r = e.matmul(PS[6][0:64, :],
                                         WB[1][:, c * 256 + 64:c * 256 + 128],
                                         xn(c)[:, qb * 512:(qb + 1) * 512],
                                         start=(c == 0), stop=(c == NCH - 1))
                        for c in range(NCH):
                            r = e.matmul(PS[6][64:128, :],
                                         WB[1][:, c * 256 + 192:c * 256 + 256],
                                         xn(c)[:, qb * 512:(qb + 1) * 512],
                                         start=(c == 0), stop=(c == NCH - 1))
                        return r
                    dmm = [t_wks] + xn_toks
                    if ps_freeh[6] is not None:
                        dmm.append(ps_freeh[6])
                    t_mm = p.emit("pe", f_mmv, deps=dmm)

                    def f_ev(e, qb=qb, hg=hg, VSTH=VSTH):
                        e.tensor_tensor(out=FS[:, 0:512], in0=PS[6][:, :],
                                        in1=CBC[:, qb * 512:(qb + 1) * 512],
                                        op=OP.mult)
                        return e.tensor_copy(out=VSTH[:, qb * 512:(qb + 1) * 512],
                                             in_=FS[:, 0:512])
                    t_ev = p.emit("dve", f_ev, deps=[t_mm, t_cbc])
                    ps_freeh[6] = t_ev
                    tvh.append(t_ev)
                t_hw["S_WB"] = tvh[-1]
                t_ms = p.emit("dve", lambda e: e.memset(VAUGH[:], 1.0),
                              deps=prev_scanh)
                ttr = []
                for ct in range(16):
                    def f_tr(e, ct=ct, VSTH=VSTH):
                        return e.transpose(PS[7].bitcast(BF16)[:, 0:128],
                                           VSTH[:, ct * 128:(ct + 1) * 128], SID[:])
                    dtr = [tvh[ct // 4]]
                    if ttr:
                        dtr.append(ttr[-1])
                    if ps_freeh[7] is not None:
                        dtr.append(ps_freeh[7])
                    t_tr = p.emit("pe", f_tr, deps=dtr)

                    def f_ev(e, ct=ct):
                        e.tensor_copy(out=VAUGH[:, ct * 130:ct * 130 + 64],
                                      in_=PS[7].bitcast(BF16)[:, 0:64])
                        return e.tensor_copy(
                            out=VAUGH[:, ct * 130 + 65:ct * 130 + 129],
                            in_=PS[7].bitcast(BF16)[:, 64:128])
                    t_ev = p.emit("dve", f_ev, deps=[t_tr, t_ms])
                    ps_freeh[7] = t_ev
                    ttr.append(t_ev)
                t_oe_prev = [None, None]
                t_pt_prev = None
                scan_tail = []
                PTH = WB[0][:, 1024:1536]
                for hl in range(2):
                    h = 2 * hg + hl
                    t_pn = []
                    for qb in range(4):
                        def f_sn(e, hl=hl, h=h, qb=qb):
                            return e.matmul(PS[1][0:1, :],
                                            NKD[hl * 64:(hl + 1) * 64, h:h + 1],
                                            QH[hl * 64:(hl + 1) * 64,
                                               qb * 512:(qb + 1) * 512],
                                            start=True, stop=True)
                        dsn = [tqh[qb], t_gr]
                        if t_pn:
                            dsn.append(t_pn[-1])
                        t_s = p.emit("pe", f_sn, deps=dsn)
                        t_e = p.emit("act", lambda e, qb=qb: e.activation(
                            out=PNUL[:, qb * 512:(qb + 1) * 512],
                            in_=PS[1][0:1, :], func=AF.Exp), deps=[t_s])
                        t_pn.append(t_e)
                    for qseg in range(4):
                        davn = [t_pn[qseg], t_gr]
                        if t_oe_prev[hl] is not None:
                            davn.append(t_oe_prev[hl])
                        t_av = p.emit("pe", lambda e, hl=hl, h=h, qseg=qseg:
                                      e.matmul(PS[2 + hl][0:65, :],
                                               NVAUG[:, h * 65:(h + 1) * 65],
                                               PNUL[:, qseg * 512:(qseg + 1) * 512],
                                               start=True, stop=False), deps=davn)
                        for kt in range(16):
                            def f_s(e, hl=hl, kt=kt, qseg=qseg):
                                return e.matmul(
                                    PS[0][:, :],
                                    KH[hl * 64:(hl + 1) * 64, kt * 128:(kt + 1) * 128],
                                    QH[hl * 64:(hl + 1) * 64,
                                       qseg * 512:(qseg + 1) * 512],
                                    start=True, stop=True)
                            ds = [tqh[qseg], tkh[kt // 4]]
                            if t_pt_prev is not None:
                                ds.append(t_pt_prev)
                            t_s = p.emit("pe", f_s, deps=ds)
                            t_pt = p.emit("act", lambda e: e.activation(
                                out=PTH, in_=PS[0][:, :], func=AF.Exp), deps=[t_s])
                            t_pt_prev = t_pt
                            t_av = p.emit("pe", lambda e, hl=hl, kt=kt: e.matmul(
                                PS[2 + hl][0:65, :],
                                VAUGH[:, kt * 130 + hl * 65:kt * 130 + (hl + 1) * 65],
                                PTH, start=False, stop=(kt == 15)),
                                deps=[t_pt, ttr[kt], t_av])

                        def f_oe(e, hg=hg, hl=hl, qseg=qseg):
                            e.tensor_copy(
                                out=SLOTA[hl * 64:(hl + 1) * 64,
                                          hg * 2048 + qseg * 512:
                                          hg * 2048 + (qseg + 1) * 512],
                                in_=PS[2 + hl][0:64, :])
                            return e.tensor_copy(
                                out=DEN[hl * 64:hl * 64 + 1,
                                        qseg * 512:(qseg + 1) * 512],
                                in_=PS[2 + hl][64:65, :])
                        t_oe = p.emit("dve", f_oe, deps=[t_av])
                        t_oe_prev[hl] = t_oe
                        scan_tail.append(t_oe)
                t_dd = p.dma("sp", lambda e, hg=hg: [
                    e.dma_start(out=dscr[24 + 2 * hg:24 + 2 * hg + 1, :],
                                in_=DEN[0:1, :]),
                    e.dma_start(out=dscr[24 + 2 * hg + 1:24 + 2 * hg + 2, :],
                                in_=DEN[64:65, :])],
                    deps=scan_tail, sem="S_B", n=2)
                den_dumph.append(t_dd)
                prev_scanh = scan_tail + [t_dd]
            OH = SLOTA[:, 0:8192]
            t_normh = []
            DBC2 = FA[:, 4096:6144]
            for hg in range(4):
                t_di = p.dma("sp", lambda e, hg=hg: [
                    e.dma_start(out=DBC2[0:64, :],
                                in_=dscr[24 + 2 * hg:24 + 2 * hg + 1, :]
                                .partition_broadcast(64)),
                    e.dma_start(out=DBC2[64:128, :],
                                in_=dscr[24 + 2 * hg + 1:24 + 2 * hg + 2, :]
                                .partition_broadcast(64))],
                    deps=[den_dumph[hg]] + (t_normh[-1:] if t_normh else []),
                    sem="S_B", n=2)

                def f_nm(e, hg=hg):
                    e.tensor_scalar(out=DBC2, in0=DBC2, scalar1=1e-6, scalar2=None,
                                    op0=OP.add)
                    recip_d(e, DBC2, DBC2)
                    return e.tensor_tensor(
                        out=OH[:, hg * 2048:(hg + 1) * 2048],
                        in0=OH[:, hg * 2048:(hg + 1) * 2048], in1=DBC2, op=OP.mult)
                t_nm = p.emit("dve", f_nm, deps=[t_di] + prev_scanh)
                t_normh.append(t_nm)
            SELB = FA[:, 2048:4096]
            t_selb = p.dma("sp", lambda e: [e.dma_start(
                out=SELB, in_=dscr[32:33, :].partition_broadcast(128))],
                deps=[t_selo] + x2_toks + t_normh, sem="S_B")
            g_toks1 = []
            t_prev = None
            for dchunk in range(NCH):
                b = dchunk % 2

                def f_wl(e, dchunk=dchunk, b=b):
                    return [e.dma_start(
                        out=WB[b][:, 0:512].rearrange("p (c f) -> p c f", c=4),
                        in_=dt["l1_wout_h"][:, dchunk * 128:(dchunk + 1) * 128]
                        .rearrange("(c p) f -> p c f", p=128))]
                dps = [t_gr] + [t for t in
                                (t_hw["S_WA"], t_hw["S_WB"]) if t]
                if t_prev is not None:
                    dps.append(t_prev)
                t_wl = p.dma("gp", f_wl, deps=dps,
                             sem=("S_WA" if b == 0 else "S_WB"))
                for qb in range(4):
                    def f_mm(e, qb=qb, b=b):
                        r = None
                        for ec in range(4):
                            r = e.matmul(PS[qb][:, :], WB[b][:, ec * 128:(ec + 1) * 128],
                                         OH[:, ec * 2048 + qb * 512:
                                            ec * 2048 + (qb + 1) * 512],
                                         start=(ec == 0), stop=(ec == 3))
                        return r
                    dmm = [t_wl] + t_normh
                    if g_toks1:
                        dmm.append(g_toks1[-1])
                    t_mm = p.emit("pe", f_mm, deps=dmm)
                    gstg = SLOTA[:, 8192 + (qb % 2) * 512:8192 + (qb % 2) * 512 + 512]

                    def f_g(e, qb=qb, dchunk=dchunk):
                        e.tensor_scalar(out=FS[:, 0:512], in0=PS[qb][:, :],
                                        scalar1=NQCOL[:, dchunk:dchunk + 1],
                                        scalar2=None, op0=OP.subtract)
                        e.tensor_tensor(out=FS[:, 0:512], in0=FS[:, 0:512],
                                        in1=SELB[:, qb * 512:(qb + 1) * 512],
                                        op=OP.mult)
                        e.tensor_tensor(out=FS[:, 0:512], in0=FS[:, 0:512],
                                        in1=LOUT[:, dchunk * 2048 + qb * 512:
                                                 dchunk * 2048 + (qb + 1) * 512],
                                        op=OP.add)
                        return e.tensor_scalar(out=FS[:, 0:512], in0=FS[:, 0:512],
                                               scalar1=NQCOL[:, dchunk:dchunk + 1],
                                               scalar2=None, op0=OP.add)
                    t_gv = p.emit("dve", f_g, deps=[t_mm, t_selb, t_gr])
                    t_gs = p.emit("act", lambda e, gstg=gstg: e.activation(
                        out=gstg, in_=FS[:, 0:512], func=AF.Silu), deps=[t_gv])
                    t_sp = p.dma("gp", lambda e, dchunk=dchunk, qb=qb, gstg=gstg: [
                        e.dma_start(out=gscr[:, dchunk * T + qb * 512:
                                             dchunk * T + (qb + 1) * 512],
                                    in_=gstg)], deps=[t_gs], sem="S_G")
                    g_toks1.append(t_sp)
                t_prev = g_toks1[-1]
            return emit_mha([t_gr] + g_toks1[-1:], xn_toks, g_toks1[-1],
                            dt["l1_wq_a"], dt["l1_wk_a"], dt["l1_wv_a"],
                            dt["l1_wo"], 0.125)

        if nl >= 2:
            comb = emit_l1(comb)

        def bb3(a3, row2d):
            return bass.broadcast_tensor_aps(
                a3, row2d.rearrange("p (t o) -> p t o", o=1))

        def emit_router_layer(prev_comb, wkey, gwt_ap):
            pass

        def emit_l2(prev_comb):
            t_bc = emit_ln(prev_comb)
            GWT = FA[:, 6144:6208]
            LOG = FA[:, 6208:6336]
            SIDF3 = FA[:, 6336:6464]
            IOTT3 = FA[:, 6464:6592]
            EQ = FA[:, 6592:6720]
            TMP = FA[:, 6720:6848]
            BQK = FA[:, 7904:7928]
            BO2 = FA[:, 7928:7936]
            t_w0 = p.dma("sp", lambda e: [
                e.dma_start(out=GWT, in_=dt["l2_gwT"]),
                e.dma_start(out=BQK, in_=dt["l2_bqkv"]),
                e.dma_start(out=BO2, in_=dt["l2_bproj"]),
                e.dma_start(out=IOTT3, in_=dt["iott"])],
                deps=list(prev_comb) + [t_bc], sem="S_B", n=4)
            t_li = p.emit("dve", lambda e: e.memset(LOG, 0.0), deps=[t_bc])
            t_si3 = p.emit("dve", lambda e: e.tensor_copy(out=SIDF3, in_=SID[:]),
                           deps=[t_bc])

            def per_chunk(c, t_fs):
                def f_r(e, c=c):
                    r = None
                    for tt in range(16):
                        r = e.matmul(PS[5][:, tt * 8:(tt + 1) * 8],
                                     FS[:, tt * 128:(tt + 1) * 128],
                                     GWT[:, c * 8:(c + 1) * 8],
                                     start=True, stop=True)
                    return r
                t_r = p.emit("pe", f_r, deps=[t_fs, t_w0])
                t_a = p.emit("dve", lambda e: e.tensor_tensor(
                    out=LOG, in0=LOG, in1=PS[5][:, 0:128], op=OP.add),
                    deps=[t_r, t_li])
                return [t_a]
            xn_toks, last_us = emit_xn(t_bc, list(prev_comb), per_chunk)

            L3d = LOG.rearrange("p (t e) -> p t e", e=8)
            MX = ROWS[:, 8:24]
            RW = ROWS[:, 24:40]
            FMIN = ROWS[:, 40:56]
            t_mx = p.emit("dve", lambda e: e.tensor_reduce(
                out=MX, in_=L3d, axis=AX.X, op=OP.max), deps=last_us + xn_toks)
            t_rw = p.emit("act", lambda e: e.activation(
                out=RW, in_=MX, func=AF.Sigmoid), deps=[t_mx])

            def f_eq(e):
                i0, i1 = bb3(L3d, MX)
                e.tensor_tensor(out=EQ.rearrange("p (t e) -> p t e", e=8),
                                in0=i0, in1=i1, op=OP.is_equal)
                e.tensor_tensor(out=TMP, in0=EQ, in1=IOTT3, op=OP.mult)
                e.tensor_scalar(out=TMP, in0=TMP, scalar1=-1.0, scalar2=8.0,
                                op0=OP.mult, op1=OP.add)
                return e.tensor_reduce(
                    out=FMIN, in_=TMP.rearrange("p (t e) -> p t e", e=8),
                    axis=AX.X, op=OP.min)
            t_eq = p.emit("dve", f_eq, deps=[t_mx, t_w0])

            def f_msk(e):
                i0, i1 = bb3(TMP.rearrange("p (t e) -> p t e", e=8), FMIN)
                return e.tensor_tensor(
                    out=EQ.rearrange("p (t e) -> p t e", e=8),
                    in0=i0, in1=i1, op=OP.is_equal)
            t_msk = p.emit("dve", f_msk, deps=[t_eq])

            MSKR = FA[0:8, 0:2048]
            t_mt = []
            for tt in range(16):
                t_tr = p.emit("pe", lambda e, tt=tt: e.transpose(
                    PS[6][0:8, 0:128], EQ[:, tt * 8:(tt + 1) * 8], SIDF3),
                    deps=[t_msk, t_si3] + (t_mt[-1:] if t_mt else []))
                t_cp = p.emit("dve", lambda e, tt=tt: e.tensor_copy(
                    out=MSKR[:, tt * 128:(tt + 1) * 128], in_=PS[6][0:8, 0:128]),
                    deps=[t_tr])
                t_mt.append(t_cp)
            t_rt = p.emit("pe", lambda e: e.transpose(
                PS[6][0:16, 128:256], RW, SIDF3), deps=[t_rw, t_si3] + t_mt[-1:])
            R16C = FA[0:16, 2048:2176]
            t_rc = p.emit("dve", lambda e: e.tensor_copy(
                out=R16C, in_=PS[6][0:16, 128:256]), deps=[t_rt])
            t_msko = p.dma("sp", lambda e: [
                e.dma_start(out=dscr[0:8, :], in_=MSKR),
                e.dma_start(out=dscr[8:9, :].rearrange("o (t p) -> (o t) p", t=16),
                            in_=R16C)], deps=t_mt + [t_rc], sem="S_B", n=2)
            RWB = FA[:, 2048:4096]
            t_rwb = p.dma("sp", lambda e: [e.dma_start(
                out=RWB, in_=dscr[8:9, :].partition_broadcast(128))],
                deps=[t_msko], sem="S_B")

            def xr(c):
                return SLOTA[:, c * T:(c + 1) * T]
            xr_toks = []
            t_px = None
            for c in range(NCH):
                t_px = p.emit("dve", lambda e, c=c: e.tensor_tensor(
                    out=xr(c), in0=xn(c), in1=RWB, op=OP.mult),
                    deps=[t_rwb] + xn_toks + ([t_px] if t_px else []))
                xr_toks.append(t_px)

            MBC = FA[:, 0:512]
            H3 = SLOT2
            g2_toks = []
            t_wp = {"S_WA": None, "S_WB": None}
            t_hp = None
            t_spp = None
            for tb in range(4):
                t0 = tb * 512
                h_toks = []
                for ex in range(8):
                    t_mb = p.dma("sp", lambda e, ex=ex, t0=t0: [e.dma_start(
                        out=MBC, in_=dscr[ex:ex + 1, t0:t0 + 512]
                        .partition_broadcast(128))],
                        deps=[t_msko] + (h_toks[-1:] if h_toks else []) +
                        ([t_spp] if t_spp else []), sem="S_B")
                    for ph, wsrc in ((0, "l2_wgT"), (1, "l2_wuT")):
                        for oh in range(2):
                            b = oh
                            sname = "S_WA" if b == 0 else "S_WB"

                            def f_wl(e, ex=ex, oh=oh, b=b, wsrc=wsrc):
                                return [e.dma_start(
                                    out=WB[b][:, 0:2048].rearrange(
                                        "p (c f) -> p c f", c=8),
                                    in_=dt[wsrc + str(ex)][:, oh * 256:(oh + 1) * 256]
                                    .rearrange("(c p) f -> p c f", p=128))]
                            t_wl = p.dma("gp", f_wl, deps=list(prev_comb) +
                                         ([t_wp[sname]] if t_wp[sname] else []),
                                         sem=sname)
                            for sub in range(2):
                                hslot = H3[:, ex * 2048 + (oh * 2 + sub) * 512:
                                           ex * 2048 + (oh * 2 + sub) * 512 + 512]

                                def f_mm(e, b=b, sub=sub, t0=t0):
                                    r = None
                                    for c in range(NCH):
                                        r = e.matmul(
                                            PS[0][:, :],
                                            WB[b][:, c * 256 + sub * 128:
                                                  c * 256 + (sub + 1) * 128],
                                            xr(c)[:, t0:t0 + 512],
                                            start=(c == 0), stop=(c == NCH - 1))
                                    return r
                                t_mm = p.emit("pe", f_mm, deps=[t_wl] + xr_toks +
                                              ([t_hp] if t_hp else []))
                                if ph == 0:
                                    t_h = p.emit("act", lambda e, hslot=hslot:
                                                 e.activation(out=hslot,
                                                              in_=PS[0][:, :],
                                                              func=AF.Silu),
                                                 deps=[t_mm])
                                else:
                                    def f_h(e, hslot=hslot):
                                        e.tensor_tensor(out=FS[:, 0:512],
                                                        in0=PS[0][:, :],
                                                        in1=hslot, op=OP.mult)
                                        e.tensor_tensor(out=FS[:, 0:512],
                                                        in0=FS[:, 0:512],
                                                        in1=MBC, op=OP.mult)
                                        return e.tensor_copy(out=hslot,
                                                             in_=FS[:, 0:512])
                                    t_h = p.emit("dve", f_h, deps=[t_mm, t_mb])
                                t_hp = t_h
                                h_toks.append(t_h)
                            t_wp[sname] = t_hp
                for dchunk in range(NCH):
                    par = dchunk % 2
                    t_dm = None
                    for ex in range(8):
                        b = ex % 2
                        sname = "S_WA" if b == 0 else "S_WB"

                        def f_wl(e, ex=ex, dchunk=dchunk, b=b):
                            return [e.dma_start(
                                out=WB[b][:, 0:512].rearrange(
                                    "p (c f) -> p c f", c=4),
                                in_=dt["l2_wdT" + str(ex)][:, dchunk * 128:
                                                           (dchunk + 1) * 128]
                                .rearrange("(c p) f -> p c f", p=128))]
                        t_wd = p.dma("gp", f_wl, deps=h_toks[-1:] +
                                     ([t_wp[sname]] if t_wp[sname] else []),
                                     sem=sname)

                        def f_dmm(e, ex=ex, par=par, b=b):
                            r = None
                            for oc in range(4):
                                r = e.matmul(
                                    PS[2 + par][:, :],
                                    WB[b][:, oc * 128:(oc + 1) * 128],
                                    H3[:, ex * 2048 + oc * 512:
                                       ex * 2048 + (oc + 1) * 512],
                                    start=(ex == 0 and oc == 0),
                                    stop=(ex == 7 and oc == 3))
                            return r
                        ddm = [t_wd] + h_toks + ([t_dm] if t_dm else [])
                        if ex == 0 and t_spp is not None:
                            ddm.append(t_spp)
                        t_dm = p.emit("pe", f_dmm, deps=ddm)
                        t_wp[sname] = t_dm
                    t_gs = p.emit("act", lambda e, par=par: e.activation(
                        out=FS[:, 0:512], in_=PS[2 + par][:, :], func=AF.Silu),
                        deps=[t_dm] + ([t_spp] if t_spp else []))
                    t_sp = p.dma("gp", lambda e, dchunk=dchunk, t0=t0: [
                        e.dma_start(out=gscr[:, dchunk * T + t0:
                                             dchunk * T + t0 + 512],
                                    in_=FS[:, 0:512])], deps=[t_gs], sem="S_G")
                    t_spp = t_sp
                    g2_toks.append(t_sp)
            BQC = lambda g2: BQK[:, g2:g2 + 1]
            BKC = lambda g2: BQK[:, 8 + g2:8 + g2 + 1]
            BVC = lambda g2: BQK[:, 16 + g2:16 + g2 + 1]
            BOC = lambda dchunk: BO2[:, dchunk:dchunk + 1]
            return emit_mha([t_w0] + g2_toks[-1:], xn_toks, g2_toks[-1],
                            dt["l2_wq_a"], dt["l2_wk_a"], dt["l2_wv_a"],
                            dt["l2_wproj"], 0.125, causal=True,
                            bq_col=BQC, bk_col=BKC, bv_col=BVC, bo_col=BOC)

        if nl >= 3:
            comb = emit_l2(comb)

        def emit_l3(prev_comb):
            t_bc = emit_ln(prev_comb)
            RWT = FA[:, 6144:6208]
            LOG = FA[:, 6208:6336]
            SIDF3 = FA[:, 6336:6464]
            IOTT3 = FA[:, 6464:6592]
            EQ = FA[:, 6592:6720]
            TMP = FA[:, 6720:6848]
            M1 = FA[:, 6848:6976]
            M2 = FA[:, 6976:7104]
            L2ND = FA[:, 7104:7232]
            RBT = FA[:, 7232:7360]
            GWB = FA[:, 7360:7872]
            BO3 = ROWS[:, 56:64]
            PRIOR = ROWS[0:8, 0:1]
            V1 = ROWS[:, 8:24]
            V2 = ROWS[:, 24:40]
            FMIN = ROWS[:, 40:56]
            t_w0 = p.dma("sp", lambda e: [
                e.dma_start(out=RWT, in_=dt["l3_rwT"]),
                e.dma_start(out=RBT, in_=dt["l3_rbt"].partition_broadcast(128)),
                e.dma_start(out=BO3, in_=dt["l3_bo"]),
                e.dma_start(out=PRIOR, in_=dt["prior_mask"]),
                e.dma_start(out=IOTT3, in_=dt["iott"])],
                deps=list(prev_comb) + [t_bc], sem="S_B", n=5)
            t_li = p.emit("dve", lambda e: e.memset(LOG, 0.0), deps=[t_bc])
            t_si3 = p.emit("dve", lambda e: e.tensor_copy(out=SIDF3, in_=SID[:]),
                           deps=[t_bc])

            def per_chunk(c, t_fs):
                def f_r(e, c=c):
                    r = None
                    for tt in range(16):
                        r = e.matmul(PS[5][:, tt * 8:(tt + 1) * 8],
                                     FS[:, tt * 128:(tt + 1) * 128],
                                     RWT[:, c * 8:(c + 1) * 8],
                                     start=True, stop=True)
                    return r
                t_r = p.emit("pe", f_r, deps=[t_fs, t_w0])
                t_a = p.emit("dve", lambda e: e.tensor_tensor(
                    out=LOG, in0=LOG, in1=PS[5][:, 0:128], op=OP.add),
                    deps=[t_r, t_li])
                return [t_a]
            xn_toks, last_us = emit_xn(t_bc, list(prev_comb), per_chunk)
            t_lb = p.emit("dve", lambda e: e.tensor_tensor(
                out=LOG, in0=LOG, in1=RBT, op=OP.add),
                deps=last_us + xn_toks + [t_w0])

            L3d = LOG.rearrange("p (t e) -> p t e", e=8)
            t_v1 = p.emit("dve", lambda e: e.tensor_reduce(
                out=V1, in_=L3d, axis=AX.X, op=OP.max), deps=[t_lb])

            def fmatch(dst, src, vals, deps):
                def f_eq(e):
                    i0, i1 = bb3(src.rearrange("p (t e) -> p t e", e=8), vals)
                    e.tensor_tensor(out=EQ.rearrange("p (t e) -> p t e", e=8),
                                    in0=i0, in1=i1, op=OP.is_equal)
                    e.tensor_tensor(out=TMP, in0=EQ, in1=IOTT3, op=OP.mult)
                    e.tensor_scalar(out=TMP, in0=TMP, scalar1=-1.0, scalar2=8.0,
                                    op0=OP.mult, op1=OP.add)
                    return e.tensor_reduce(
                        out=FMIN, in_=TMP.rearrange("p (t e) -> p t e", e=8),
                        axis=AX.X, op=OP.min)
                t_e = p.emit("dve", f_eq, deps=deps)

                def f_m(e):
                    i0, i1 = bb3(TMP.rearrange("p (t e) -> p t e", e=8), FMIN)
                    return e.tensor_tensor(
                        out=dst.rearrange("p (t e) -> p t e", e=8),
                        in0=i0, in1=i1, op=OP.is_equal)
                return p.emit("dve", f_m, deps=[t_e])
            t_m1 = fmatch(M1, LOG, V1, [t_v1, t_w0])

            def f_l2(e):
                e.tensor_scalar(out=L2ND, in0=M1, scalar1=1e30, scalar2=None,
                                op0=OP.mult)
                e.tensor_tensor(out=L2ND, in0=LOG, in1=L2ND, op=OP.subtract)
                return e.tensor_reduce(out=V2,
                                       in_=L2ND.rearrange("p (t e) -> p t e", e=8),
                                       axis=AX.X, op=OP.max)
            t_v2 = p.emit("dve", f_l2, deps=[t_m1])
            t_m2 = fmatch(M2, L2ND, V2, [t_v2])

            t_d = p.emit("dve", lambda e: e.tensor_tensor(
                out=V2, in0=V2, in1=V1, op=OP.subtract), deps=[t_m2])
            t_ed = p.emit("act", lambda e: e.activation(
                out=V2, in_=V2, func=AF.Exp), deps=[t_d])

            def f_p(e):
                e.tensor_scalar(out=FMIN, in0=V2, scalar1=1.0, scalar2=None,
                                op0=OP.add)
                recip_d(e, FMIN, FMIN)
                return e.tensor_tensor(out=V2, in0=V2, in1=FMIN, op=OP.mult)
            t_p = p.emit("dve", f_p, deps=[t_ed])

            AEQ = EQ
            WTT = TMP

            def f_aw(e):
                e.tensor_tensor(out=AEQ, in0=M1, in1=M2, op=OP.add)
                i0, i1 = bb3(M1.rearrange("p (t e) -> p t e", e=8), FMIN)
                e.tensor_tensor(out=M1.rearrange("p (t e) -> p t e", e=8),
                                in0=i0, in1=i1, op=OP.mult)
                i0, i1 = bb3(M2.rearrange("p (t e) -> p t e", e=8), V2)
                e.tensor_tensor(out=M2.rearrange("p (t e) -> p t e", e=8),
                                in0=i0, in1=i1, op=OP.mult)
                return e.tensor_tensor(out=WTT, in0=M1, in1=M2, op=OP.add)
            t_aw = p.emit("dve", f_aw, deps=[t_p])

            AR = FA[0:8, 0:2048]
            WR = FA[0:8, 2048:4096]
            t_mt = []
            for tt in range(16):
                def f_tr(e, tt=tt):
                    e.transpose(PS[6][0:8, 0:128], AEQ[:, tt * 8:(tt + 1) * 8],
                                SIDF3)
                    return e.transpose(PS[6][0:8, 128:256],
                                       WTT[:, tt * 8:(tt + 1) * 8], SIDF3)
                t_tr = p.emit("pe", f_tr, deps=[t_aw, t_si3] +
                              (t_mt[-1:] if t_mt else []))

                def f_cp(e, tt=tt):
                    e.tensor_copy(out=AR[:, tt * 128:(tt + 1) * 128],
                                  in_=PS[6][0:8, 0:128])
                    return e.tensor_copy(out=WR[:, tt * 128:(tt + 1) * 128],
                                         in_=PS[6][0:8, 128:256])
                t_cp = p.emit("dve", f_cp, deps=[t_tr])
                t_mt.append(t_cp)

            ZR8 = FA[0:8, 4096:6144]
            PRE3 = FA[0:8, 6208:8256]

            def f_pre(e):
                e.memset(ZR8, 0.0)
                return e.tensor_tensor_scan(out=PRE3, data0=AR, data1=ZR8,
                                            initial=0.0, op0=OP.add, op1=OP.add)
            t_pre = p.emit("dve", f_pre, deps=t_mt)
            t_cnt = p.dma("gp", lambda e: [e.dma_start(
                out=coll_in.rearrange("o e -> e o"), in_=PRE3[:, 2047:2048])],
                deps=[t_pre], sem="S_C")
            t_coll = p.emit("gp", lambda e: e.collective_compute(
                "AllGather", OP.bypass, ins=[coll_in], outs=[coll_out],
                replica_groups=[[0, 1, 2, 3, 4, 5, 6, 7]]), deps=[t_cnt],
                sem="S_Q", inc=1)
            CNT = FA[0:8, 8256:8264]
            t_cl = p.dma("gp", lambda e: [e.dma_start(out=CNT, in_=coll_out)],
                         deps=[t_coll], sem="S_C")
            OFFC = FA[0:8, 8264:8265]
            t_off = p.emit("pe", lambda e: e.matmul(
                PS[6][0:8, 0:1], CNT, PRIOR, start=True, stop=True),
                deps=[t_cl, t_w0])
            t_oc = p.emit("dve", lambda e: e.tensor_copy(
                out=OFFC, in_=PS[6][0:8, 0:1]), deps=[t_off])

            def f_k(e):
                e.tensor_scalar(out=PRE3, in0=PRE3, scalar1=OFFC, scalar2=2048.0,
                                op0=OP.add, op1=OP.is_le)
                e.tensor_tensor(out=AR, in0=AR, in1=PRE3, op=OP.mult)
                return e.tensor_tensor(out=WR, in0=WR, in1=AR, op=OP.mult)
            t_k = p.emit("dve", f_k, deps=[t_oc, t_pre])
            t_gwo = p.dma("sp", lambda e: [e.dma_start(out=dscr[8:16, :],
                                                       in_=WR)],
                          deps=[t_k], sem="S_B")

            H33 = SLOTA
            ACC2 = SLOT2
            GWB4 = FA[:, 6208:8256]
            SBF = FA[:, 8256:8768]
            g3_toks = []
            t_wp = {"S_WA": None, "S_WB": None}
            t_hp = None
            t_adl = None
            t_az = p.emit("dve", lambda e: e.memset(ACC2, 0.0), deps=[t_k])
            for ex in range(8):
                t_gw = p.dma("sp", lambda e, ex=ex: [e.dma_start(
                    out=GWB4, in_=dscr[8 + ex:9 + ex, :]
                    .partition_broadcast(128))],
                    deps=[t_gwo] + ([t_hp] if t_hp else []), sem="S_B")
                for half in range(2):
                    h_toks = []
                    for hc8 in range(8):
                        hcg = half * 8 + hc8
                        b = hc8 % 2
                        sname = "S_WA" if b == 0 else "S_WB"

                        def f_wl(e, ex=ex, hcg=hcg, b=b):
                            return [e.dma_start(
                                out=WB[b][:, 0:1024].rearrange(
                                    "p (c f) -> p c f", c=8),
                                in_=dt["l3_W" + str(ex)][:, hcg * 128:
                                                         (hcg + 1) * 128]
                                .rearrange("(c p) f -> p c f", p=128)),
                                e.dma_start(
                                out=WB[b][:, 1024:2048].rearrange(
                                    "p (c f) -> p c f", c=8),
                                in_=dt["l3_V" + str(ex)][:, hcg * 128:
                                                         (hcg + 1) * 128]
                                .rearrange("(c p) f -> p c f", p=128))]
                        t_wl = p.dma("gp", f_wl, deps=[t_k] +
                                     ([t_wp[sname]] if t_wp[sname] else []),
                                     sem=sname, n=2)

                        def f_amm(e, b=b):
                            r = None
                            for tb4 in range(4):
                                for c in range(NCH):
                                    r = e.matmul(PS[tb4][:, :],
                                                 WB[b][:, c * 128:(c + 1) * 128],
                                                 xn(c)[:, tb4 * 512:
                                                       (tb4 + 1) * 512],
                                                 start=(c == 0),
                                                 stop=(c == NCH - 1))
                            for tb4 in range(4):
                                for c in range(NCH):
                                    r = e.matmul(PS[4 + tb4][:, :],
                                                 WB[b][:, 1024 + c * 128:
                                                       1024 + (c + 1) * 128],
                                                 xn(c)[:, tb4 * 512:
                                                       (tb4 + 1) * 512],
                                                 start=(c == 0),
                                                 stop=(c == NCH - 1))
                            return r
                        dam = [t_wl] + xn_toks + ([t_hp] if t_hp else [])
                        if t_adl is not None and hc8 == 0:
                            dam.append(t_adl)
                        t_am = p.emit("pe", f_amm, deps=dam)
                        for tb4 in range(4):
                            t_sb = p.emit("act", lambda e, tb4=tb4: e.activation(
                                out=SBF, in_=PS[4 + tb4][:, :], func=AF.Silu),
                                deps=[t_am] + ([t_hp] if t_hp else []))

                            def f_h3(e, hc8=hc8, tb4=tb4):
                                e.tensor_tensor(out=FS[:, 0:512],
                                                in0=PS[tb4][:, :],
                                                in1=SBF, op=OP.mult)
                                e.tensor_tensor(out=FS[:, 0:512],
                                                in0=FS[:, 0:512],
                                                in1=GWB4[:, tb4 * 512:
                                                         (tb4 + 1) * 512],
                                                op=OP.mult)
                                return e.tensor_copy(
                                    out=H33[:, hc8 * 2048 + tb4 * 512:
                                            hc8 * 2048 + tb4 * 512 + 512],
                                    in_=FS[:, 0:512])
                            t_h3 = p.emit("dve", f_h3, deps=[t_sb, t_gw])
                            t_hp = t_h3
                            h_toks.append(t_h3)
                        t_wp[sname] = t_hp
                    for dchunk in range(NCH):
                        b = dchunk % 2
                        sname = "S_WA" if b == 0 else "S_WB"

                        def f_wo(e, ex=ex, half=half, dchunk=dchunk, b=b):
                            return [e.dma_start(
                                out=WB[b][:, 0:1024].rearrange(
                                    "p (c f) -> p c f", c=8),
                                in_=dt["l3_Wout" + str(ex)][
                                    half * 1024:(half + 1) * 1024,
                                    dchunk * 128:(dchunk + 1) * 128]
                                .rearrange("(c p) f -> p c f", p=128))]
                        t_wo = p.dma("gp", f_wo, deps=h_toks[-1:] +
                                     ([t_wp[sname]] if t_wp[sname] else []),
                                     sem=sname)

                        def f_ymm(e, b=b):
                            r = None
                            for tb4 in range(4):
                                for hc8 in range(8):
                                    r = e.matmul(PS[tb4][:, :],
                                                 WB[b][:, hc8 * 128:
                                                       (hc8 + 1) * 128],
                                                 H33[:, hc8 * 2048 + tb4 * 512:
                                                     hc8 * 2048 +
                                                     (tb4 + 1) * 512],
                                                 start=(hc8 == 0),
                                                 stop=(hc8 == 7))
                            return r
                        dym = [t_wo] + h_toks + ([t_adl] if t_adl else [])
                        t_ym = p.emit("pe", f_ymm, deps=dym)
                        t_wp[sname] = t_ym

                        def f_ad(e, dchunk=dchunk,
                                 first=(ex == 0 and half == 0)):
                            r = None
                            for tb4 in range(4):
                                sl = ACC2[:, dchunk * 2048 + tb4 * 512:
                                          dchunk * 2048 + (tb4 + 1) * 512]
                                if first:
                                    r = e.tensor_copy(out=sl, in_=PS[tb4][:, :])
                                else:
                                    r = e.tensor_tensor(out=sl, in0=sl,
                                                        in1=PS[tb4][:, :],
                                                        op=OP.add)
                            return r
                        t_ad = p.emit("dve", f_ad, deps=[t_ym, t_az])
                        t_adl = t_ad
                        t_hp = t_ad
            t_spp = None
            for dchunk in range(NCH):
                for tb4 in range(4):
                    t_gs = p.emit("act", lambda e, dchunk=dchunk, tb4=tb4:
                                  e.activation(
                                      out=FS[:, 0:512],
                                      in_=ACC2[:, dchunk * 2048 + tb4 * 512:
                                               dchunk * 2048 + (tb4 + 1) * 512],
                                      func=AF.Silu),
                                  deps=[t_adl] + ([t_spp] if t_spp else []))
                    t_sp = p.dma("gp", lambda e, dchunk=dchunk, tb4=tb4: [
                        e.dma_start(out=gscr[:, dchunk * T + tb4 * 512:
                                             dchunk * T + (tb4 + 1) * 512],
                                    in_=FS[:, 0:512])], deps=[t_gs], sem="S_G")
                    g3_toks.append(t_sp)
                    t_spp = t_sp
            BOC = lambda dchunk: BO3[:, dchunk:dchunk + 1]
            return emit_mha([t_w0] + g3_toks[-1:], xn_toks, g3_toks[-1],
                            dt["l3_wq"], dt["l3_wk"], None, dt["l3_wo"],
                            1.0 / 32.0, v_from_k=True, bo_col=BOC)

        if nl >= 4:
            comb = emit_l3(comb)

        t_st = p.dma("sp", lambda e: [e.dma_start(
            out=xout.rearrange("(c p) t -> p c t", p=128),
            in_=XF[:].rearrange("p (c t) -> p c t", c=8))],
            deps=comb, sem="S_O")
        p.emit("sp", lambda e: None, deps=[t_st], inc=0)

        block = E(nc.Block())

        @block.tensor
        def _(e):
            for f in p.th["pe"]:
                f(e)

        @block.scalar
        def _(e):
            ep = VProxy(e)
            for f in p.th["act"]:
                f(ep)

        @block.vector
        def _(e):
            ep = VProxy(e)
            for f in p.th["dve"]:
                f(ep)

        @block.sync
        def _(e):
            for f in p.th["sp"]:
                f(e)

        @block.gpsimd
        def _(e):
            for f in p.th["gp"]:
                f(e)

    return nc


def prep_inputs(inp, nl=4):
    f32 = np.float32
    base = {}
    base["ident"] = np.eye(128, dtype=BFDT)
    base["tri"] = (np.arange(128)[:, None] <= np.arange(128)[None, :]).astype(BFDT)
    if nl >= 2:
        w = np.transpose(inp["L1_attn_w_qkv"], (2, 1, 0, 3)).reshape(D, 3 * D)
        base["l1_wq_a"] = np.ascontiguousarray(w[:, 0:D], f32)
        base["l1_wk_a"] = np.ascontiguousarray(w[:, D:2 * D], f32)
        base["l1_wv_a"] = np.ascontiguousarray(w[:, 2 * D:3 * D], f32)
        base["l1_wo"] = np.ascontiguousarray(inp["L1_attn_w_o"], f32)
        base["l1_ln_g"] = np.ascontiguousarray(
            np.asarray(inp["L1_ffn_ln_g"], f32).reshape(8, 128).T)
        base["l1_ln_b"] = np.ascontiguousarray(
            np.asarray(inp["L1_ffn_ln_b"], f32).reshape(8, 128).T)
        base["l1_wqkv_l"] = np.ascontiguousarray(inp["L1_ffn_w_qkv_l"], f32)
        base["l1_wout_l"] = np.ascontiguousarray(inp["L1_ffn_w_out_l"], f32)
        base["l1_null_q"] = np.ascontiguousarray(
            np.asarray(inp["L1_ffn_null_q"], f32).reshape(8, 128).T)
        base["l1_rt_q"] = np.ascontiguousarray(
            np.asarray(inp["L1_ffn_rt_q"], f32).reshape(8, 128).T)
        base["l1_rms_g"] = np.ascontiguousarray(
            np.asarray(inp["L1_ffn_rms_g"], f32).reshape(8, 128).T)
        base["l1_wq_h"] = np.ascontiguousarray(inp["L1_ffn_w_q_h"], f32)
        base["l1_wkv_h"] = np.ascontiguousarray(
            np.asarray(inp["L1_ffn_w_kv_h"], f32).reshape(D, 1024))
        nk = np.asarray(inp["L1_ffn_null_kv"], f32)[0]
        nv = np.asarray(inp["L1_ffn_null_kv"], f32)[1]
        nkd = np.zeros((128, 8), f32)
        nkd[0:64, :] = nk.T
        nkd[64:128, :] = nk.T
        base["l1_nkd"] = nkd.astype(BFDT)
        nva = np.zeros((1, 520), f32)
        for h in range(8):
            nva[0, h * 65:h * 65 + 64] = nv[h]
            nva[0, h * 65 + 64] = 1.0
        base["l1_nvaug"] = nva.astype(BFDT)
        base["l1_wout_h"] = np.ascontiguousarray(inp["L1_ffn_w_out_h"], f32)
        kk = np.arange(128)[:, None]
        qq = np.arange(128)[None, :]
        m_prev = ((kk >= 64) & (qq < 64)).astype(BFDT)
        m_next = ((kk < 64) & (qq >= 64)).astype(BFDT)
        base["lmask"] = np.concatenate([m_prev, m_next], axis=1)
    if nl >= 3:
        w = np.asarray(inp["L2_attn_w_qkv"], f32)
        base["l2_wq_a"] = np.ascontiguousarray(w[:, 0:D])
        base["l2_wk_a"] = np.ascontiguousarray(w[:, D:2 * D])
        base["l2_wv_a"] = np.ascontiguousarray(w[:, 2 * D:3 * D])
        base["l2_wproj"] = np.ascontiguousarray(inp["L2_attn_w_proj"], f32)
        bq = np.asarray(inp["L2_attn_b_qkv"], f32).reshape(3, 16, 64)
        arr = np.zeros((128, 24), f32)
        for a_ in range(3):
            for g2 in range(8):
                arr[0:64, a_ * 8 + g2] = bq[a_, 2 * g2]
                arr[64:128, a_ * 8 + g2] = bq[a_, 2 * g2 + 1]
        base["l2_bqkv"] = arr
        base["l2_bproj"] = np.ascontiguousarray(
            np.asarray(inp["L2_attn_b_proj"], f32).reshape(8, 128).T)
        base["l2_gwT"] = np.ascontiguousarray(
            np.asarray(inp["L2_ffn_gate_w"], f32).T.reshape(8, 128, 8)
            .transpose(1, 0, 2).reshape(128, 64))
        base["iott"] = np.ascontiguousarray(
            np.broadcast_to(np.tile(8.0 - np.arange(8, dtype=f32), 16),
                            (128, 128)))
        for ex in range(8):
            base["l2_wgT" + str(ex)] = np.ascontiguousarray(
                np.asarray(inp["L2_ffn_w_gate"][ex], f32).T).astype(BFDT)
            base["l2_wuT" + str(ex)] = np.ascontiguousarray(
                np.asarray(inp["L2_ffn_w_up"][ex], f32).T).astype(BFDT)
            base["l2_wdT" + str(ex)] = np.ascontiguousarray(
                np.asarray(inp["L2_ffn_w_down"][ex], f32).T).astype(BFDT)
    if nl >= 4:
        base["l3_wq"] = np.ascontiguousarray(inp["L3_attn_Wq"], f32)
        base["l3_wk"] = np.ascontiguousarray(inp["L3_attn_Wk"], f32)
        base["l3_wo"] = np.ascontiguousarray(inp["L3_attn_Wo"], f32)
        base["l3_bo"] = np.ascontiguousarray(
            np.asarray(inp["L3_attn_bo"], f32).reshape(8, 128).T)
        base["l3_rwT"] = np.ascontiguousarray(
            np.asarray(inp["L3_ffn_router_w"], f32).T.reshape(8, 128, 8)
            .transpose(1, 0, 2).reshape(128, 64))
        base["l3_rbt"] = np.ascontiguousarray(
            np.tile(np.asarray(inp["L3_ffn_router_b"], f32), 16)[None, :])
        for ex in range(8):
            base["l3_W" + str(ex)] = np.ascontiguousarray(
                np.asarray(inp["L3_ffn_W"][ex], f32)).astype(BFDT)
            base["l3_V" + str(ex)] = np.ascontiguousarray(
                np.asarray(inp["L3_ffn_V"][ex], f32)).astype(BFDT)
            base["l3_Wout" + str(ex)] = np.ascontiguousarray(
                np.asarray(inp["L3_ffn_W_out"][ex], f32)).astype(BFDT)
    base["l0_wq"] = np.ascontiguousarray(inp["L0_attn_wq"], f32)
    base["l0_wk"] = np.ascontiguousarray(inp["L0_attn_wk"], f32)
    base["l0_wv"] = np.ascontiguousarray(inp["L0_attn_wv"], f32)
    base["l0_wo"] = np.ascontiguousarray(inp["L0_attn_wo"], f32)
    base["l0_wgT"] = np.ascontiguousarray(inp["L0_ffn_w_gate"].T, f32)
    base["l0_wuT"] = np.ascontiguousarray(inp["L0_ffn_w_up"].T, f32)
    base["l0_wdT"] = np.ascontiguousarray(inp["L0_ffn_w_down"].T, f32)
    maps = []
    for c in range(8):
        m = dict(base)
        m["xT"] = np.ascontiguousarray(inp["L0_attn_x"][c].T, f32)
        if nl >= 4:
            m["prior_mask"] = np.ascontiguousarray(
                (np.arange(8) < c).astype(f32)[:, None])
        maps.append(m)
    return maps


def kernel(**inputs):
    nl = 4
    nc = build(nl=nl)
    maps = prep_inputs(inputs, nl=nl)
    res = run_bass_kernel_spmd(nc, maps, core_ids=list(range(8)))
    out = np.stack([np.asarray(r["xout"]).T for r in res.results], axis=0)
    return np.ascontiguousarray(out, np.float32)


# version 05
# speedup vs baseline: 1.5626x; 1.0752x over previous
import numpy as np
import ml_dtypes
import concourse.bass as bass
import concourse.mybir as mybir
from concourse.bass_utils import run_bass_kernel_spmd
from contextlib import ExitStack

F32 = mybir.dt.float32
BF16 = mybir.dt.bfloat16
I32 = mybir.dt.int32
OP = mybir.AluOpType
AF = mybir.ActivationFunctionType
AX = mybir.AxisListType
BFDT = ml_dtypes.bfloat16

D = 1024
T = 2048
NCH = 8
LOGK = float(np.log(np.float32(np.float32(1152.0))))
SIMPROOF = True
LAYER_MARKS = []
STRICT = False

ENGS = ("pe", "act", "dve", "sp", "gp")
DMAS = ("S_X", "S_WA", "S_WB", "S_B", "S_O", "S_Q", "S_G", "S_C")
SERIAL_DMAS = {"S_X", "S_B", "S_O", "S_Q", "S_C"}


class Prog:
    def __init__(self, nc):
        self.nc = nc
        self.strict = STRICT
        self.th = {e: [] for e in ENGS}
        self.cnt = {e: 0 for e in ENGS}
        self.cnt.update({q: 0 for q in DMAS})
        self.waited = {e: {} for e in ENGS}
        self.sems = {}

    def alloc_sems(self, ctx):
        for name in ENGS + DMAS:
            self.sems[name] = ctx.enter_context(self.nc.semaphore(name))

    def _deps(self, eng, deps):
        out = []
        for d in deps:
            if d is None:
                continue
            se, sv = d
            if se == eng:
                continue
            if sv <= self.waited[eng].get(se, 0):
                continue
            self.waited[eng][se] = sv
            out.append((se, sv))
        return out

    def emit(self, eng, fn, deps=(), sem=None, inc=1):
        semname = sem or eng
        if self.strict and getattr(self, "last_tok", None) is not None:
            deps = list(deps) + [self.last_tok]
        deps2 = self._deps(eng, deps)
        self.cnt[semname] += inc
        target = self.cnt[semname]

        def thunk(e, deps2=deps2, fn=fn, semname=semname, inc=inc):
            for (se, sv) in deps2:
                e.wait_ge(self.sems[se], sv)
            last = fn(e)
            if inc and last is not None:
                last.then_inc(self.sems[semname], inc)
        self.th[eng].append(thunk)
        if inc:
            self.last_tok = (semname, target)
        return (semname, target)

    def dma(self, eng, fn, deps=(), sem="S_B", n=1):
        if self.strict and getattr(self, "last_tok", None) is not None:
            deps = list(deps) + [self.last_tok]
        self_wait = None
        if sem in SERIAL_DMAS and self.cnt[sem] > 0:
            self_wait = (sem, self.cnt[sem])
        deps2 = self._deps(eng, list(deps))
        if self_wait is not None and self_wait[1] > self.waited[eng].get(sem, 0):
            self.waited[eng][sem] = self_wait[1]
            deps2.append(self_wait)
        self.cnt[sem] += 16 * n
        target = self.cnt[sem]

        def thunk(e, deps2=deps2, fn=fn, sem=sem, n=n):
            for (se, sv) in deps2:
                e.wait_ge(self.sems[se], sv)
            ins = fn(e)
            assert len(ins) == n, f"dma task expected {n} got {len(ins)}"
            for i in ins:
                i.then_inc(self.sems[sem], 16)
        self.th[eng].append(thunk)
        self.last_tok = (sem, target)
        return (sem, target)


_DTSZ = {F32: 4, BF16: 2, I32: 4}


class VProxy:
    TRACK = 3

    def __init__(self, e):
        object.__setattr__(self, "_e", e)
        object.__setattr__(self, "_hist", [])

    def __getattr__(self, k):
        return getattr(self._e, k)

    def _rng(self, ap):
        try:
            esz = _DTSZ.get(ap.dtype, 4)
            lo = int(ap.offset) * esz
            fb = ap.free_nbytes
            if callable(fb):
                fb = fb()
            return (ap.tensor.name, lo, lo + int(fb))
        except Exception:
            return None

    def _chk(self, ins):
        for ap in ins:
            if ap is None or isinstance(ap, (int, float)):
                continue
            r = self._rng(ap)
            if r is None:
                continue
            for (n, lo, hi) in self._hist:
                if n == r[0] and r[1] < hi and lo < r[2]:
                    self._e.drain()
                    object.__setattr__(self, "_hist", [])
                    return

    def _rec(self, out):
        r = self._rng(out)
        h = ([r] if r else []) + self._hist[:self.TRACK - 1]
        object.__setattr__(self, "_hist", h)

    def tensor_tensor(self, out, in0, in1, op):
        self._chk([in0, in1])
        r = self._e.tensor_tensor(out=out, in0=in0, in1=in1, op=op)
        self._rec(out)
        return r

    def tensor_scalar(self, out, in0, scalar1, scalar2=None, op0=None, op1=None):
        self._chk([in0, scalar1, scalar2])
        kw = dict(out=out, in0=in0, scalar1=scalar1, scalar2=scalar2, op0=op0)
        if op1 is not None:
            kw["op1"] = op1
        r = self._e.tensor_scalar(**kw)
        self._rec(out)
        return r

    def scalar_tensor_tensor(self, out, in0, scalar, in1, op0, op1):
        self._chk([in0, scalar, in1])
        r = self._e.scalar_tensor_tensor(out=out, in0=in0, scalar=scalar, in1=in1,
                                         op0=op0, op1=op1)
        self._rec(out)
        return r

    def tensor_copy(self, out, in_):
        self._chk([in_])
        r = self._e.tensor_copy(out=out, in_=in_)
        self._rec(out)
        return r

    def tensor_reduce(self, out, in_, axis, op, apply_absolute_value=None):
        self._chk([in_])
        kw = dict(out=out, in_=in_, axis=axis, op=op)
        if apply_absolute_value is not None:
            kw["apply_absolute_value"] = apply_absolute_value
        r = self._e.tensor_reduce(**kw)
        self._rec(out)
        return r

    def reciprocal(self, out, in_):
        self._chk([in_])
        self._e.reciprocal(out=out, in_=in_)
        r = self._e.drain()
        object.__setattr__(self, "_hist", [])
        return r

    def tensor_tensor_scan(self, out, data0, data1, initial, op0, op1):
        self._chk([data0, data1])
        r = self._e.tensor_tensor_scan(out=out, data0=data0, data1=data1,
                                       initial=initial, op0=op0, op1=op1)
        self._rec(out)
        self._e.drain()
        object.__setattr__(self, "_hist", [])
        return r

    def memset(self, ap, v):
        r = self._e.memset(ap, v)
        self._rec(ap)
        return r

    def select(self, out, mask, on_true, on_false):
        self._chk([mask, on_true, on_false])
        r = self._e.select(out=out, mask=mask, on_true=on_true, on_false=on_false)
        self._rec(out)
        return r

    def activation(self, out, in_, func, bias=None, scale=None, accum_out=None):
        self._chk([in_, bias])
        kw = dict(out=out, in_=in_, func=func)
        if bias is not None:
            kw["bias"] = bias
        if scale is not None:
            kw["scale"] = scale
        if accum_out is not None:
            kw["accum_out"] = accum_out
        r = self._e.activation(**kw)
        self._rec(out)
        if accum_out is not None:
            self._rec(accum_out)
        return r


def recip_d(e, out, in_):
    e.reciprocal(out=out, in_=in_)
    return e.drain()


def build(nl=4):
    nc = bass.Bass("TRN2", target_bir_lowering=False)
    dt = {}

    def din(name, shape, dtype=F32):
        dt[name] = nc.dram_tensor(name, list(shape), dtype, kind="ExternalInput").ap()

    din("xT", (D, T))
    din("ident", (128, 128), BF16)
    din("tri", (128, 128), BF16)
    din("l0_wq", (D, D)); din("l0_wk", (D, D)); din("l0_wv", (D, D)); din("l0_wo", (D, D))
    din("l0_wgT", (D, 4096)); din("l0_wuT", (D, 4096)); din("l0_wdT", (4096, D))
    if nl >= 2:
        din("l1_wq_a", (D, D)); din("l1_wk_a", (D, D)); din("l1_wv_a", (D, D))
        din("l1_wo", (D, D))
        din("l1_ln_g", (128, 8)); din("l1_ln_b", (128, 8))
        din("l1_wqkv_l", (D, 1536)); din("l1_wout_l", (512, D))
        din("l1_null_q", (128, 8)); din("l1_rt_q", (128, 8))
        din("l1_rms_g", (128, 8))
        din("l1_wq_h", (D, 512)); din("l1_wkv_h", (D, 1024))
        din("l1_nkd", (128, 8), BF16); din("l1_nvaug", (1, 520), BF16)
        din("l1_wout_h", (512, D))
        din("lmask", (128, 256), BF16)

    if nl >= 3:
        din("l2_wq_a", (D, D)); din("l2_wk_a", (D, D)); din("l2_wv_a", (D, D))
        din("l2_wproj", (D, D)); din("l2_bqkv", (128, 24)); din("l2_bproj", (128, 8))
        din("l2_gwT", (128, 64)); din("iott", (128, 128))
        for ex in range(8):
            din("l2_wgT" + str(ex), (D, 512), BF16)
            din("l2_wuT" + str(ex), (D, 512), BF16)
            din("l2_wdT" + str(ex), (512, D), BF16)
    if nl >= 4:
        din("l3_wq", (D, D)); din("l3_wk", (D, D)); din("l3_wo", (D, D))
        din("l3_bo", (128, 8)); din("l3_rwT", (128, 64)); din("l3_rbt", (1, 128))
        din("prior_mask", (8, 1))
        for ex in range(8):
            din("l3_W" + str(ex), (D, 2048), BF16)
            din("l3_V" + str(ex), (D, 2048), BF16)
            din("l3_Wout" + str(ex), (2048, D), BF16)
        coll_in = nc.dram_tensor("coll_in", [1, 8], F32, kind="Internal").ap()
        coll_out = nc.dram_tensor("coll_out", [8, 8], F32, kind="Internal",
                                  addr_space="Shared").ap()

    xout = nc.dram_tensor("xout", [D, T], F32, kind="ExternalOutput").ap()

    dscr = nc.dram_tensor("dscr", [34, T], F32, kind="Internal").ap()
    gscr = nc.dram_tensor("gscr", [128, NCH * T], BF16, kind="Internal").ap()
    scr_wgq = nc.dram_tensor("scr_wgq", [D, 4096], BF16, kind="Internal").ap()
    scr_wuq = nc.dram_tensor("scr_wuq", [D, 4096], BF16, kind="Internal").ap()
    scr_wdq = nc.dram_tensor("scr_wdq", [4096, D], BF16, kind="Internal").ap()

    p = Prog(nc)
    with ExitStack() as ctx:
        E = ctx.enter_context
        p.alloc_sems(ctx)
        XF = E(nc.sbuf_tensor("XF", [128, NCH * T], F32))
        FA = E(nc.sbuf_tensor("FA", [128, 9984], F32))
        BA = E(nc.sbuf_tensor("BA", [128, 53248], BF16))
        SID = E(nc.sbuf_tensor("SID", [128, 128], BF16))
        STRI = E(nc.sbuf_tensor("STRI", [128, 128], BF16))
        ONES = E(nc.sbuf_tensor("ONES", [128, 1], F32))
        ONESB = E(nc.sbuf_tensor("ONESB", [128, 1], BF16))
        ROWS = E(nc.sbuf_tensor("ROWS", [128, 64], F32))
        PS = [E(nc.psum_tensor(f"PS{i}", [128, 512], F32)) for i in range(8)]

        BC_R = FA[:, 0:2048]
        BC_M = FA[:, 2048:4096]
        FS = FA[:, 4096:6144]
        FLEX = FA[:, 6144:9984]

        XN = BA[:, 0:16384]
        SLOTA = BA[:, 16384:32768]
        SLOT2 = BA[:, 32768:49152]
        WB = [BA[:, 49152:51200], BA[:, 51200:53248]]
        SAF = SLOTA.bitcast(F32)
        SAI = SLOTA.bitcast(I32)
        S2F = SLOT2.bitcast(F32)
        S2I = SLOT2.bitcast(I32)

        def xf(c):
            return XF[:, c * T:(c + 1) * T]

        def xn(c):
            return XN[:, c * T:(c + 1) * T]

        def f_load0(e):
            outs = [e.dma_start(out=SID[:], in_=dt["ident"]),
                    e.dma_start(out=STRI[:], in_=dt["tri"]),
                    e.dma_start(out=XF[:].rearrange("p (c t) -> p c t", c=8),
                                in_=dt["xT"].rearrange("(c p) t -> p c t", p=128))]
            return outs
        tok_x = p.dma("sp", f_load0, n=3, sem="S_X")
        tok_ones = p.emit("dve", lambda e: (e.memset(ONES[:], 1.0),
                          e.memset(ONESB[:], 1.0))[-1])
        SIDF = FA[:, 4096:4224]
        SIDF2 = FA[:, 4096:4224]

        bounce_row = [0]

        def bounce(srcs_dsts, deps, tag=""):
            t = None
            for src, dst0, npart in srcs_dsts:
                r = bounce_row[0] % 24
                bounce_row[0] += 1
                ncols = src.shape[-1] * (src.shape[0] if src.shape[0] <= 4 else 1)
                if src.shape[0] == 4:
                    t_o = p.dma("sp", lambda e, src=src, r=r: [e.dma_start(
                        out=dscr[r:r + 1, 0:src.shape[0] * src.shape[-1]].rearrange(
                            "o (p f) -> (o p) f", p=4), in_=src)],
                        deps=deps + ([t] if t else []), sem="S_B")
                else:
                    t_o = p.dma("sp", lambda e, src=src, r=r: [e.dma_start(
                        out=dscr[r:r + 1, 0:src.shape[-1]], in_=src)],
                        deps=deps + ([t] if t else []), sem="S_B")
                t = p.dma("sp", lambda e, dst0=dst0, r=r, ncols=ncols, npart=npart:
                          [e.dma_start(out=dst0, in_=dscr[r:r + 1, 0:ncols]
                                       .partition_broadcast(npart))],
                          deps=[t_o], sem="S_B")
            return t

        def emit_rsqrt_newton(e, v, r, t1):
            recip_d(e, t1, r)
            e.tensor_copy(out=r, in_=t1)
            for _ in range(2):
                e.tensor_tensor(out=t1, in0=r, in1=r, op=OP.mult)
                e.tensor_tensor(out=t1, in0=t1, in1=v, op=OP.mult)
                e.tensor_scalar(out=t1, in0=t1, scalar1=-0.5, scalar2=1.5,
                                op0=OP.mult, op1=OP.add)
                e.tensor_tensor(out=r, in0=r, in1=t1, op=OP.mult)

        def emit_round(e, z, out_bf, si, f1, m):
            e.tensor_copy(out=si, in_=z)
            e.tensor_copy(out=f1, in_=si)
            if SIMPROOF:
                e.tensor_tensor(out=z, in0=z, in1=f1, op=OP.subtract)
                e.tensor_scalar(out=m, in0=z, scalar1=0.5, scalar2=None, op0=OP.is_gt)
                e.tensor_scalar(out=z, in0=z, scalar1=-0.5, scalar2=None, op0=OP.is_lt)
                e.tensor_tensor(out=m, in0=m, in1=z, op=OP.subtract)
                return e.tensor_tensor(out=out_bf, in0=f1, in1=m, op=OP.add)
            return e.tensor_copy(out=out_bf, in_=f1)

        def emit_stats(src_fn, deps, psA=None, psB=None):
            t_mm = None
            for c in range(NCH):
                t_sq = p.emit("act", lambda e, c=c: e.activation(
                    out=FS[:], in_=src_fn(c), func=AF.Square),
                    deps=list(deps) + ([t_mm] if t_mm else []))

                def f_mm(e, c=c):
                    r = None
                    for qb in range(4):
                        r = e.matmul(PS[4 + qb][0:1, :], ONESB[:],
                                     src_fn(c)[:, qb * 512:(qb + 1) * 512],
                                     start=(c == 0), stop=(c == NCH - 1))
                    for qb in range(4):
                        r = e.matmul(PS[4 + qb][32:33, :], ONES[:],
                                     FS[:, qb * 512:(qb + 1) * 512],
                                     start=(c == 0), stop=(c == NCH - 1))
                    return r
                t_mm = p.emit("pe", f_mm, deps=[t_sq, tok_ones] + list(deps))
            return t_mm

        def emit_ln(deps_x):
            t_mm = None
            for c in range(NCH):
                t_sq = p.emit("act", lambda e, c=c: e.activation(
                    out=FS[:], in_=xf(c), func=AF.Square),
                    deps=list(deps_x) + ([t_mm] if t_mm else []))

                def f_mm(e, c=c):
                    r = None
                    for qb in range(4):
                        r = e.matmul(PS[4 + qb][0:1, :], ONES[:],
                                     xf(c)[:, qb * 512:(qb + 1) * 512],
                                     start=(c == 0), stop=(c == NCH - 1))
                    for qb in range(4):
                        r = e.matmul(PS[4 + qb][32:33, :], ONES[:],
                                     FS[:, qb * 512:(qb + 1) * 512],
                                     start=(c == 0), stop=(c == NCH - 1))
                    return r
                t_mm = p.emit("pe", f_mm, deps=[t_sq, tok_ones] + list(deps_x))

            R_ALL = FA[0:1, 6144:8192]
            MR_ALL = FA[0:1, 4096:6144]
            TM = FA[0:1, 0:512]
            TV = FA[0:1, 512:1024]
            TT1 = FA[0:1, 1024:1536]

            t_nm = None
            for qb in range(4):
                def f_q(e, qb=qb):
                    rr = R_ALL[:, qb * 512:(qb + 1) * 512]
                    mr = MR_ALL[:, qb * 512:(qb + 1) * 512]
                    e.tensor_scalar(out=TM, in0=PS[4 + qb][0:1, :],
                                    scalar1=1.0 / 1024.0, scalar2=None, op0=OP.mult)
                    e.tensor_scalar(out=TV, in0=PS[4 + qb][32:33, :],
                                    scalar1=1.0 / 1024.0, scalar2=None, op0=OP.mult)
                    e.tensor_tensor(out=TT1, in0=TM, in1=TM, op=OP.mult)
                    e.tensor_tensor(out=TV, in0=TV, in1=TT1, op=OP.subtract)
                    return e.tensor_scalar(out=TV, in0=TV, scalar1=1e-5,
                                           scalar2=None, op0=OP.add)
                t_a = p.emit("dve", f_q, deps=[t_mm] + ([t_nm] if t_nm else []))
                t_b = p.emit("act", lambda e, qb=qb: e.activation(
                    out=R_ALL[:, qb * 512:(qb + 1) * 512], in_=TV, func=AF.Sqrt),
                    deps=[t_a])

                def f_n(e, qb=qb):
                    rr = R_ALL[:, qb * 512:(qb + 1) * 512]
                    mr = MR_ALL[:, qb * 512:(qb + 1) * 512]
                    emit_rsqrt_newton(e, TV, rr, TT1)
                    return e.tensor_tensor(out=mr, in0=TM, in1=rr, op=OP.mult)
                t_nm = p.emit("dve", f_n, deps=[t_b])
            return bounce([(R_ALL, BC_R, 128), (MR_ALL, BC_M, 128)], [t_nm])

        def emit_xn(t_bc, deps_x, per_chunk=None):
            last_users = []
            out_toks = []
            for c in range(NCH):
                def f_xc(e, c=c):
                    e.tensor_tensor(out=FS[:], in0=xf(c), in1=BC_R[:], op=OP.mult)
                    return e.tensor_tensor(out=FS[:], in0=FS[:], in1=BC_M[:],
                                           op=OP.subtract)
                t_fs = p.emit("dve", f_xc, deps=[t_bc] + list(deps_x) + last_users)
                t_cp = p.emit("dve", lambda e, c=c: e.tensor_copy(out=xn(c), in_=FS[:]),
                              deps=[t_fs])
                last_users = [t_cp]
                if per_chunk is not None:
                    last_users = last_users + per_chunk(c, t_fs)
                out_toks.append(t_cp)
            return out_toks, last_users

        def l0_ffn_and_ln():
            t_bc = emit_ln([tok_x])
            AMAX = FLEX[:, 0:2048]

            AMIN = SAF[:, 0:2048]

            def per_chunk(c, t_fs):
                def f_am(e, c=c):
                    if c == 0:
                        e.tensor_copy(out=AMAX, in_=FS[:])
                        return e.tensor_copy(out=AMIN, in_=FS[:])
                    e.tensor_tensor(out=AMAX, in0=AMAX, in1=FS[:], op=OP.max)
                    return e.tensor_tensor(out=AMIN, in0=AMIN, in1=FS[:], op=OP.min)
                return [p.emit("dve", f_am, deps=[t_fs])]
            xn_toks, last_fs = emit_xn(t_bc, [tok_x], per_chunk)

            winv_col = {"wgq": ROWS[:, 4:5], "wuq": ROWS[:, 5:6],
                        "wdq": ROWS[:, 6:7]}
            wq_last = None
            for nm, src, dst, rows, cols in (
                    ("wgq", dt["l0_wgT"], scr_wgq, D, 4096),
                    ("wuq", dt["l0_wuT"], scr_wuq, D, 4096),
                    ("wdq", dt["l0_wdT"], scr_wdq, 4096, D)):
                nchunk = rows // 128
                nh = 2 if cols == 4096 else 1
                hc = cols // nh
                stg = [SAF[:, 2560:2560 + hc], SAF[:, 4608:4608 + hc]]
                qb_ = [SLOT2[:, 0:hc], SLOT2[:, hc:2 * hc]]
                m1s = S2F[:, 2048:2048 + hc]
                accc = ROWS[:, 0:1]
                tmpc = ROWS[:, 1:2]
                t_use = None
                i = 0
                for c in range(nchunk):
                    for h in range(nh):
                        b = i % 2
                        t_l = p.dma("gp", lambda e, c=c, h=h, b=b, hc=hc, src=src,
                                    stg=stg:
                                    [e.dma_start(out=stg[b],
                                     in_=src[c * 128:(c + 1) * 128, h * hc:(h + 1) * hc])],
                                    deps=[tok_x] + ([t_use] if t_use else []), sem="S_Q")

                        def f_abs(e, b=b, first=(i == 0), stg=stg):
                            e.tensor_reduce(out=tmpc, in_=stg[b], axis=AX.X, op=OP.add,
                                            apply_absolute_value=True)
                            if first:
                                return e.tensor_copy(out=accc, in_=tmpc)
                            return e.tensor_tensor(out=accc, in0=accc, in1=tmpc,
                                                   op=OP.add)
                        t_use = p.emit("dve", f_abs, deps=[t_l])
                        i += 1

                t_summ = p.emit("pe", lambda e: e.matmul(
                    PS[0][0:1, 0:1], ONES[:], accc, start=True, stop=True),
                    deps=[t_use, tok_ones])

                def f_tree(e, rows=rows, cols=cols):
                    recip_d(e, tmpc[0:1, :], PS[0][0:1, 0:1])
                    e.tensor_scalar(out=accc[0:1, :], in0=tmpc[0:1, :],
                                    scalar1=float(rows * cols), scalar2=None, op0=OP.mult)
                    return recip_d(e, tmpc[0:1, :], accc[0:1, :])
                t_tree = p.emit("dve", f_tree, deps=[t_summ])
                t_bcs = bounce([(accc[0:1, :], ROWS[:, 2:3], 128),
                                (tmpc[0:1, :], winv_col[nm], 128)], [t_tree],
                               tag=nm)
                wscol = ROWS[:, 2:3]
                t_q = None
                i = 0
                for c in range(nchunk):
                    for h in range(nh):
                        b = i % 2
                        t_l = p.dma("gp", lambda e, c=c, h=h, b=b, hc=hc, src=src,
                                    stg=stg:
                                    [e.dma_start(out=stg[b],
                                     in_=src[c * 128:(c + 1) * 128, h * hc:(h + 1) * hc])],
                                    deps=[t_bcs] + ([t_q] if t_q else []), sem="S_Q")

                        def f_q(e, b=b, hc=hc, stg=stg, m1s=m1s, qb_=qb_):
                            e.tensor_scalar(out=m1s[:, 0:hc], in0=stg[b], scalar1=wscol,
                                            scalar2=0.5, op0=OP.mult, op1=OP.is_ge)
                            e.tensor_scalar(out=stg[b], in0=stg[b], scalar1=wscol,
                                            scalar2=-0.5, op0=OP.mult, op1=OP.is_le)
                            return e.tensor_tensor(out=qb_[b], in0=m1s[:, 0:hc],
                                                   in1=stg[b], op=OP.subtract)
                        t_qv = p.emit("dve", f_q, deps=[t_l])
                        t_q = p.dma("gp", lambda e, c=c, h=h, b=b, hc=hc, dst=dst,
                                    qb_=qb_:
                                    [e.dma_start(out=dst[c * 128:(c + 1) * 128,
                                                        h * hc:(h + 1) * hc],
                                     in_=qb_[b])], deps=[t_qv], sem="S_Q")
                        i += 1
                wq_last = t_q

            SCB = FA[:, 7936:9984]

            t_sidf = p.emit("dve", lambda e: e.tensor_copy(out=SIDF, in_=SID[:]),
                            deps=last_fs + xn_toks)

            def f_amc(e):
                e.tensor_scalar(out=AMIN, in0=AMIN, scalar1=-1.0, scalar2=None,
                                op0=OP.mult)
                return e.tensor_tensor(out=AMAX, in0=AMAX, in1=AMIN, op=OP.max)
            t_amc = p.emit("dve", f_amc, deps=last_fs + xn_toks)

            AMT = ROWS[:, 8:24]
            ASC = ROWS[:, 24:40]
            AIV = ROWS[:, 40:56]
            amb_need = SLOTA.bitcast(BF16)
            t_amt = []
            for tt in range(16):
                t_tr = p.emit("pe", lambda e, tt=tt: e.transpose(
                    PS[0][:, 0:128], AMAX[:, tt * 128:(tt + 1) * 128], SIDF),
                    deps=([t_sidf, t_amc] + last_fs + xn_toks) if tt == 0 else
                    [t_sidf, t_amt[-1]])
                t_rd = p.emit("dve", lambda e, tt=tt: e.tensor_reduce(
                    out=AMT[:, tt:tt + 1], in_=PS[0][:, 0:128], axis=AX.X,
                    op=OP.max, apply_absolute_value=True), deps=[t_tr])
                t_amt.append(t_rd)

            def f_sc(e):
                recip_d(e, ASC, AMT)
                e.tensor_scalar(out=ASC, in0=ASC, scalar1=127.0, scalar2=None,
                                op0=OP.mult)
                return e.tensor_scalar(out=AIV, in0=AMT, scalar1=1.0 / 127.0,
                                       scalar2=None, op0=OP.mult)
            t_sc = p.emit("dve", f_sc, deps=[t_amt[-1]])
            R16A = FA[0:16, 7296:7424]
            R16B = FA[0:16, 7424:7552]
            t_tA = p.emit("pe", lambda e: e.transpose(
                PS[1][0:16, 0:128], ASC, SIDF), deps=[t_sc, t_sidf])
            t_cA = p.emit("dve", lambda e: e.tensor_copy(
                out=R16A, in_=PS[1][0:16, 0:128]), deps=[t_tA])
            t_tB = p.emit("pe", lambda e: e.transpose(
                PS[1][0:16, 128:256], AIV, SIDF), deps=[t_sc, t_cA])
            t_cB = p.emit("dve", lambda e: e.tensor_copy(
                out=R16B, in_=PS[1][0:16, 128:256]), deps=[t_tB])
            t_sco = p.dma("sp", lambda e: [
                e.dma_start(out=dscr[25:26, :].rearrange("o (t p) -> (o t) p", t=16),
                            in_=R16A),
                e.dma_start(out=dscr[26:27, :].rearrange("o (t p) -> (o t) p", t=16),
                            in_=R16B)], deps=[t_cA, t_cB], sem="S_B", n=2)
            t_sci = p.dma("sp", lambda e: [e.dma_start(
                out=SCB, in_=dscr[25:26, :].partition_broadcast(128))],
                deps=[t_sco], sem="S_B")
            t_invo = t_sco

            def xq(c):
                return SLOT2[:, c * T:(c + 1) * T]
            xq_toks = []
            t_prev = None
            for c in range(NCH):
                def f_z(e, c=c):
                    e.tensor_tensor(out=FS[:], in0=xf(c), in1=BC_R[:], op=OP.mult)
                    e.tensor_tensor(out=FS[:], in0=FS[:], in1=BC_M[:], op=OP.subtract)
                    return e.tensor_tensor(out=FS[:], in0=FS[:], in1=SCB, op=OP.mult)
                deps = [t_sci, tok_x, wq_last] + ([t_prev] if t_prev else [])
                t_z = p.emit("dve", f_z, deps=deps)
                t_r = p.emit("dve", lambda e, c=c: emit_round(
                    e, FS[:], xq(c), SAI[:, 0:2048], SAF[:, 2048:4096],
                    SAF[:, 4096:6144]),
                    deps=[t_z])
                t_prev = t_r
                xq_toks.append(t_r)

            HF = FA[:, 0:4096]
            IASB = FLEX[:, 0:2048]
            t_ias = p.dma("sp", lambda e: [e.dma_start(
                out=IASB, in_=dscr[26:27, :].partition_broadcast(128))],
                deps=[t_invo, t_sc] + xq_toks, sem="S_B")
            wg_inv = winv_col["wgq"]
            wu_inv = winv_col["wuq"]
            wd_inv = winv_col["wdq"]
            TB = 128
            G1 = SAF[:, 0 * TB:1 * TB]
            U1 = SAF[:, 1 * TB:2 * TB]
            HAM = SAF[:, 2 * TB:3 * TB]
            HT1 = SAF[:, 3 * TB:4 * TB]
            HSCB = SAF[:, 4 * TB:5 * TB]
            HIAB = SAF[:, 5 * TB:6 * TB]
            ZZ = SAF[:, 6 * TB:7 * TB]
            F1 = SAF[:, 7 * TB:8 * TB]
            SIV = SAI[:, 8 * TB:9 * TB]
            HAMN = SAF[:, 9 * TB:10 * TB]
            MRND = SAF[:, 10 * TB:11 * TB]

            def hqap(oc):
                return SLOTA[:, 8192 + oc * TB:8192 + (oc + 1) * TB]

            t_sidf2 = p.emit("dve", lambda e: e.tensor_copy(out=SIDF2, in_=SID[:]),
                             deps=xq_toks + [t_ias])
            g_last = None
            t_wprev = [None, None]
            t_ev_prev = None
            t_gv_prev = [None, None]
            for tb in range(T // TB):
                t0 = tb * TB
                t_h = []
                for oc in range(32):
                    b = oc % 2

                    def f_wl(e, oc=oc, b=b):
                        return [e.dma_start(
                                out=WB[b][:, 0:1024].rearrange("p (c o) -> p c o", c=8),
                                in_=scr_wgq[:, oc * 128:(oc + 1) * 128]
                                .rearrange("(c p) o -> p c o", p=128)),
                                e.dma_start(
                                out=WB[b][:, 1024:2048].rearrange("p (c o) -> p c o", c=8),
                                in_=scr_wuq[:, oc * 128:(oc + 1) * 128]
                                .rearrange("(c p) o -> p c o", p=128))]
                    dpw = [wq_last]
                    if t_wprev[b] is not None:
                        dpw.append(t_wprev[b])
                    t_wl = p.dma("gp", f_wl, deps=dpw,
                                 sem=("S_WA" if b == 0 else "S_WB"), n=2)

                    def f_gmm(e, oc=oc, b=b, t0=t0):
                        r = None
                        for c in range(NCH):
                            r = e.matmul(PS[0][:, 0:TB], WB[b][:, c * 128:(c + 1) * 128],
                                         xq(c)[:, t0:t0 + TB], start=(c == 0),
                                         stop=(c == NCH - 1))
                        for c in range(NCH):
                            r = e.matmul(PS[1][:, 0:TB],
                                         WB[b][:, 1024 + c * 128:1024 + (c + 1) * 128],
                                         xq(c)[:, t0:t0 + TB], start=(c == 0),
                                         stop=(c == NCH - 1))
                        return r
                    dmm = [t_wl] + xq_toks
                    if t_ev_prev is not None:
                        dmm.append(t_ev_prev)
                    t_mm = p.emit("pe", f_gmm, deps=dmm)
                    t_wprev[b] = t_mm

                    def f_ev1(e, t0=t0):
                        e.tensor_scalar(out=G1, in0=PS[0][:, 0:TB], scalar1=wg_inv,
                                        scalar2=None, op0=OP.mult)
                        e.tensor_tensor(out=G1, in0=G1, in1=IASB[:, t0:t0 + TB],
                                        op=OP.mult)
                        e.tensor_scalar(out=U1, in0=PS[1][:, 0:TB], scalar1=wu_inv,
                                        scalar2=None, op0=OP.mult)
                        return e.tensor_tensor(out=U1, in0=U1, in1=IASB[:, t0:t0 + TB],
                                               op=OP.mult)
                    t_ev1 = p.emit("dve", f_ev1, deps=[t_mm, t_ias])
                    t_ev_prev = t_ev1
                    t_sil = p.emit("act", lambda e: e.activation(
                        out=G1, in_=G1, func=AF.Silu), deps=[t_ev1])

                    def f_h(e, oc=oc):
                        hsl = HF[:, oc * TB:(oc + 1) * TB]
                        e.tensor_tensor(out=hsl, in0=G1, in1=U1, op=OP.mult)
                        if oc == 0:
                            e.tensor_copy(out=HAM, in_=hsl)
                            return e.tensor_copy(out=HAMN, in_=hsl)
                        e.tensor_tensor(out=HAM, in0=HAM, in1=hsl, op=OP.max)
                        return e.tensor_tensor(out=HAMN, in0=HAMN, in1=hsl, op=OP.min)
                    t_hh = p.emit("dve", f_h, deps=[t_sil])
                    t_ev_prev = t_hh
                    t_h.append(t_hh)

                NHT = TB // 128
                HMT = ROWS[:, 56:56 + NHT]
                HAS = ROWS[:, 58:58 + NHT]
                HAI = ROWS[:, 60:60 + NHT]
                def f_hamc(e):
                    e.tensor_scalar(out=HAMN, in0=HAMN, scalar1=-1.0, scalar2=None,
                                    op0=OP.mult)
                    return e.tensor_tensor(out=HAM, in0=HAM, in1=HAMN, op=OP.max)
                t_hamc = p.emit("dve", f_hamc, deps=[t_h[-1]])
                t_htr = []
                for tt in range(NHT):
                    t_tr = p.emit("pe", lambda e, tt=tt: e.transpose(
                        PS[4][:, 0:128], HAM[:, tt * 128:(tt + 1) * 128], SIDF2),
                        deps=[t_sidf2, t_hamc] if tt == 0 else [t_htr[-1]])
                    t_rd = p.emit("dve", lambda e, tt=tt: e.tensor_reduce(
                        out=HMT[:, tt:tt + 1], in_=PS[4][:, 0:128], axis=AX.X,
                        op=OP.max, apply_absolute_value=True), deps=[t_tr])
                    t_htr.append(t_rd)

                def f_hsc(e):
                    recip_d(e, HAS, HMT)
                    e.tensor_scalar(out=HAS, in0=HAS, scalar1=127.0, scalar2=None,
                                    op0=OP.mult)
                    return e.tensor_scalar(out=HAI, in0=HMT, scalar1=1.0 / 127.0,
                                           scalar2=None, op0=OP.mult)
                t_hsc = p.emit("dve", f_hsc, deps=[t_htr[-1]])
                R2A = FA[0:NHT, 8192:8192 + 128]
                R2B = FA[0:NHT, 8320:8320 + 128]
                t_t2 = p.emit("pe", lambda e: e.transpose(
                    PS[5][0:NHT, 0:128], HAS, SIDF2), deps=[t_hsc, t_sidf2])
                t_c2 = p.emit("dve", lambda e: e.tensor_copy(
                    out=R2A, in_=PS[5][0:NHT, 0:128]), deps=[t_t2])
                t_t3 = p.emit("pe", lambda e: e.transpose(
                    PS[5][0:NHT, 128:256], HAI, SIDF2), deps=[t_hsc, t_c2])
                t_c3 = p.emit("dve", lambda e: e.tensor_copy(
                    out=R2B, in_=PS[5][0:NHT, 128:256]), deps=[t_t3])
                r0 = 27 + (tb % 2)
                t_ho = p.dma("sp", lambda e, r0=r0: [
                    e.dma_start(out=dscr[r0:r0 + 1, 0:TB]
                                .rearrange("o (t p) -> (o t) p", t=NHT), in_=R2A),
                    e.dma_start(out=dscr[r0:r0 + 1, 256:256 + TB]
                                .rearrange("o (t p) -> (o t) p", t=NHT), in_=R2B)],
                    deps=[t_c2, t_c3], sem="S_B", n=2)
                t_hb = p.dma("sp", lambda e, r0=r0: [
                    e.dma_start(out=HSCB, in_=dscr[r0:r0 + 1, 0:TB]
                                .partition_broadcast(128)),
                    e.dma_start(out=HIAB, in_=dscr[r0:r0 + 1, 256:256 + TB]
                                .partition_broadcast(128))],
                    deps=[t_ho], sem="S_B", n=2)
                hq_toks = []
                for oc in range(32):
                    def f_hq(e, oc=oc):
                        hsl = HF[:, oc * TB:(oc + 1) * TB]
                        e.tensor_tensor(out=ZZ, in0=hsl, in1=HSCB, op=OP.mult)
                        return emit_round(e, ZZ, hqap(oc), SIV, F1, MRND)
                    hq_toks.append(p.emit("dve", f_hq, deps=[t_hb, t_h[oc]]))

                for dchunk in range(NCH):
                    par = dchunk % 2

                    def f_wdl(e, dchunk=dchunk):
                        ap = scr_wdq[:, dchunk * 128:(dchunk + 1) * 128].rearrange(
                            "(c p) o -> p c o", p=128)
                        return [e.dma_start(
                            out=WB[0][:, 0:2048].rearrange("p (c o) -> p c o", c=16),
                            in_=ap[:, 0:16]),
                            e.dma_start(
                            out=WB[1][:, 0:2048].rearrange("p (c o) -> p c o", c=16),
                            in_=ap[:, 16:32])]
                    dpw = [wq_last] + hq_toks
                    for tt in (t_wprev[0], t_wprev[1]):
                        if tt is not None:
                            dpw.append(tt)
                    t_wl = p.dma("gp", f_wdl, deps=dpw, sem="S_WA", n=2)

                    def f_dmm(e, par=par):
                        r = None
                        for oc in range(32):
                            w = WB[oc // 16][:, (oc % 16) * 128:(oc % 16 + 1) * 128]
                            r = e.matmul(PS[2 + par][:, 0:TB], w, hqap(oc),
                                         start=(oc == 0), stop=(oc == 31))
                        return r
                    dmm = [t_wl]
                    if t_gv_prev[par] is not None:
                        dmm.append(t_gv_prev[par])
                    t_dm = p.emit("pe", f_dmm, deps=dmm)
                    t_wprev[0] = t_dm
                    t_wprev[1] = t_dm

                    def f_gv(e, par=par):
                        e.tensor_scalar(out=ZZ, in0=PS[2 + par][:, 0:TB],
                                        scalar1=wd_inv, scalar2=None, op0=OP.mult)
                        return e.tensor_tensor(out=ZZ, in0=ZZ, in1=HIAB, op=OP.mult)
                    t_gv = p.emit("dve", f_gv, deps=[t_dm, t_hb])
                    gstg = SLOTA[:, 4608 + par * TB:4608 + (par + 1) * TB]
                    t_gs = p.emit("act", lambda e, gstg=gstg: e.activation(
                        out=gstg, in_=ZZ, func=AF.Silu), deps=[t_gv])
                    t_gv_prev[par] = t_gs
                    g_last = p.dma("gp", lambda e, dchunk=dchunk, t0=t0, gstg=gstg:
                                   [e.dma_start(out=gscr[:, dchunk * T + t0:
                                                         dchunk * T + t0 + TB],
                                     in_=gstg)], deps=[t_gs], sem="S_G")
            return xn_toks, g_last

        def l0_attn(xn_toks, g_last):
            OFm = SLOTA
            DEN = FA[:, 7936:9984]

            def stb(h):
                pb = 64 * (h % 2)
                off = 15528 + (h // 2) * 65
                return SLOT2[pb:pb + 64, off:off + 65]

            def stf(h):
                pb = 64 * (h % 2)
                off = 6144 + (h // 2) * 65
                return FA[pb:pb + 64, off:off + 65]

            Q = SLOT2[:, 0:2048]
            K = SLOT2[:, 2048:4096]
            KT = SLOT2[:, 4096:6144]
            VST = SLOT2[:, 6144:8192]
            VAUG = SLOT2[:, 8192:10280]
            PT = SLOT2[:, 10280:10408]
            FT1 = SLOT2[:, 10408:10920]
            FT2 = SLOT2[:, 10920:11432]
            GST = [SLOT2[:, 11432:13480], SLOT2[:, 13480:15528]]

            t_lw = {"S_WA": None, "S_WB": None}
            t_ofm = []
            den_dump = []
            ft_guard = [None]
            fs_guard = [None]
            t_attn_ps_free = [None] * 8
            prev_vaug_users = []
            for g2 in range(8):
                def f_wq(e, g2=g2):
                    return [e.dma_start(
                            out=WB[0][:, 0:1024].rearrange("p (c f) -> p c f", c=8),
                            in_=dt["l0_wq"][:, g2 * 128:(g2 + 1) * 128]
                            .rearrange("(c p) f -> p c f", p=128))]
                dq = [tok_x, g_last] + ([t_lw["S_WA"]] if t_lw["S_WA"] else [])
                t_wq = p.dma("gp", f_wq, deps=dq, sem="S_WA")

                def f_wkv(e, g2=g2):
                    return [e.dma_start(
                            out=WB[1][:, 0:1024].rearrange("p (c f) -> p c f", c=8),
                            in_=dt["l0_wk"][:, g2 * 128:(g2 + 1) * 128]
                            .rearrange("(c p) f -> p c f", p=128)),
                            e.dma_start(
                            out=WB[1][:, 1024:2048].rearrange("p (c f) -> p c f", c=8),
                            in_=dt["l0_wv"][:, g2 * 128:(g2 + 1) * 128]
                            .rearrange("(c p) f -> p c f", p=128))]
                dk = [tok_x, g_last] + ([t_lw["S_WB"]] if t_lw["S_WB"] else [])
                t_wkv = p.dma("gp", f_wkv, deps=dk, sem="S_WB", n=2)

                def proj_feat(woff, wbuf, dst, t_w, prev_scan):
                    toks = []
                    for qb in range(4):
                        def f_mm(e, qb=qb, woff=woff, wbuf=wbuf):
                            r = None
                            for c in range(NCH):
                                r = e.matmul(PS[qb][:, :],
                                             wbuf[:, woff + c * 128:woff + (c + 1) * 128],
                                             xn(c)[:, qb * 512:(qb + 1) * 512],
                                             start=(c == 0), stop=(c == NCH - 1))
                            return r
                        dmm = [t_w] + xn_toks
                        if t_attn_ps_free[qb] is not None:
                            dmm.append(t_attn_ps_free[qb])
                        t_mm = p.emit("pe", f_mm, deps=dmm)
                        t_r = p.emit("act", lambda e, qb=qb: e.activation(
                            out=FT1, in_=PS[qb][:, :], func=AF.Relu),
                            deps=[t_mm] + ft_guard[0:1])
                        t_m = p.emit("dve", lambda e, qb=qb: e.tensor_scalar(
                            out=FS[:, 0:512], in0=PS[qb][:, :], scalar1=0.0,
                            scalar2=None, op0=OP.min),
                            deps=[t_mm, t_r] + fs_guard[0:1])
                        t_attn_ps_free[qb] = t_m
                        t_e = p.emit("act", lambda e: e.activation(
                            out=FT2, in_=FS[:, 0:512], func=AF.Exp), deps=[t_m])
                        fs_guard[0] = t_e
                        t_a = p.emit("dve", lambda e, qb=qb, dst=dst: e.tensor_tensor(
                            out=dst[:, qb * 512:(qb + 1) * 512], in0=FT1, in1=FT2,
                            op=OP.add), deps=[t_e] + prev_scan)
                        ft_guard[0] = t_a
                        toks.append(t_a)
                    return toks
                tq = proj_feat(0, WB[0], Q, t_wq, prev_vaug_users)
                tk_ = proj_feat(0, WB[1], K, t_wkv, prev_vaug_users)
                t_lw["S_WA"] = tq[-1]
                tv = []
                for qb in range(4):
                    def f_mm(e, qb=qb):
                        r = None
                        for c in range(NCH):
                            r = e.matmul(PS[4 + qb % 2][:, :],
                                         WB[1][:, 1024 + c * 128:1024 + (c + 1) * 128],
                                         xn(c)[:, qb * 512:(qb + 1) * 512],
                                         start=(c == 0), stop=(c == NCH - 1))
                        return r
                    dmm = [t_wkv] + xn_toks
                    if t_attn_ps_free[4 + qb % 2] is not None:
                        dmm.append(t_attn_ps_free[4 + qb % 2])
                    t_mm = p.emit("pe", f_mm, deps=dmm)
                    t_v = p.emit("dve", lambda e, qb=qb: e.tensor_copy(
                        out=VST[:, qb * 512:(qb + 1) * 512], in_=PS[4 + qb % 2][:, :]),
                        deps=[t_mm])
                    t_attn_ps_free[4 + qb % 2] = t_v
                    tv.append(t_v)
                t_lw["S_WB"] = tv[-1]

                t_ms = p.emit("dve", lambda e: e.memset(VAUG[:], 1.0),
                              deps=prev_vaug_users)
                ttr = []
                for ct in range(16):
                    def f_tr(e, ct=ct):
                        e.transpose(PS[6].bitcast(BF16)[:, 0:128],
                                    VST[:, ct * 128:(ct + 1) * 128], SID[:])
                        return e.transpose(PS[7].bitcast(BF16)[:, 0:128],
                                           K[:, ct * 128:(ct + 1) * 128], SID[:])
                    dtr = [tv[ct // 4], tk_[ct // 4]]
                    if ttr:
                        dtr.append(ttr[-1])
                    if t_attn_ps_free[6] is not None:
                        dtr.append(t_attn_ps_free[6])
                    t_tr = p.emit("pe", f_tr, deps=dtr)

                    def f_ev(e, ct=ct):
                        e.tensor_copy(out=VAUG[:, ct * 130:ct * 130 + 64],
                                      in_=PS[6].bitcast(BF16)[:, 0:64])
                        e.tensor_copy(out=VAUG[:, ct * 130 + 65:ct * 130 + 129],
                                      in_=PS[6].bitcast(BF16)[:, 64:128])
                        return e.tensor_copy(out=KT[:, ct * 128:(ct + 1) * 128],
                                             in_=PS[7].bitcast(BF16)[:, 0:128])
                    t_ev = p.emit("dve", f_ev, deps=[t_tr, t_ms])
                    t_attn_ps_free[6] = t_ev
                    ttr.append(t_ev)

                t_pt_prev = None
                t_oe_prev = None
                t_av_prev = None
                scan_tail = []
                for hl in range(2):
                    h = 2 * g2 + hl
                    t_prev_state = None
                    for c in range(16):
                        def f_sc(e, hl=hl, c=c):
                            return e.matmul(
                                PS[0][:, 0:128],
                                K[hl * 64:(hl + 1) * 64, c * 128:(c + 1) * 128],
                                Q[hl * 64:(hl + 1) * 64, c * 128:(c + 1) * 128],
                                start=True, stop=True)
                        dsc = [tq[c // 4], tk_[c // 4]]
                        if t_pt_prev is not None:
                            dsc.append(t_pt_prev)
                        if t_attn_ps_free[0] is not None:
                            dsc.append(t_attn_ps_free[0])
                            t_attn_ps_free[0] = None
                        t_s = p.emit("pe", f_sc, deps=dsc)
                        t_pt = p.emit("dve", lambda e: e.tensor_tensor(
                            out=PT, in0=PS[0][:, 0:128], in1=STRI[:], op=OP.mult),
                            deps=[t_s] + ([t_av_prev] if t_av_prev else []))
                        t_pt_prev = t_pt

                        def f_av(e, hl=hl, h=h, c=c):
                            if c > 0:
                                e.matmul(PS[1][0:65, 0:128], stb(h),
                                         Q[hl * 64:(hl + 1) * 64, c * 128:(c + 1) * 128],
                                         start=True, stop=False)
                                return e.matmul(
                                    PS[1][0:65, 0:128],
                                    VAUG[:, c * 130 + hl * 65:c * 130 + (hl + 1) * 65],
                                    PT, start=False, stop=True)
                            return e.matmul(
                                PS[1][0:65, 0:128],
                                VAUG[:, c * 130 + hl * 65:c * 130 + (hl + 1) * 65],
                                PT, start=True, stop=True)
                        davd = [t_pt, ttr[c]]
                        if t_prev_state is not None:
                            davd.append(t_prev_state)
                        if t_oe_prev is not None:
                            davd.append(t_oe_prev)
                        t_av = p.emit("pe", f_av, deps=davd)
                        t_av_prev = t_av

                        def f_su(e, hl=hl, c=c):
                            return e.matmul(
                                PS[2][0:64, 0:65],
                                KT[:, c * 128 + hl * 64:c * 128 + (hl + 1) * 64],
                                VAUG[:, c * 130 + hl * 65:c * 130 + (hl + 1) * 65],
                                start=True, stop=True)
                        dsu = [ttr[c], t_av]
                        if t_prev_state is not None:
                            dsu.append(t_prev_state)
                        t_su = p.emit("pe", f_su, deps=dsu)

                        def f_st(e, h=h, c=c):
                            if c == 0:
                                e.tensor_copy(out=stf(h), in_=PS[2][0:64, 0:65])
                            else:
                                e.tensor_tensor(out=stf(h), in0=stf(h),
                                                in1=PS[2][0:64, 0:65], op=OP.add)
                            return e.tensor_copy(out=stb(h), in_=stf(h))
                        t_st = p.emit("dve", f_st, deps=[t_su])
                        t_prev_state = t_st

                        def f_oe(e, g2=g2, hl=hl, h=h, c=c):
                            e.tensor_copy(
                                out=OFm[hl * 64:(hl + 1) * 64,
                                        g2 * 2048 + c * 128:g2 * 2048 + (c + 1) * 128],
                                in_=PS[1][0:64, 0:128])
                            return e.tensor_copy(
                                out=DEN[hl * 64:hl * 64 + 1, c * 128:(c + 1) * 128],
                                in_=PS[1][64:65, 0:128])
                        t_oe = p.emit("dve", f_oe, deps=[t_av])
                        t_oe_prev = t_oe
                        t_ofm.append(t_oe)
                    scan_tail.append(t_prev_state)
                t_dd = p.dma("sp", lambda e, g2=g2: [
                    e.dma_start(out=dscr[2 * g2:2 * g2 + 1, :], in_=DEN[0:1, :]),
                    e.dma_start(out=dscr[2 * g2 + 1:2 * g2 + 2, :], in_=DEN[64:65, :])],
                    deps=[t_oe_prev] + scan_tail, sem="S_B", n=2)
                den_dump.append(t_dd)
                prev_vaug_users = scan_tail + [t_oe_prev]
                t_attn_ps_free[0] = t_pt_prev
                t_attn_ps_free[1] = t_oe_prev
                t_attn_ps_free[2] = scan_tail[-1]

            t_norm = []
            DBC = FA[:, 0:2048]
            for g2 in range(8):
                t_di = p.dma("sp", lambda e, g2=g2: [
                    e.dma_start(out=DBC[0:64, :],
                                in_=dscr[2 * g2:2 * g2 + 1, :].partition_broadcast(64)),
                    e.dma_start(out=DBC[64:128, :],
                                in_=dscr[2 * g2 + 1:2 * g2 + 2, :]
                                .partition_broadcast(64))],
                    deps=[den_dump[g2]] + (t_norm[-1:] if t_norm else []),
                    sem="S_B", n=2)

                def f_nm(e, g2=g2):
                    e.tensor_scalar(out=DBC, in0=DBC, scalar1=1e-6, scalar2=None,
                                    op0=OP.add)
                    recip_d(e, DBC, DBC)
                    return e.tensor_tensor(
                        out=OFm[:, g2 * 2048:(g2 + 1) * 2048],
                        in0=OFm[:, g2 * 2048:(g2 + 1) * 2048], in1=DBC, op=OP.mult)
                t_nm = p.emit("dve", f_nm, deps=[t_di] + t_ofm[-1:])
                t_norm.append(t_nm)

            comb_toks = []
            t_prev = None
            t_gld_prev = [None, None]
            for dchunk in range(NCH):
                b = dchunk % 2

                def f_wl(e, dchunk=dchunk, b=b):
                    return [e.dma_start(
                            out=WB[b][:, 0:1024].rearrange("p (c f) -> p c f", c=8),
                            in_=dt["l0_wo"][:, dchunk * 128:(dchunk + 1) * 128]
                            .rearrange("(c p) f -> p c f", p=128))]
                dps = [tok_x]
                if t_prev is not None:
                    dps.append(t_prev)
                t_wl = p.dma("gp", f_wl, deps=dps,
                             sem=("S_WA" if b == 0 else "S_WB"))
                dgl = [g_last]
                if t_gld_prev[b] is not None:
                    dgl.append(t_gld_prev[b])
                t_gld = p.dma("gp", lambda e, dchunk=dchunk, b=b: [
                    e.dma_start(out=GST[b], in_=gscr[:, dchunk * T:(dchunk + 1) * T])],
                    deps=dgl, sem="S_G")
                qtoks = []
                for qb in range(4):
                    def f_mm(e, qb=qb, b=b):
                        r = None
                        for ec in range(NCH):
                            r = e.matmul(PS[qb][:, :], WB[b][:, ec * 128:(ec + 1) * 128],
                                         OFm[:, ec * 2048 + qb * 512:
                                             ec * 2048 + (qb + 1) * 512],
                                         start=(ec == 0), stop=(ec == NCH - 1))
                        return r
                    dmm = [t_wl] + t_norm
                    if comb_toks:
                        dmm.append(comb_toks[-4] if len(comb_toks) >= 4
                                   else comb_toks[-1])
                    t_mm = p.emit("pe", f_mm, deps=dmm)

                    def f_cmb(e, qb=qb, dchunk=dchunk, b=b):
                        e.tensor_tensor(out=FS[:, 0:512], in0=PS[qb][:, :],
                                        in1=GST[b][:, qb * 512:(qb + 1) * 512],
                                        op=OP.mult)
                        return e.tensor_tensor(
                            out=xf(dchunk)[:, qb * 512:(qb + 1) * 512],
                            in0=xf(dchunk)[:, qb * 512:(qb + 1) * 512],
                            in1=FS[:, 0:512], op=OP.add)
                    t_cb = p.emit("dve", f_cmb, deps=[t_mm, t_gld])
                    qtoks.append(t_cb)
                    comb_toks.append(t_cb)
                t_gld_prev[b] = qtoks[-1]
                t_prev = qtoks[-1]
            return comb_toks

        xn_toks, g_last = l0_ffn_and_ln()
        comb = l0_attn(xn_toks, g_last)
        LAYER_MARKS.append(dict(p.cnt))

        def emit_mha(deps0, xn_toks, g_last, wq_ap, wk_ap, wv_ap, wo_ap,
                     qscale, causal=False, v_from_k=False,
                     bq_col=None, bk_col=None, bv_col=None, bo_col=None):
            OFm = SLOTA
            DEN = FA[:, 7936:9984]
            Q = SLOT2[:, 0:2048]
            K = SLOT2[:, 2048:4096]
            VST = SLOT2[:, 6144:8192]
            VAUG = SLOT2[:, 8192:10280]
            PTd = [SLOT2[:, 10280:10792], SLOT2[:, 10792:11304]]
            last_exp = [None, None]
            GST = [SLOT2[:, 11432:13480], SLOT2[:, 13480:15528]]
            t_lw = {"S_WA": None, "S_WB": None}
            den_dump = []
            prev_scan = list(deps0)
            ps_free = [None] * 8
            for g2 in range(8):
                def f_wq(e, g2=g2):
                    return [e.dma_start(
                        out=WB[0][:, 0:1024].rearrange("p (c f) -> p c f", c=8),
                        in_=wq_ap[:, g2 * 128:(g2 + 1) * 128]
                        .rearrange("(c p) f -> p c f", p=128))]
                dq = list(deps0) + ([t_lw["S_WA"]] if t_lw["S_WA"] else [])
                t_wq = p.dma("gp", f_wq, deps=dq, sem="S_WA")

                def f_wkv(e, g2=g2):
                    outs = [e.dma_start(
                        out=WB[1][:, 0:1024].rearrange("p (c f) -> p c f", c=8),
                        in_=wk_ap[:, g2 * 128:(g2 + 1) * 128]
                        .rearrange("(c p) f -> p c f", p=128))]
                    if not v_from_k:
                        outs.append(e.dma_start(
                            out=WB[1][:, 1024:2048].rearrange("p (c f) -> p c f", c=8),
                            in_=wv_ap[:, g2 * 128:(g2 + 1) * 128]
                            .rearrange("(c p) f -> p c f", p=128)))
                    return outs
                dk = list(deps0) + ([t_lw["S_WB"]] if t_lw["S_WB"] else [])
                t_wkv = p.dma("gp", f_wkv, deps=dk, sem="S_WB",
                              n=(1 if v_from_k else 2))

                def proj(woff, wbuf, dst, t_w, scale, bcol):
                    toks = []
                    for qb in range(4):
                        def f_mm(e, qb=qb, woff=woff, wbuf=wbuf):
                            r = None
                            for c in range(NCH):
                                r = e.matmul(PS[4 + qb % 2][:, :],
                                             wbuf[:, woff + c * 128:woff + (c + 1) * 128],
                                             xn(c)[:, qb * 512:(qb + 1) * 512],
                                             start=(c == 0), stop=(c == NCH - 1))
                            return r
                        dmm = [t_w] + xn_toks + prev_scan
                        if ps_free[4 + qb % 2] is not None:
                            dmm.append(ps_free[4 + qb % 2])
                        t_mm = p.emit("pe", f_mm, deps=dmm)

                        def f_ev(e, qb=qb, dst=dst, scale=scale, bcol=bcol, g2=g2):
                            ps = PS[4 + qb % 2][:, :]
                            d = dst[:, qb * 512:(qb + 1) * 512]
                            if bcol is not None:
                                return e.tensor_scalar(out=d, in0=ps,
                                                       scalar1=bcol(g2),
                                                       scalar2=scale, op0=OP.add,
                                                       op1=OP.mult)
                            if scale != 1.0:
                                return e.tensor_scalar(out=d, in0=ps, scalar1=scale,
                                                       scalar2=None, op0=OP.mult)
                            return e.tensor_copy(out=d, in_=ps)
                        t_ev = p.emit("dve", f_ev, deps=[t_mm])
                        ps_free[4 + qb % 2] = t_ev
                        toks.append(t_ev)
                    return toks
                tq = proj(0, WB[0], Q, t_wq, qscale, bq_col)
                tk_ = proj(0, WB[1], K, t_wkv, 1.0, bk_col)
                t_lw["S_WA"] = tq[-1]
                if v_from_k:
                    tv = tk_
                    vsrc = K
                else:
                    tv = proj(1024, WB[1], VST, t_wkv, 1.0, bv_col)
                    vsrc = VST
                t_lw["S_WB"] = tv[-1]

                t_ms = p.emit("dve", lambda e: e.memset(VAUG[:], 1.0), deps=prev_scan)
                ttr = []
                for ct in range(16):
                    def f_tr(e, ct=ct, vsrc=vsrc):
                        return e.transpose(PS[6].bitcast(BF16)[:, 0:128],
                                           vsrc[:, ct * 128:(ct + 1) * 128], SID[:])
                    dtr = [tv[ct // 4]]
                    if ttr:
                        dtr.append(ttr[-1])
                    if ps_free[6] is not None:
                        dtr.append(ps_free[6])
                    t_tr = p.emit("pe", f_tr, deps=dtr)

                    def f_ev(e, ct=ct):
                        e.tensor_copy(out=VAUG[:, ct * 130:ct * 130 + 64],
                                      in_=PS[6].bitcast(BF16)[:, 0:64])
                        return e.tensor_copy(out=VAUG[:, ct * 130 + 65:ct * 130 + 129],
                                             in_=PS[6].bitcast(BF16)[:, 64:128])
                    t_ev = p.emit("dve", f_ev, deps=[t_tr, t_ms])
                    ps_free[6] = t_ev
                    ttr.append(t_ev)

                t_oe_prev = [None, None]
                scan_tail = []
                for hl in range(2):
                    for qseg in range(4):
                        kt_hi = (qseg * 4 + 4) if causal else 16
                        msk_toks = []
                        t_av = None

                        def emit_av(kt, kt_hi=kt_hi, hl=hl):
                            def f_av(e, hl=hl, kt=kt, first=(kt == 0),
                                     last=(kt == kt_hi - 1)):
                                return e.matmul(
                                    PS[2 + hl][0:65, :],
                                    VAUG[:, kt * 130 + hl * 65:
                                         kt * 130 + (hl + 1) * 65],
                                    PTd[kt % 2], start=first, stop=last)
                            da = [msk_toks[kt], ttr[kt]]
                            if kt == 0 and t_oe_prev[hl] is not None:
                                da.append(t_oe_prev[hl])
                            return p.emit("pe", f_av, deps=da)

                        for kt in range(kt_hi):
                            def f_s(e, hl=hl, kt=kt, qseg=qseg):
                                return e.matmul(
                                    PS[kt % 2][:, :],
                                    K[hl * 64:(hl + 1) * 64,
                                      kt * 128:(kt + 1) * 128],
                                    Q[hl * 64:(hl + 1) * 64,
                                      qseg * 512:(qseg + 1) * 512],
                                    start=True, stop=True)
                            ds = [tq[qseg], tk_[kt // 4]]
                            if last_exp[kt % 2] is not None:
                                ds.append(last_exp[kt % 2])
                            t_s = p.emit("pe", f_s, deps=ds)
                            t_pt = p.emit("act", lambda e, kt=kt: e.activation(
                                out=PTd[kt % 2], in_=PS[kt % 2][:, :],
                                func=AF.Exp), deps=[t_s])
                            last_exp[kt % 2] = t_pt
                            if causal and qseg * 512 <= kt * 128 < (qseg + 1) * 512:
                                off = kt * 128 - qseg * 512

                                def f_mask(e, off=off, kt=kt):
                                    r = e.tensor_tensor(
                                        out=PTd[kt % 2][:, off:off + 128],
                                        in0=PTd[kt % 2][:, off:off + 128],
                                        in1=STRI[:], op=OP.mult)
                                    if off > 0:
                                        r = e.memset(PTd[kt % 2][:, 0:off], 0.0)
                                    return r
                                t_msk = p.emit("dve", f_mask, deps=[t_pt])
                            else:
                                t_msk = t_pt
                            msk_toks.append(t_msk)
                            if kt >= 1:
                                t_av = emit_av(kt - 1)
                        t_av = emit_av(kt_hi - 1)

                        def f_oe(e, g2=g2, hl=hl, qseg=qseg):
                            e.tensor_copy(
                                out=OFm[hl * 64:(hl + 1) * 64,
                                        g2 * 2048 + qseg * 512:
                                        g2 * 2048 + (qseg + 1) * 512],
                                in_=PS[2 + hl][0:64, :])
                            return e.tensor_copy(
                                out=DEN[hl * 64:hl * 64 + 1,
                                        qseg * 512:(qseg + 1) * 512],
                                in_=PS[2 + hl][64:65, :])
                        t_oe = p.emit("dve", f_oe, deps=[t_av])
                        t_oe_prev[hl] = t_oe
                        scan_tail.append(t_oe)
                t_dd = p.dma("sp", lambda e, g2=g2: [
                    e.dma_start(out=dscr[2 * g2:2 * g2 + 1, :], in_=DEN[0:1, :]),
                    e.dma_start(out=dscr[2 * g2 + 1:2 * g2 + 2, :],
                                in_=DEN[64:65, :])],
                    deps=scan_tail, sem="S_B", n=2)
                den_dump.append(t_dd)
                prev_scan = scan_tail + [t_dd]

            t_norm = []
            DBC = FA[:, 0:2048]
            for g2 in range(8):
                t_di = p.dma("sp", lambda e, g2=g2: [
                    e.dma_start(out=DBC[0:64, :],
                                in_=dscr[2 * g2:2 * g2 + 1, :].partition_broadcast(64)),
                    e.dma_start(out=DBC[64:128, :],
                                in_=dscr[2 * g2 + 1:2 * g2 + 2, :]
                                .partition_broadcast(64))],
                    deps=[den_dump[g2]] + (t_norm[-1:] if t_norm else []),
                    sem="S_B", n=2)

                def f_nm(e, g2=g2):
                    e.tensor_scalar(out=DBC, in0=DBC, scalar1=1e-6, scalar2=None,
                                    op0=OP.add)
                    recip_d(e, DBC, DBC)
                    return e.tensor_tensor(
                        out=OFm[:, g2 * 2048:(g2 + 1) * 2048],
                        in0=OFm[:, g2 * 2048:(g2 + 1) * 2048], in1=DBC, op=OP.mult)
                t_nm = p.emit("dve", f_nm, deps=[t_di] + prev_scan)
                t_norm.append(t_nm)

            comb_toks = []
            t_prev = None
            t_gld_prev = [None, None]
            for dchunk in range(NCH):
                b = dchunk % 2

                def f_wl(e, dchunk=dchunk, b=b):
                    return [e.dma_start(
                        out=WB[b][:, 0:1024].rearrange("p (c f) -> p c f", c=8),
                        in_=wo_ap[:, dchunk * 128:(dchunk + 1) * 128]
                        .rearrange("(c p) f -> p c f", p=128))]
                dps = list(deps0) + [t for t in
                                     (t_lw["S_WA"], t_lw["S_WB"]) if t]
                if t_prev is not None:
                    dps.append(t_prev)
                t_wl = p.dma("gp", f_wl, deps=dps,
                             sem=("S_WA" if b == 0 else "S_WB"))
                dgl = [g_last]
                if t_gld_prev[b] is not None:
                    dgl.append(t_gld_prev[b])
                t_gld = p.dma("gp", lambda e, dchunk=dchunk, b=b: [
                    e.dma_start(out=GST[b], in_=gscr[:, dchunk * T:(dchunk + 1) * T])],
                    deps=dgl, sem="S_G")
                qtoks = []
                for qb in range(4):
                    def f_mm(e, qb=qb, b=b):
                        r = None
                        for ec in range(NCH):
                            r = e.matmul(PS[qb][:, :], WB[b][:, ec * 128:(ec + 1) * 128],
                                         OFm[:, ec * 2048 + qb * 512:
                                             ec * 2048 + (qb + 1) * 512],
                                         start=(ec == 0), stop=(ec == NCH - 1))
                        return r
                    dmm = [t_wl] + t_norm
                    if comb_toks:
                        dmm.append(comb_toks[-1])
                    t_mm = p.emit("pe", f_mm, deps=dmm)

                    def f_cmb(e, qb=qb, dchunk=dchunk, b=b):
                        ps = PS[qb][:, :]
                        if bo_col is not None:
                            e.tensor_scalar(out=FS[:, 0:512], in0=ps,
                                            scalar1=bo_col(dchunk), scalar2=None,
                                            op0=OP.add)
                            e.tensor_tensor(out=FS[:, 0:512], in0=FS[:, 0:512],
                                            in1=GST[b][:, qb * 512:(qb + 1) * 512],
                                            op=OP.mult)
                        else:
                            e.tensor_tensor(out=FS[:, 0:512], in0=ps,
                                            in1=GST[b][:, qb * 512:(qb + 1) * 512],
                                            op=OP.mult)
                        return e.tensor_tensor(
                            out=xf(dchunk)[:, qb * 512:(qb + 1) * 512],
                            in0=xf(dchunk)[:, qb * 512:(qb + 1) * 512],
                            in1=FS[:, 0:512], op=OP.add)
                    t_cb = p.emit("dve", f_cmb, deps=[t_mm, t_gld])
                    qtoks.append(t_cb)
                    comb_toks.append(t_cb)
                t_gld_prev[b] = qtoks[-1]
                t_prev = qtoks[-1]
            return comb_toks

        def emit_l1(prev_comb):
            SMSK = WB[0][:, 1536:1792]
            t_smsk = p.dma("sp", lambda e: [e.dma_start(out=SMSK, in_=dt["lmask"])],
                           deps=prev_comb, sem="S_B")
            t_bc = emit_ln(prev_comb)
            RT = ROWS[:, 8:16]
            t_rt = p.dma("sp", lambda e: [e.dma_start(out=RT, in_=dt["l1_rt_q"])],
                deps=prev_comb, sem="S_B")
            s_ps = [(PS[4], 0), (PS[4], 32), (PS[4], 64), (PS[5], 0)]

            def per_chunk(c, t_fs):
                def f_s(e, c=c):
                    r = None
                    for qb in range(4):
                        tn, b0 = s_ps[qb]
                        r = e.matmul(tn[b0:b0 + 1, :], RT[:, c:c + 1],
                                     FS[:, qb * 512:(qb + 1) * 512],
                                     start=(c == 0), stop=(c == NCH - 1))
                    return r
                return [p.emit("pe", f_s, deps=[t_fs, t_rt])]
            xn_toks, last_fs = emit_xn(t_bc, prev_comb, per_chunk)
            SROW = FA[0:1, 6144:8192]

            def f_sev(e):
                r = None
                for qb in range(4):
                    tn, b0 = s_ps[qb]
                    r = e.tensor_copy(out=SROW[:, qb * 512:(qb + 1) * 512],
                                      in_=tn[b0:b0 + 1, :])
                return r
            t_sev = p.emit("dve", f_sev, deps=last_fs + xn_toks)

            A = FA[0:1, 4096:4097]
            NA = FA[0:1, 4097:4098]
            ES = FA[0:1, 4098:4099]
            L0R = FA[0:1, 4099:4100]
            E0R = FA[0:1, 4100:4101]
            U = FA[0:1, 0:2048]
            a1 = float(np.float32(np.float32(LOGK) -
                                  np.float32(np.log(np.float32(2048.0)))))

            def f_init(e):
                e.memset(A[:], a1)
                return e.memset(NA[:], -a1)
            t_it = p.emit("dve", f_init, deps=[t_sev])
            for it in range(49):
                t_u = p.emit("dve", lambda e: e.tensor_scalar(
                    out=U, in0=SROW, scalar1=NA, scalar2=None, op0=OP.min),
                    deps=[t_it])
                t_e = p.emit("act", lambda e: e.activation(
                    out=U, in_=U, func=AF.Exp), deps=[t_u])
                t_er = p.emit("dve", lambda e: e.tensor_reduce(
                    out=ES, in_=U, axis=AX.X, op=OP.add), deps=[t_e])
                t_ua = p.emit("act", lambda e: (
                    e.activation(out=L0R, in_=ES, func=AF.Ln),
                    e.activation(out=E0R, in_=L0R, func=AF.Exp, scale=-1.0))[-1],
                    deps=[t_er])

                def f_upd(e):
                    e.tensor_tensor(out=E0R, in0=E0R, in1=ES, op=OP.mult)
                    e.scalar_tensor_tensor(out=L0R, in0=E0R, scalar=-1.0, in1=L0R,
                                           op0=OP.add, op1=OP.add)
                    e.tensor_scalar(out=A, in0=L0R, scalar1=-1.0, scalar2=LOGK,
                                    op0=OP.mult, op1=OP.add)
                    return e.tensor_scalar(out=NA, in0=L0R, scalar1=LOGK,
                                           scalar2=None, op0=OP.subtract)
                t_it = p.emit("dve", f_upd, deps=[t_ua])

            SAT = FA[0:1, 2048:4096]
            PRE = SROW

            def f_sel(e):
                return e.tensor_scalar(out=SAT, in0=SROW, scalar1=A, scalar2=0.0,
                                       op0=OP.add, op1=OP.is_ge)
            t_sat = p.emit("dve", f_sel, deps=[t_it])

            def f_pref(e):
                e.memset(U, 0.0)
                e.tensor_tensor_scan(out=PRE, data0=SAT, data1=U,
                                     initial=0.0, op0=OP.add, op1=OP.add)
                e.tensor_scalar(out=PRE, in0=PRE, scalar1=1024.0, scalar2=None,
                                op0=OP.is_le)
                return e.tensor_tensor(out=SAT, in0=SAT, in1=PRE, op=OP.mult)
            t_selr = p.emit("dve", f_pref, deps=[t_sat])
            t_selo = p.dma("sp", lambda e: [e.dma_start(out=dscr[32:33, :],
                                                        in_=SAT)],
                           deps=[t_selr], sem="S_B")

            t_mm2 = emit_stats(xn, xn_toks + [t_selo], PS[6], PS[7])
            TM = FA[0:1, 4112:4624]
            TV = FA[0:1, 4624:5136]
            TT1 = FA[0:1, 5136:5648]
            R2_ALL = FA[0:1, 0:2048]
            M2R_ALL = FA[0:1, 2048:4096]
            CR_ALL = FA[0:1, 6144:8192]
            t_lr = None
            for qb in range(4):
                def f_q(e, qb=qb):
                    cr = CR_ALL[:, qb * 512:(qb + 1) * 512]
                    e.tensor_copy(out=cr, in_=PS[4 + qb][32:33, :])
                    e.tensor_scalar(out=TM, in0=PS[4 + qb][0:1, :],
                                    scalar1=1.0 / 1024.0, scalar2=None, op0=OP.mult)
                    e.tensor_scalar(out=TV, in0=PS[4 + qb][32:33, :],
                                    scalar1=1.0 / 1024.0, scalar2=None, op0=OP.mult)
                    e.tensor_tensor(out=TT1, in0=TM, in1=TM, op=OP.mult)
                    e.tensor_tensor(out=TV, in0=TV, in1=TT1, op=OP.subtract)
                    return e.tensor_scalar(out=TV, in0=TV, scalar1=1e-5,
                                           scalar2=None, op0=OP.add)
                t_a = p.emit("dve", f_q, deps=[t_mm2] + ([t_lr] if t_lr else []))
                t_b = p.emit("act", lambda e, qb=qb: e.activation(
                    out=R2_ALL[:, qb * 512:(qb + 1) * 512], in_=TV, func=AF.Sqrt),
                    deps=[t_a])

                def f_n(e, qb=qb):
                    rr = R2_ALL[:, qb * 512:(qb + 1) * 512]
                    mr = M2R_ALL[:, qb * 512:(qb + 1) * 512]
                    emit_rsqrt_newton(e, TV, rr, TT1)
                    return e.tensor_tensor(out=mr, in0=TM, in1=rr, op=OP.mult)
                t_c = p.emit("dve", f_n, deps=[t_b])
                t_d = p.emit("act", lambda e, qb=qb: e.activation(
                    out=TT1, in_=CR_ALL[:, qb * 512:(qb + 1) * 512], func=AF.Sqrt),
                    deps=[t_c])

                def f_cr(e, qb=qb):
                    cr = CR_ALL[:, qb * 512:(qb + 1) * 512]
                    recip_d(e, TM, TT1)
                    for _ in range(2):
                        e.tensor_tensor(out=TV, in0=TM, in1=TM, op=OP.mult)
                        e.tensor_tensor(out=TV, in0=TV, in1=cr, op=OP.mult)
                        e.tensor_scalar(out=TV, in0=TV, scalar1=-0.5, scalar2=1.5,
                                        op0=OP.mult, op1=OP.add)
                        e.tensor_tensor(out=TM, in0=TM, in1=TV, op=OP.mult)
                    return e.tensor_scalar(out=cr, in0=TM, scalar1=32.0,
                                           scalar2=None, op0=OP.mult)
                t_lr = p.emit("dve", f_cr, deps=[t_d])
            BC2R = FA[:, 0:2048]
            BC2M = FA[:, 2048:4096]
            t_bc2 = bounce([(R2_ALL, BC2R, 128), (M2R_ALL, BC2M, 128)], [t_lr])
            CBC = FA[:, 6144:8192]
            t_cbc = bounce([(CR_ALL, CBC, 128)], [t_bc2, t_lr])

            GCOL = ROWS[:, 16:24]
            BCOL = ROWS[:, 24:32]
            t_gb = p.dma("sp", lambda e: [
                e.dma_start(out=GCOL, in_=dt["l1_ln_g"]),
                e.dma_start(out=BCOL, in_=dt["l1_ln_b"])], deps=prev_comb,
                sem="S_B", n=2)

            def x2(c):
                return SLOT2[:, c * T:(c + 1) * T]
            x2_toks = []
            t_px = None
            for c in range(NCH):
                def f_x2(e, c=c):
                    e.tensor_tensor(out=FS[:], in0=xn(c), in1=BC2R[:], op=OP.mult)
                    e.tensor_tensor(out=FS[:], in0=FS[:], in1=BC2M[:], op=OP.subtract)
                    e.tensor_scalar(out=FS[:], in0=FS[:], scalar1=GCOL[:, c:c + 1],
                                    scalar2=None, op0=OP.mult)
                    e.tensor_scalar(out=FS[:], in0=FS[:], scalar1=BCOL[:, c:c + 1],
                                    scalar2=None, op0=OP.add)
                    return e.tensor_copy(out=x2(c), in_=FS[:])
                dps = [t_bc2, t_gb] + xn_toks + ([t_px] if t_px else [])
                t_px = p.emit("dve", f_x2, deps=dps)
                x2_toks.append(t_px)

            QL = SLOTA[:, 0:2048]
            KL = SLOTA[:, 2048:4096]
            VSTL = SLOTA[:, 4096:6144]
            VAUGL = SLOTA[:, 6144:8192]
            OL = SLOTA[:, 8192:16384]
            DEN = FA[:, 2048:4096]
            t_lwl = {"S_WA": None, "S_WB": None}
            den_dumpl = []
            prev_scanl = [t_cbc, t_smsk]
            ps_freel = [None] * 8
            for lg in range(4):
                def f_wq(e, lg=lg):
                    return [e.dma_start(
                        out=WB[0][:, 0:1024].rearrange("p (c f) -> p c f", c=8),
                        in_=dt["l1_wqkv_l"][:, lg * 128:(lg + 1) * 128]
                        .rearrange("(c p) f -> p c f", p=128))]
                dq = [t_gb] + ([t_lwl["S_WA"]] if t_lwl["S_WA"] else [])
                t_wq = p.dma("gp", f_wq, deps=dq, sem="S_WA")

                def f_wkv(e, lg=lg):
                    return [e.dma_start(
                        out=WB[1][:, 0:1024].rearrange("p (c f) -> p c f", c=8),
                        in_=dt["l1_wqkv_l"][:, 512 + lg * 128:512 + (lg + 1) * 128]
                        .rearrange("(c p) f -> p c f", p=128)),
                        e.dma_start(
                        out=WB[1][:, 1024:2048].rearrange("p (c f) -> p c f", c=8),
                        in_=dt["l1_wqkv_l"][:, 1024 + lg * 128:1024 + (lg + 1) * 128]
                        .rearrange("(c p) f -> p c f", p=128))]
                dk = [t_gb] + ([t_lwl["S_WB"]] if t_lwl["S_WB"] else [])
                t_wkv = p.dma("gp", f_wkv, deps=dk, sem="S_WB", n=2)

                def projl(woff, wbuf, dst, t_w, scale):
                    toks = []
                    for qb in range(4):
                        def f_mm(e, qb=qb, woff=woff, wbuf=wbuf):
                            r = None
                            for c in range(NCH):
                                r = e.matmul(PS[4 + qb % 2][:, :],
                                             wbuf[:, woff + c * 128:woff + (c + 1) * 128],
                                             x2(c)[:, qb * 512:(qb + 1) * 512],
                                             start=(c == 0), stop=(c == NCH - 1))
                            return r
                        dmm = [t_w] + x2_toks + prev_scanl
                        if ps_freel[4 + qb % 2] is not None:
                            dmm.append(ps_freel[4 + qb % 2])
                        t_mm = p.emit("pe", f_mm, deps=dmm)

                        def f_ev(e, qb=qb, dst=dst, scale=scale):
                            ps = PS[4 + qb % 2][:, :]
                            d = dst[:, qb * 512:(qb + 1) * 512]
                            if scale != 1.0:
                                return e.tensor_scalar(out=d, in0=ps, scalar1=scale,
                                                       scalar2=None, op0=OP.mult)
                            return e.tensor_copy(out=d, in_=ps)
                        t_ev = p.emit("dve", f_ev, deps=[t_mm])
                        ps_freel[4 + qb % 2] = t_ev
                        toks.append(t_ev)
                    return toks
                tql = projl(0, WB[0], QL, t_wq, 0.125)
                tkl = projl(0, WB[1], KL, t_wkv, 1.0)
                tvl = projl(1024, WB[1], VSTL, t_wkv, 1.0)
                t_lwl["S_WA"] = tql[-1]
                t_lwl["S_WB"] = tvl[-1]
                ttr = []
                for ct in range(16):
                    def f_tr(e, ct=ct):
                        return e.transpose(PS[6].bitcast(BF16)[:, 0:128],
                                           VSTL[:, ct * 128:(ct + 1) * 128], SID[:])
                    dtr = [tvl[ct // 4]] + prev_scanl
                    if ttr:
                        dtr.append(ttr[-1])
                    if ps_freel[6] is not None:
                        dtr.append(ps_freel[6])
                    t_tr = p.emit("pe", f_tr, deps=dtr)

                    def f_ev(e, ct=ct):
                        return e.tensor_copy(
                            out=VAUGL[:, ct * 128:(ct + 1) * 128],
                            in_=PS[6].bitcast(BF16)[:, 0:128])
                    t_ev = p.emit("dve", f_ev, deps=[t_tr])
                    ps_freel[6] = t_ev
                    ttr.append(t_ev)
                t_oe_prev = [None, None]
                t_pt_prev = None
                scan_tail = []
                PTT = WB[0][:, 1024:1152]
                for hl in range(2):
                    for m in range(16):
                        tiles = []
                        if m > 0:
                            tiles.append((m - 1, 0))
                        tiles.append((m, None))
                        if m < 15:
                            tiles.append((m + 1, 128))
                        t_av = None
                        for ti, (kt, moff) in enumerate(tiles):
                            def f_s(e, hl=hl, kt=kt, m=m):
                                return e.matmul(
                                    PS[0][:, 0:128],
                                    KL[hl * 64:(hl + 1) * 64, kt * 128:(kt + 1) * 128],
                                    QL[hl * 64:(hl + 1) * 64, m * 128:(m + 1) * 128],
                                    start=True, stop=True)
                            ds = [tql[m // 4], tkl[kt // 4]]
                            if t_pt_prev is not None:
                                ds.append(t_pt_prev)
                            t_s = p.emit("pe", f_s, deps=ds)
                            t_pt = p.emit("act", lambda e: e.activation(
                                out=PTT, in_=PS[0][:, 0:128], func=AF.Exp),
                                deps=[t_s])
                            if moff is not None:
                                t_mk = p.emit("dve", lambda e, moff=moff:
                                              e.tensor_tensor(
                                                  out=PTT, in0=PTT,
                                                  in1=SMSK[:, moff:moff + 128],
                                                  op=OP.mult), deps=[t_pt])
                            else:
                                t_mk = t_pt
                            t_pt_prev = t_mk

                            def f_av(e, hl=hl, kt=kt, first=(ti == 0),
                                     last=(ti == len(tiles) - 1)):
                                e.matmul(
                                    PS[2 + hl][0:64, 0:128],
                                    VAUGL[:, kt * 128 + hl * 64:
                                          kt * 128 + (hl + 1) * 64],
                                    PTT, start=first, stop=last)
                                return e.matmul(
                                    PS[5][hl * 64:hl * 64 + 1, 0:128],
                                    ONESB[:], PTT, start=first, stop=last)
                            da = [t_mk, ttr[kt], tok_ones]
                            if t_av is not None:
                                da.append(t_av)
                            if ti == 0 and t_oe_prev[hl] is not None:
                                da.append(t_oe_prev[hl])
                            t_av = p.emit("pe", f_av, deps=da)

                        def f_oe(e, lg=lg, hl=hl, m=m):
                            e.tensor_copy(
                                out=OL[hl * 64:(hl + 1) * 64,
                                       lg * 2048 + m * 128:lg * 2048 + (m + 1) * 128],
                                in_=PS[2 + hl][0:64, 0:128])
                            return e.tensor_copy(
                                out=DEN[hl * 64:hl * 64 + 1, m * 128:(m + 1) * 128],
                                in_=PS[5][hl * 64:hl * 64 + 1, 0:128])
                        t_oe = p.emit("dve", f_oe, deps=[t_av])
                        t_oe_prev[hl] = t_oe
                        scan_tail.append(t_oe)
                t_dd = p.dma("sp", lambda e, lg=lg: [
                    e.dma_start(out=dscr[16 + 2 * lg:16 + 2 * lg + 1, :],
                                in_=DEN[0:1, :]),
                    e.dma_start(out=dscr[16 + 2 * lg + 1:16 + 2 * lg + 2, :],
                                in_=DEN[64:65, :])],
                    deps=scan_tail, sem="S_B", n=2)
                den_dumpl.append(t_dd)
                prev_scanl = scan_tail + [t_dd]
            t_norml = []
            DBCL = FA[:, 4096:6144]
            for lg in range(4):
                t_di = p.dma("sp", lambda e, lg=lg: [
                    e.dma_start(out=DBCL[0:64, :],
                                in_=dscr[16 + 2 * lg:16 + 2 * lg + 1, :]
                                .partition_broadcast(64)),
                    e.dma_start(out=DBCL[64:128, :],
                                in_=dscr[16 + 2 * lg + 1:16 + 2 * lg + 2, :]
                                .partition_broadcast(64))],
                    deps=[den_dumpl[lg]] + (t_norml[-1:] if t_norml else []),
                    sem="S_B", n=2)

                def f_nm(e, lg=lg):
                    e.tensor_scalar(out=DBCL, in0=DBCL, scalar1=1e-6, scalar2=None,
                                    op0=OP.add)
                    recip_d(e, DBCL, DBCL)
                    return e.tensor_tensor(
                        out=OL[:, lg * 2048:(lg + 1) * 2048],
                        in0=OL[:, lg * 2048:(lg + 1) * 2048], in1=DBCL, op=OP.mult)
                t_nm = p.emit("dve", f_nm, deps=[t_di] + prev_scanl)
                t_norml.append(t_nm)
            LOUT = SLOT2
            lo_toks = []
            t_prev = None
            for dchunk in range(NCH):
                b = dchunk % 2

                def f_wl(e, dchunk=dchunk, b=b):
                    return [e.dma_start(
                        out=WB[b][:, 0:512].rearrange("p (c f) -> p c f", c=4),
                        in_=dt["l1_wout_l"][:, dchunk * 128:(dchunk + 1) * 128]
                        .rearrange("(c p) f -> p c f", p=128))]
                dps = [t_gb] + x2_toks + \
                    [t for t in (t_lwl["S_WA"], t_lwl["S_WB"]) if t]
                if t_prev is not None:
                    dps.append(t_prev)
                t_wl = p.dma("gp", f_wl, deps=dps,
                             sem=("S_WA" if b == 0 else "S_WB"))
                for qb in range(4):
                    def f_mm(e, qb=qb, b=b):
                        r = None
                        for ec in range(4):
                            r = e.matmul(PS[qb][:, :], WB[b][:, ec * 128:(ec + 1) * 128],
                                         OL[:, ec * 2048 + qb * 512:
                                            ec * 2048 + (qb + 1) * 512],
                                         start=(ec == 0), stop=(ec == 3))
                        return r
                    dmm = [t_wl] + t_norml
                    if lo_toks:
                        dmm.append(lo_toks[-1])
                    t_mm = p.emit("pe", f_mm, deps=dmm)
                    t_lo = p.emit("dve", lambda e, qb=qb, dchunk=dchunk:
                                  e.tensor_copy(
                                      out=LOUT[:, dchunk * 2048 + qb * 512:
                                               dchunk * 2048 + (qb + 1) * 512],
                                      in_=PS[qb][:, :]), deps=[t_mm])
                    lo_toks.append(t_lo)
                t_prev = lo_toks[-1]

            QH = SLOTA[:, 8192:10240]
            KH = SLOTA[:, 10240:12288]
            VAUGH = SLOTA[:, 12288:14368]
            NVAUG = SLOTA[0:1, 14368:14888]
            NKD = SLOTA[:, 14888:14896]
            GCOLR = ROWS[:, 32:40]
            NQCOL = ROWS[:, 40:48]
            t_gr = p.dma("sp", lambda e: [
                e.dma_start(out=GCOLR, in_=dt["l1_rms_g"]),
                e.dma_start(out=NQCOL, in_=dt["l1_null_q"]),
                e.dma_start(out=NKD, in_=dt["l1_nkd"]),
                e.dma_start(out=NVAUG, in_=dt["l1_nvaug"])],
                deps=lo_toks, sem="S_B", n=4)
            PNUL = WB[1][0:1, 0:2048]
            t_hw = {"S_WA": None, "S_WB": None}
            den_dumph = []
            prev_scanh = lo_toks[-1:]
            ps_freeh = [None] * 8
            for hg in range(4):
                def f_wq(e, hg=hg):
                    return [e.dma_start(
                        out=WB[0][:, 0:1024].rearrange("p (c f) -> p c f", c=8),
                        in_=dt["l1_wq_h"][:, hg * 128:(hg + 1) * 128]
                        .rearrange("(c p) f -> p c f", p=128))]
                dq = [t_gr] + ([t_hw["S_WA"]] if t_hw["S_WA"] else [])
                t_wq = p.dma("gp", f_wq, deps=dq, sem="S_WA")

                def f_wqs(e):
                    r = None
                    for cc in range(NCH):
                        r = e.tensor_scalar(out=WB[0][:, cc * 128:(cc + 1) * 128],
                                            in0=WB[0][:, cc * 128:(cc + 1) * 128],
                                            scalar1=GCOLR[:, cc:cc + 1],
                                            scalar2=None, op0=OP.mult)
                    return r
                t_wqs = p.emit("dve", f_wqs, deps=[t_wq])

                def f_wkv(e, hg=hg):
                    return [e.dma_start(
                        out=WB[1][:, 0:2048].rearrange("p (c f) -> p c f", c=8),
                        in_=dt["l1_wkv_h"][:, hg * 256:(hg + 1) * 256]
                        .rearrange("(c p) f -> p c f", p=128))]
                dk = [t_gr] + prev_scanh + ([t_hw["S_WB"]] if t_hw["S_WB"] else [])
                t_wkv = p.dma("gp", f_wkv, deps=dk, sem="S_WB")

                def f_wks(e):
                    r = None
                    for cc in range(NCH):
                        r = e.tensor_scalar(out=WB[1][:, cc * 256:(cc + 1) * 256],
                                            in0=WB[1][:, cc * 256:(cc + 1) * 256],
                                            scalar1=GCOLR[:, cc:cc + 1],
                                            scalar2=None, op0=OP.mult)
                    return r
                t_wks = p.emit("dve", f_wks, deps=[t_wkv])

                VSTH = SLOTA[:, hg * 2048:(hg + 1) * 2048]
                tqh = []
                for qb in range(4):
                    def f_mm(e, qb=qb):
                        r = None
                        for c in range(NCH):
                            r = e.matmul(PS[4 + qb % 2][:, :],
                                         WB[0][:, c * 128:(c + 1) * 128],
                                         xn(c)[:, qb * 512:(qb + 1) * 512],
                                         start=(c == 0), stop=(c == NCH - 1))
                        return r
                    dmm = [t_wqs] + xn_toks + prev_scanh
                    if ps_freeh[4 + qb % 2] is not None:
                        dmm.append(ps_freeh[4 + qb % 2])
                    t_mm = p.emit("pe", f_mm, deps=dmm)

                    def f_ev(e, qb=qb):
                        e.scalar_tensor_tensor(
                            out=FS[:, 0:512], in0=PS[4 + qb % 2][:, :], scalar=0.125,
                            in1=CBC[:, qb * 512:(qb + 1) * 512], op0=OP.mult,
                            op1=OP.mult)
                        return e.tensor_copy(out=QH[:, qb * 512:(qb + 1) * 512],
                                             in_=FS[:, 0:512])
                    t_ev = p.emit("dve", f_ev, deps=[t_mm, t_cbc])
                    ps_freeh[4 + qb % 2] = t_ev
                    tqh.append(t_ev)
                t_hw["S_WA"] = tqh[-1]
                tkh = []
                tvh = []
                for qb in range(4):
                    def f_mmk(e, qb=qb):
                        r = None
                        for c in range(NCH):
                            r = e.matmul(PS[4 + qb % 2][0:64, :],
                                         WB[1][:, c * 256:c * 256 + 64],
                                         xn(c)[:, qb * 512:(qb + 1) * 512],
                                         start=(c == 0), stop=(c == NCH - 1))
                        for c in range(NCH):
                            r = e.matmul(PS[4 + qb % 2][64:128, :],
                                         WB[1][:, c * 256 + 128:c * 256 + 192],
                                         xn(c)[:, qb * 512:(qb + 1) * 512],
                                         start=(c == 0), stop=(c == NCH - 1))
                        return r
                    dmm = [t_wks] + xn_toks
                    if ps_freeh[4 + qb % 2] is not None:
                        dmm.append(ps_freeh[4 + qb % 2])
                    t_mm = p.emit("pe", f_mmk, deps=dmm)

                    def f_ev(e, qb=qb):
                        e.tensor_tensor(out=FS[:, 0:512], in0=PS[4 + qb % 2][:, :],
                                        in1=CBC[:, qb * 512:(qb + 1) * 512],
                                        op=OP.mult)
                        return e.tensor_copy(out=KH[:, qb * 512:(qb + 1) * 512],
                                             in_=FS[:, 0:512])
                    t_ev = p.emit("dve", f_ev, deps=[t_mm, t_cbc])
                    ps_freeh[4 + qb % 2] = t_ev
                    tkh.append(t_ev)
                for qb in range(4):
                    def f_mmv(e, qb=qb):
                        r = None
                        for c in range(NCH):
                            r = e.matmul(PS[6][0:64, :],
                                         WB[1][:, c * 256 + 64:c * 256 + 128],
                                         xn(c)[:, qb * 512:(qb + 1) * 512],
                                         start=(c == 0), stop=(c == NCH - 1))
                        for c in range(NCH):
                            r = e.matmul(PS[6][64:128, :],
                                         WB[1][:, c * 256 + 192:c * 256 + 256],
                                         xn(c)[:, qb * 512:(qb + 1) * 512],
                                         start=(c == 0), stop=(c == NCH - 1))
                        return r
                    dmm = [t_wks] + xn_toks
                    if ps_freeh[6] is not None:
                        dmm.append(ps_freeh[6])
                    t_mm = p.emit("pe", f_mmv, deps=dmm)

                    def f_ev(e, qb=qb, hg=hg, VSTH=VSTH):
                        e.tensor_tensor(out=FS[:, 0:512], in0=PS[6][:, :],
                                        in1=CBC[:, qb * 512:(qb + 1) * 512],
                                        op=OP.mult)
                        return e.tensor_copy(out=VSTH[:, qb * 512:(qb + 1) * 512],
                                             in_=FS[:, 0:512])
                    t_ev = p.emit("dve", f_ev, deps=[t_mm, t_cbc])
                    ps_freeh[6] = t_ev
                    tvh.append(t_ev)
                t_hw["S_WB"] = tvh[-1]
                t_ms = p.emit("dve", lambda e: e.memset(VAUGH[:], 1.0),
                              deps=prev_scanh)
                ttr = []
                for ct in range(16):
                    def f_tr(e, ct=ct, VSTH=VSTH):
                        return e.transpose(PS[7].bitcast(BF16)[:, 0:128],
                                           VSTH[:, ct * 128:(ct + 1) * 128], SID[:])
                    dtr = [tvh[ct // 4]]
                    if ttr:
                        dtr.append(ttr[-1])
                    if ps_freeh[7] is not None:
                        dtr.append(ps_freeh[7])
                    t_tr = p.emit("pe", f_tr, deps=dtr)

                    def f_ev(e, ct=ct):
                        e.tensor_copy(out=VAUGH[:, ct * 130:ct * 130 + 64],
                                      in_=PS[7].bitcast(BF16)[:, 0:64])
                        return e.tensor_copy(
                            out=VAUGH[:, ct * 130 + 65:ct * 130 + 129],
                            in_=PS[7].bitcast(BF16)[:, 64:128])
                    t_ev = p.emit("dve", f_ev, deps=[t_tr, t_ms])
                    ps_freeh[7] = t_ev
                    ttr.append(t_ev)
                t_oe_prev = [None, None]
                t_pt_prev = None
                scan_tail = []
                PTH = WB[0][:, 1024:1536]
                for hl in range(2):
                    h = 2 * hg + hl
                    t_pn = []
                    for qb in range(4):
                        def f_sn(e, hl=hl, h=h, qb=qb):
                            return e.matmul(PS[1][0:1, :],
                                            NKD[hl * 64:(hl + 1) * 64, h:h + 1],
                                            QH[hl * 64:(hl + 1) * 64,
                                               qb * 512:(qb + 1) * 512],
                                            start=True, stop=True)
                        dsn = [tqh[qb], t_gr]
                        if t_pn:
                            dsn.append(t_pn[-1])
                        t_s = p.emit("pe", f_sn, deps=dsn)
                        t_e = p.emit("act", lambda e, qb=qb: e.activation(
                            out=PNUL[:, qb * 512:(qb + 1) * 512],
                            in_=PS[1][0:1, :], func=AF.Exp), deps=[t_s])
                        t_pn.append(t_e)
                    for qseg in range(4):
                        davn = [t_pn[qseg], t_gr]
                        if t_oe_prev[hl] is not None:
                            davn.append(t_oe_prev[hl])
                        t_av = p.emit("pe", lambda e, hl=hl, h=h, qseg=qseg:
                                      e.matmul(PS[2 + hl][0:65, :],
                                               NVAUG[:, h * 65:(h + 1) * 65],
                                               PNUL[:, qseg * 512:(qseg + 1) * 512],
                                               start=True, stop=False), deps=davn)
                        for kt in range(16):
                            def f_s(e, hl=hl, kt=kt, qseg=qseg):
                                return e.matmul(
                                    PS[0][:, :],
                                    KH[hl * 64:(hl + 1) * 64, kt * 128:(kt + 1) * 128],
                                    QH[hl * 64:(hl + 1) * 64,
                                       qseg * 512:(qseg + 1) * 512],
                                    start=True, stop=True)
                            ds = [tqh[qseg], tkh[kt // 4]]
                            if t_pt_prev is not None:
                                ds.append(t_pt_prev)
                            t_s = p.emit("pe", f_s, deps=ds)
                            t_pt = p.emit("act", lambda e: e.activation(
                                out=PTH, in_=PS[0][:, :], func=AF.Exp), deps=[t_s])
                            t_pt_prev = t_pt
                            t_av = p.emit("pe", lambda e, hl=hl, kt=kt: e.matmul(
                                PS[2 + hl][0:65, :],
                                VAUGH[:, kt * 130 + hl * 65:kt * 130 + (hl + 1) * 65],
                                PTH, start=False, stop=(kt == 15)),
                                deps=[t_pt, ttr[kt], t_av])

                        def f_oe(e, hg=hg, hl=hl, qseg=qseg):
                            e.tensor_copy(
                                out=SLOTA[hl * 64:(hl + 1) * 64,
                                          hg * 2048 + qseg * 512:
                                          hg * 2048 + (qseg + 1) * 512],
                                in_=PS[2 + hl][0:64, :])
                            return e.tensor_copy(
                                out=DEN[hl * 64:hl * 64 + 1,
                                        qseg * 512:(qseg + 1) * 512],
                                in_=PS[2 + hl][64:65, :])
                        t_oe = p.emit("dve", f_oe, deps=[t_av])
                        t_oe_prev[hl] = t_oe
                        scan_tail.append(t_oe)
                t_dd = p.dma("sp", lambda e, hg=hg: [
                    e.dma_start(out=dscr[24 + 2 * hg:24 + 2 * hg + 1, :],
                                in_=DEN[0:1, :]),
                    e.dma_start(out=dscr[24 + 2 * hg + 1:24 + 2 * hg + 2, :],
                                in_=DEN[64:65, :])],
                    deps=scan_tail, sem="S_B", n=2)
                den_dumph.append(t_dd)
                prev_scanh = scan_tail + [t_dd]
            OH = SLOTA[:, 0:8192]
            t_normh = []
            DBC2 = FA[:, 4096:6144]
            for hg in range(4):
                t_di = p.dma("sp", lambda e, hg=hg: [
                    e.dma_start(out=DBC2[0:64, :],
                                in_=dscr[24 + 2 * hg:24 + 2 * hg + 1, :]
                                .partition_broadcast(64)),
                    e.dma_start(out=DBC2[64:128, :],
                                in_=dscr[24 + 2 * hg + 1:24 + 2 * hg + 2, :]
                                .partition_broadcast(64))],
                    deps=[den_dumph[hg]] + (t_normh[-1:] if t_normh else []),
                    sem="S_B", n=2)

                def f_nm(e, hg=hg):
                    e.tensor_scalar(out=DBC2, in0=DBC2, scalar1=1e-6, scalar2=None,
                                    op0=OP.add)
                    recip_d(e, DBC2, DBC2)
                    return e.tensor_tensor(
                        out=OH[:, hg * 2048:(hg + 1) * 2048],
                        in0=OH[:, hg * 2048:(hg + 1) * 2048], in1=DBC2, op=OP.mult)
                t_nm = p.emit("dve", f_nm, deps=[t_di] + prev_scanh)
                t_normh.append(t_nm)
            SELB = FA[:, 2048:4096]
            t_selb = p.dma("sp", lambda e: [e.dma_start(
                out=SELB, in_=dscr[32:33, :].partition_broadcast(128))],
                deps=[t_selo] + x2_toks + t_normh, sem="S_B")
            g_toks1 = []
            t_prev = None
            for dchunk in range(NCH):
                b = dchunk % 2

                def f_wl(e, dchunk=dchunk, b=b):
                    return [e.dma_start(
                        out=WB[b][:, 0:512].rearrange("p (c f) -> p c f", c=4),
                        in_=dt["l1_wout_h"][:, dchunk * 128:(dchunk + 1) * 128]
                        .rearrange("(c p) f -> p c f", p=128))]
                dps = [t_gr] + [t for t in
                                (t_hw["S_WA"], t_hw["S_WB"]) if t]
                if t_prev is not None:
                    dps.append(t_prev)
                t_wl = p.dma("gp", f_wl, deps=dps,
                             sem=("S_WA" if b == 0 else "S_WB"))
                for qb in range(4):
                    def f_mm(e, qb=qb, b=b):
                        r = None
                        for ec in range(4):
                            r = e.matmul(PS[qb][:, :], WB[b][:, ec * 128:(ec + 1) * 128],
                                         OH[:, ec * 2048 + qb * 512:
                                            ec * 2048 + (qb + 1) * 512],
                                         start=(ec == 0), stop=(ec == 3))
                        return r
                    dmm = [t_wl] + t_normh
                    if g_toks1:
                        dmm.append(g_toks1[-1])
                    t_mm = p.emit("pe", f_mm, deps=dmm)
                    gstg = SLOTA[:, 8192 + (qb % 2) * 512:8192 + (qb % 2) * 512 + 512]

                    def f_g(e, qb=qb, dchunk=dchunk):
                        e.tensor_scalar(out=FS[:, 0:512], in0=PS[qb][:, :],
                                        scalar1=NQCOL[:, dchunk:dchunk + 1],
                                        scalar2=None, op0=OP.subtract)
                        e.tensor_tensor(out=FS[:, 0:512], in0=FS[:, 0:512],
                                        in1=SELB[:, qb * 512:(qb + 1) * 512],
                                        op=OP.mult)
                        e.tensor_tensor(out=FS[:, 0:512], in0=FS[:, 0:512],
                                        in1=LOUT[:, dchunk * 2048 + qb * 512:
                                                 dchunk * 2048 + (qb + 1) * 512],
                                        op=OP.add)
                        return e.tensor_scalar(out=FS[:, 0:512], in0=FS[:, 0:512],
                                               scalar1=NQCOL[:, dchunk:dchunk + 1],
                                               scalar2=None, op0=OP.add)
                    t_gv = p.emit("dve", f_g, deps=[t_mm, t_selb, t_gr])
                    t_gs = p.emit("act", lambda e, gstg=gstg: e.activation(
                        out=gstg, in_=FS[:, 0:512], func=AF.Silu), deps=[t_gv])
                    t_sp = p.dma("gp", lambda e, dchunk=dchunk, qb=qb, gstg=gstg: [
                        e.dma_start(out=gscr[:, dchunk * T + qb * 512:
                                             dchunk * T + (qb + 1) * 512],
                                    in_=gstg)], deps=[t_gs], sem="S_G")
                    g_toks1.append(t_sp)
                t_prev = g_toks1[-1]
            return emit_mha([t_gr] + g_toks1[-1:], xn_toks, g_toks1[-1],
                            dt["l1_wq_a"], dt["l1_wk_a"], dt["l1_wv_a"],
                            dt["l1_wo"], 0.125)

        if nl >= 2:
            comb = emit_l1(comb)
            LAYER_MARKS.append(dict(p.cnt))

        def bb3(a3, row2d):
            return bass.broadcast_tensor_aps(
                a3, row2d.rearrange("p (t o) -> p t o", o=1))

        def emit_router_layer(prev_comb, wkey, gwt_ap):
            pass

        def emit_l2(prev_comb):
            t_bc = emit_ln(prev_comb)
            GWT = FA[:, 6144:6208]
            LOG = FA[:, 6208:6336]
            SIDF3 = FA[:, 6336:6464]
            IOTT3 = FA[:, 6464:6592]
            EQ = FA[:, 6592:6720]
            TMP = FA[:, 6720:6848]
            BQK = FA[:, 7904:7928]
            BO2 = FA[:, 7928:7936]
            t_w0 = p.dma("sp", lambda e: [
                e.dma_start(out=GWT, in_=dt["l2_gwT"]),
                e.dma_start(out=BQK, in_=dt["l2_bqkv"]),
                e.dma_start(out=BO2, in_=dt["l2_bproj"]),
                e.dma_start(out=IOTT3, in_=dt["iott"])],
                deps=list(prev_comb) + [t_bc], sem="S_B", n=4)
            t_li = p.emit("dve", lambda e: e.memset(LOG, 0.0), deps=[t_bc])
            t_si3 = p.emit("dve", lambda e: e.tensor_copy(out=SIDF3, in_=SID[:]),
                           deps=[t_bc])

            def per_chunk(c, t_fs):
                def f_r(e, c=c):
                    r = None
                    for tt in range(16):
                        r = e.matmul(PS[5][:, tt * 8:(tt + 1) * 8],
                                     FS[:, tt * 128:(tt + 1) * 128],
                                     GWT[:, c * 8:(c + 1) * 8],
                                     start=True, stop=True)
                    return r
                t_r = p.emit("pe", f_r, deps=[t_fs, t_w0])
                t_a = p.emit("dve", lambda e: e.tensor_tensor(
                    out=LOG, in0=LOG, in1=PS[5][:, 0:128], op=OP.add),
                    deps=[t_r, t_li])
                return [t_a]
            xn_toks, last_us = emit_xn(t_bc, list(prev_comb), per_chunk)

            L3d = LOG.rearrange("p (t e) -> p t e", e=8)
            MX = ROWS[:, 8:24]
            RW = ROWS[:, 24:40]
            FMIN = ROWS[:, 40:56]
            t_mx = p.emit("dve", lambda e: e.tensor_reduce(
                out=MX, in_=L3d, axis=AX.X, op=OP.max), deps=last_us + xn_toks)
            t_rw = p.emit("act", lambda e: e.activation(
                out=RW, in_=MX, func=AF.Sigmoid), deps=[t_mx])

            def f_eq(e):
                i0, i1 = bb3(L3d, MX)
                e.tensor_tensor(out=EQ.rearrange("p (t e) -> p t e", e=8),
                                in0=i0, in1=i1, op=OP.is_equal)
                e.tensor_tensor(out=TMP, in0=EQ, in1=IOTT3, op=OP.mult)
                e.tensor_scalar(out=TMP, in0=TMP, scalar1=-1.0, scalar2=8.0,
                                op0=OP.mult, op1=OP.add)
                return e.tensor_reduce(
                    out=FMIN, in_=TMP.rearrange("p (t e) -> p t e", e=8),
                    axis=AX.X, op=OP.min)
            t_eq = p.emit("dve", f_eq, deps=[t_mx, t_w0])

            def f_msk(e):
                i0, i1 = bb3(TMP.rearrange("p (t e) -> p t e", e=8), FMIN)
                return e.tensor_tensor(
                    out=EQ.rearrange("p (t e) -> p t e", e=8),
                    in0=i0, in1=i1, op=OP.is_equal)
            t_msk = p.emit("dve", f_msk, deps=[t_eq])

            MSKR = FA[0:8, 0:2048]
            t_mt = []
            for tt in range(16):
                t_tr = p.emit("pe", lambda e, tt=tt: e.transpose(
                    PS[6][0:8, 0:128], EQ[:, tt * 8:(tt + 1) * 8], SIDF3),
                    deps=[t_msk, t_si3] + (t_mt[-1:] if t_mt else []))
                t_cp = p.emit("dve", lambda e, tt=tt: e.tensor_copy(
                    out=MSKR[:, tt * 128:(tt + 1) * 128], in_=PS[6][0:8, 0:128]),
                    deps=[t_tr])
                t_mt.append(t_cp)
            t_rt = p.emit("pe", lambda e: e.transpose(
                PS[6][0:16, 128:256], RW, SIDF3), deps=[t_rw, t_si3] + t_mt[-1:])
            R16C = FA[0:16, 2048:2176]
            t_rc = p.emit("dve", lambda e: e.tensor_copy(
                out=R16C, in_=PS[6][0:16, 128:256]), deps=[t_rt])
            t_msko = p.dma("sp", lambda e: [
                e.dma_start(out=dscr[0:8, :], in_=MSKR),
                e.dma_start(out=dscr[8:9, :].rearrange("o (t p) -> (o t) p", t=16),
                            in_=R16C)], deps=t_mt + [t_rc], sem="S_B", n=2)
            RWB = FA[:, 2048:4096]
            t_rwb = p.dma("sp", lambda e: [e.dma_start(
                out=RWB, in_=dscr[8:9, :].partition_broadcast(128))],
                deps=[t_msko], sem="S_B")

            def xr(c):
                return SLOTA[:, c * T:(c + 1) * T]
            xr_toks = []
            t_px = None
            for c in range(NCH):
                t_px = p.emit("dve", lambda e, c=c: e.tensor_tensor(
                    out=xr(c), in0=xn(c), in1=RWB, op=OP.mult),
                    deps=[t_rwb] + xn_toks + ([t_px] if t_px else []))
                xr_toks.append(t_px)

            MBC = FA[:, 0:512]
            H3 = SLOT2
            g2_toks = []
            t_wp = {"S_WA": None, "S_WB": None}
            t_hp = None
            t_spp = None
            for tb in range(4):
                t0 = tb * 512
                h_toks = []
                for ex in range(8):
                    t_mb = p.dma("sp", lambda e, ex=ex, t0=t0: [e.dma_start(
                        out=MBC, in_=dscr[ex:ex + 1, t0:t0 + 512]
                        .partition_broadcast(128))],
                        deps=[t_msko] + (h_toks[-1:] if h_toks else []) +
                        ([t_spp] if t_spp else []), sem="S_B")
                    for ph, wsrc in ((0, "l2_wgT"), (1, "l2_wuT")):
                        for oh in range(2):
                            b = oh
                            sname = "S_WA" if b == 0 else "S_WB"

                            def f_wl(e, ex=ex, oh=oh, b=b, wsrc=wsrc):
                                return [e.dma_start(
                                    out=WB[b][:, 0:2048].rearrange(
                                        "p (c f) -> p c f", c=8),
                                    in_=dt[wsrc + str(ex)][:, oh * 256:(oh + 1) * 256]
                                    .rearrange("(c p) f -> p c f", p=128))]
                            t_wl = p.dma("gp", f_wl, deps=list(prev_comb) +
                                         ([t_wp[sname]] if t_wp[sname] else []),
                                         sem=sname)
                            for sub in range(2):
                                hslot = H3[:, ex * 2048 + (oh * 2 + sub) * 512:
                                           ex * 2048 + (oh * 2 + sub) * 512 + 512]

                                def f_mm(e, b=b, sub=sub, t0=t0):
                                    r = None
                                    for c in range(NCH):
                                        r = e.matmul(
                                            PS[0][:, :],
                                            WB[b][:, c * 256 + sub * 128:
                                                  c * 256 + (sub + 1) * 128],
                                            xr(c)[:, t0:t0 + 512],
                                            start=(c == 0), stop=(c == NCH - 1))
                                    return r
                                t_mm = p.emit("pe", f_mm, deps=[t_wl] + xr_toks +
                                              ([t_hp] if t_hp else []))
                                if ph == 0:
                                    t_h = p.emit("act", lambda e, hslot=hslot:
                                                 e.activation(out=hslot,
                                                              in_=PS[0][:, :],
                                                              func=AF.Silu),
                                                 deps=[t_mm])
                                else:
                                    def f_h(e, hslot=hslot):
                                        e.tensor_tensor(out=FS[:, 0:512],
                                                        in0=PS[0][:, :],
                                                        in1=hslot, op=OP.mult)
                                        e.tensor_tensor(out=FS[:, 0:512],
                                                        in0=FS[:, 0:512],
                                                        in1=MBC, op=OP.mult)
                                        return e.tensor_copy(out=hslot,
                                                             in_=FS[:, 0:512])
                                    t_h = p.emit("dve", f_h, deps=[t_mm, t_mb])
                                t_hp = t_h
                                h_toks.append(t_h)
                            t_wp[sname] = t_hp
                for dchunk in range(NCH):
                    par = dchunk % 2
                    t_dm = None
                    for ex in range(8):
                        b = ex % 2
                        sname = "S_WA" if b == 0 else "S_WB"

                        def f_wl(e, ex=ex, dchunk=dchunk, b=b):
                            return [e.dma_start(
                                out=WB[b][:, 0:512].rearrange(
                                    "p (c f) -> p c f", c=4),
                                in_=dt["l2_wdT" + str(ex)][:, dchunk * 128:
                                                           (dchunk + 1) * 128]
                                .rearrange("(c p) f -> p c f", p=128))]
                        t_wd = p.dma("gp", f_wl, deps=h_toks[-1:] +
                                     ([t_wp[sname]] if t_wp[sname] else []),
                                     sem=sname)

                        def f_dmm(e, ex=ex, par=par, b=b):
                            r = None
                            for oc in range(4):
                                r = e.matmul(
                                    PS[2 + par][:, :],
                                    WB[b][:, oc * 128:(oc + 1) * 128],
                                    H3[:, ex * 2048 + oc * 512:
                                       ex * 2048 + (oc + 1) * 512],
                                    start=(ex == 0 and oc == 0),
                                    stop=(ex == 7 and oc == 3))
                            return r
                        ddm = [t_wd] + h_toks + ([t_dm] if t_dm else [])
                        if ex == 0 and t_spp is not None:
                            ddm.append(t_spp)
                        t_dm = p.emit("pe", f_dmm, deps=ddm)
                        t_wp[sname] = t_dm
                    t_gs = p.emit("act", lambda e, par=par: e.activation(
                        out=FS[:, 0:512], in_=PS[2 + par][:, :], func=AF.Silu),
                        deps=[t_dm] + ([t_spp] if t_spp else []))
                    t_sp = p.dma("gp", lambda e, dchunk=dchunk, t0=t0: [
                        e.dma_start(out=gscr[:, dchunk * T + t0:
                                             dchunk * T + t0 + 512],
                                    in_=FS[:, 0:512])], deps=[t_gs], sem="S_G")
                    t_spp = t_sp
                    g2_toks.append(t_sp)
            BQC = lambda g2: BQK[:, g2:g2 + 1]
            BKC = lambda g2: BQK[:, 8 + g2:8 + g2 + 1]
            BVC = lambda g2: BQK[:, 16 + g2:16 + g2 + 1]
            BOC = lambda dchunk: BO2[:, dchunk:dchunk + 1]
            return emit_mha([t_w0] + g2_toks[-1:], xn_toks, g2_toks[-1],
                            dt["l2_wq_a"], dt["l2_wk_a"], dt["l2_wv_a"],
                            dt["l2_wproj"], 0.125, causal=True,
                            bq_col=BQC, bk_col=BKC, bv_col=BVC, bo_col=BOC)

        if nl >= 3:
            comb = emit_l2(comb)
            LAYER_MARKS.append(dict(p.cnt))

        def emit_l3(prev_comb):
            t_bc = emit_ln(prev_comb)
            RWT = FA[:, 6144:6208]
            LOG = FA[:, 6208:6336]
            SIDF3 = FA[:, 6336:6464]
            IOTT3 = FA[:, 6464:6592]
            EQ = FA[:, 6592:6720]
            TMP = FA[:, 6720:6848]
            M1 = FA[:, 6848:6976]
            M2 = FA[:, 6976:7104]
            L2ND = FA[:, 7104:7232]
            RBT = FA[:, 7232:7360]
            GWB = FA[:, 7360:7872]
            BO3 = ROWS[:, 56:64]
            PRIOR = ROWS[0:8, 0:1]
            V1 = ROWS[:, 8:24]
            V2 = ROWS[:, 24:40]
            FMIN = ROWS[:, 40:56]
            t_w0 = p.dma("sp", lambda e: [
                e.dma_start(out=RWT, in_=dt["l3_rwT"]),
                e.dma_start(out=RBT, in_=dt["l3_rbt"].partition_broadcast(128)),
                e.dma_start(out=BO3, in_=dt["l3_bo"]),
                e.dma_start(out=PRIOR, in_=dt["prior_mask"]),
                e.dma_start(out=IOTT3, in_=dt["iott"])],
                deps=list(prev_comb) + [t_bc], sem="S_B", n=5)
            t_li = p.emit("dve", lambda e: e.memset(LOG, 0.0), deps=[t_bc])
            t_si3 = p.emit("dve", lambda e: e.tensor_copy(out=SIDF3, in_=SID[:]),
                           deps=[t_bc])

            def per_chunk(c, t_fs):
                def f_r(e, c=c):
                    r = None
                    for tt in range(16):
                        r = e.matmul(PS[5][:, tt * 8:(tt + 1) * 8],
                                     FS[:, tt * 128:(tt + 1) * 128],
                                     RWT[:, c * 8:(c + 1) * 8],
                                     start=True, stop=True)
                    return r
                t_r = p.emit("pe", f_r, deps=[t_fs, t_w0])
                t_a = p.emit("dve", lambda e: e.tensor_tensor(
                    out=LOG, in0=LOG, in1=PS[5][:, 0:128], op=OP.add),
                    deps=[t_r, t_li])
                return [t_a]
            xn_toks, last_us = emit_xn(t_bc, list(prev_comb), per_chunk)
            t_lb = p.emit("dve", lambda e: e.tensor_tensor(
                out=LOG, in0=LOG, in1=RBT, op=OP.add),
                deps=last_us + xn_toks + [t_w0])

            L3d = LOG.rearrange("p (t e) -> p t e", e=8)
            t_v1 = p.emit("dve", lambda e: e.tensor_reduce(
                out=V1, in_=L3d, axis=AX.X, op=OP.max), deps=[t_lb])

            def fmatch(dst, src, vals, deps):
                def f_eq(e):
                    i0, i1 = bb3(src.rearrange("p (t e) -> p t e", e=8), vals)
                    e.tensor_tensor(out=EQ.rearrange("p (t e) -> p t e", e=8),
                                    in0=i0, in1=i1, op=OP.is_equal)
                    e.tensor_tensor(out=TMP, in0=EQ, in1=IOTT3, op=OP.mult)
                    e.tensor_scalar(out=TMP, in0=TMP, scalar1=-1.0, scalar2=8.0,
                                    op0=OP.mult, op1=OP.add)
                    return e.tensor_reduce(
                        out=FMIN, in_=TMP.rearrange("p (t e) -> p t e", e=8),
                        axis=AX.X, op=OP.min)
                t_e = p.emit("dve", f_eq, deps=deps)

                def f_m(e):
                    i0, i1 = bb3(TMP.rearrange("p (t e) -> p t e", e=8), FMIN)
                    return e.tensor_tensor(
                        out=dst.rearrange("p (t e) -> p t e", e=8),
                        in0=i0, in1=i1, op=OP.is_equal)
                return p.emit("dve", f_m, deps=[t_e])
            t_m1 = fmatch(M1, LOG, V1, [t_v1, t_w0])

            def f_l2(e):
                e.tensor_scalar(out=L2ND, in0=M1, scalar1=1e30, scalar2=None,
                                op0=OP.mult)
                e.tensor_tensor(out=L2ND, in0=LOG, in1=L2ND, op=OP.subtract)
                return e.tensor_reduce(out=V2,
                                       in_=L2ND.rearrange("p (t e) -> p t e", e=8),
                                       axis=AX.X, op=OP.max)
            t_v2 = p.emit("dve", f_l2, deps=[t_m1])
            t_m2 = fmatch(M2, L2ND, V2, [t_v2])

            t_d = p.emit("dve", lambda e: e.tensor_tensor(
                out=V2, in0=V2, in1=V1, op=OP.subtract), deps=[t_m2])
            t_ed = p.emit("act", lambda e: e.activation(
                out=V2, in_=V2, func=AF.Exp), deps=[t_d])

            def f_p(e):
                e.tensor_scalar(out=FMIN, in0=V2, scalar1=1.0, scalar2=None,
                                op0=OP.add)
                recip_d(e, FMIN, FMIN)
                return e.tensor_tensor(out=V2, in0=V2, in1=FMIN, op=OP.mult)
            t_p = p.emit("dve", f_p, deps=[t_ed])

            AEQ = EQ
            WTT = TMP

            def f_aw(e):
                e.tensor_tensor(out=AEQ, in0=M1, in1=M2, op=OP.add)
                i0, i1 = bb3(M1.rearrange("p (t e) -> p t e", e=8), FMIN)
                e.tensor_tensor(out=M1.rearrange("p (t e) -> p t e", e=8),
                                in0=i0, in1=i1, op=OP.mult)
                i0, i1 = bb3(M2.rearrange("p (t e) -> p t e", e=8), V2)
                e.tensor_tensor(out=M2.rearrange("p (t e) -> p t e", e=8),
                                in0=i0, in1=i1, op=OP.mult)
                return e.tensor_tensor(out=WTT, in0=M1, in1=M2, op=OP.add)
            t_aw = p.emit("dve", f_aw, deps=[t_p])

            AR = FA[0:8, 0:2048]
            WR = FA[0:8, 2048:4096]
            t_mt = []
            for tt in range(16):
                def f_tr(e, tt=tt):
                    e.transpose(PS[6][0:8, 0:128], AEQ[:, tt * 8:(tt + 1) * 8],
                                SIDF3)
                    return e.transpose(PS[6][0:8, 128:256],
                                       WTT[:, tt * 8:(tt + 1) * 8], SIDF3)
                t_tr = p.emit("pe", f_tr, deps=[t_aw, t_si3] +
                              (t_mt[-1:] if t_mt else []))

                def f_cp(e, tt=tt):
                    e.tensor_copy(out=AR[:, tt * 128:(tt + 1) * 128],
                                  in_=PS[6][0:8, 0:128])
                    return e.tensor_copy(out=WR[:, tt * 128:(tt + 1) * 128],
                                         in_=PS[6][0:8, 128:256])
                t_cp = p.emit("dve", f_cp, deps=[t_tr])
                t_mt.append(t_cp)

            ZR8 = FA[0:8, 4096:6144]
            PRE3 = FA[0:8, 6208:8256]

            def f_pre(e):
                e.memset(ZR8, 0.0)
                return e.tensor_tensor_scan(out=PRE3, data0=AR, data1=ZR8,
                                            initial=0.0, op0=OP.add, op1=OP.add)
            t_pre = p.emit("dve", f_pre, deps=t_mt)
            t_cnt = p.dma("gp", lambda e: [e.dma_start(
                out=coll_in.rearrange("o e -> e o"), in_=PRE3[:, 2047:2048])],
                deps=[t_pre], sem="S_C")
            t_coll = p.emit("gp", lambda e: e.collective_compute(
                "AllGather", OP.bypass, ins=[coll_in], outs=[coll_out],
                replica_groups=[[0, 1, 2, 3, 4, 5, 6, 7]]), deps=[t_cnt],
                sem="S_Q", inc=1)
            CNT = FA[0:8, 8256:8264]
            t_cl = p.dma("gp", lambda e: [e.dma_start(out=CNT, in_=coll_out)],
                         deps=[t_coll], sem="S_C")
            OFFC = FA[0:8, 8264:8265]
            t_off = p.emit("pe", lambda e: e.matmul(
                PS[6][0:8, 0:1], CNT, PRIOR, start=True, stop=True),
                deps=[t_cl, t_w0])
            t_oc = p.emit("dve", lambda e: e.tensor_copy(
                out=OFFC, in_=PS[6][0:8, 0:1]), deps=[t_off])

            def f_k(e):
                e.tensor_scalar(out=PRE3, in0=PRE3, scalar1=OFFC, scalar2=2048.0,
                                op0=OP.add, op1=OP.is_le)
                e.tensor_tensor(out=AR, in0=AR, in1=PRE3, op=OP.mult)
                return e.tensor_tensor(out=WR, in0=WR, in1=AR, op=OP.mult)
            t_k = p.emit("dve", f_k, deps=[t_oc, t_pre])
            t_gwo = p.dma("sp", lambda e: [e.dma_start(out=dscr[8:16, :],
                                                       in_=WR)],
                          deps=[t_k], sem="S_B")

            H33 = SLOTA
            ACC2 = SLOT2
            GWB4 = FA[:, 6208:8256]
            SBF = FA[:, 8256:8768]
            g3_toks = []
            t_wp = {"S_WA": None, "S_WB": None}
            t_hp = None
            t_adl = None
            t_az = p.emit("dve", lambda e: e.memset(ACC2, 0.0), deps=[t_k])
            for ex in range(8):
                t_gw = p.dma("sp", lambda e, ex=ex: [e.dma_start(
                    out=GWB4, in_=dscr[8 + ex:9 + ex, :]
                    .partition_broadcast(128))],
                    deps=[t_gwo] + ([t_hp] if t_hp else []), sem="S_B")
                for half in range(2):
                    h_toks = []
                    for hc8 in range(8):
                        hcg = half * 8 + hc8
                        b = hc8 % 2
                        sname = "S_WA" if b == 0 else "S_WB"

                        def f_wl(e, ex=ex, hcg=hcg, b=b):
                            return [e.dma_start(
                                out=WB[b][:, 0:1024].rearrange(
                                    "p (c f) -> p c f", c=8),
                                in_=dt["l3_W" + str(ex)][:, hcg * 128:
                                                         (hcg + 1) * 128]
                                .rearrange("(c p) f -> p c f", p=128)),
                                e.dma_start(
                                out=WB[b][:, 1024:2048].rearrange(
                                    "p (c f) -> p c f", c=8),
                                in_=dt["l3_V" + str(ex)][:, hcg * 128:
                                                         (hcg + 1) * 128]
                                .rearrange("(c p) f -> p c f", p=128))]
                        t_wl = p.dma("gp", f_wl, deps=[t_k] +
                                     ([t_wp[sname]] if t_wp[sname] else []),
                                     sem=sname, n=2)

                        def f_amm(e, b=b):
                            r = None
                            for tb4 in range(4):
                                for c in range(NCH):
                                    r = e.matmul(PS[tb4][:, :],
                                                 WB[b][:, c * 128:(c + 1) * 128],
                                                 xn(c)[:, tb4 * 512:
                                                       (tb4 + 1) * 512],
                                                 start=(c == 0),
                                                 stop=(c == NCH - 1))
                            for tb4 in range(4):
                                for c in range(NCH):
                                    r = e.matmul(PS[4 + tb4][:, :],
                                                 WB[b][:, 1024 + c * 128:
                                                       1024 + (c + 1) * 128],
                                                 xn(c)[:, tb4 * 512:
                                                       (tb4 + 1) * 512],
                                                 start=(c == 0),
                                                 stop=(c == NCH - 1))
                            return r
                        dam = [t_wl] + xn_toks + ([t_hp] if t_hp else [])
                        if t_adl is not None and hc8 == 0:
                            dam.append(t_adl)
                        t_am = p.emit("pe", f_amm, deps=dam)
                        for tb4 in range(4):
                            t_sb = p.emit("act", lambda e, tb4=tb4: e.activation(
                                out=SBF, in_=PS[4 + tb4][:, :], func=AF.Silu),
                                deps=[t_am] + ([t_hp] if t_hp else []))

                            def f_h3(e, hc8=hc8, tb4=tb4):
                                e.tensor_tensor(out=FS[:, 0:512],
                                                in0=PS[tb4][:, :],
                                                in1=SBF, op=OP.mult)
                                e.tensor_tensor(out=FS[:, 0:512],
                                                in0=FS[:, 0:512],
                                                in1=GWB4[:, tb4 * 512:
                                                         (tb4 + 1) * 512],
                                                op=OP.mult)
                                return e.tensor_copy(
                                    out=H33[:, hc8 * 2048 + tb4 * 512:
                                            hc8 * 2048 + tb4 * 512 + 512],
                                    in_=FS[:, 0:512])
                            t_h3 = p.emit("dve", f_h3, deps=[t_sb, t_gw])
                            t_hp = t_h3
                            h_toks.append(t_h3)
                        t_wp[sname] = t_hp
                    for dchunk in range(NCH):
                        b = dchunk % 2
                        sname = "S_WA" if b == 0 else "S_WB"

                        def f_wo(e, ex=ex, half=half, dchunk=dchunk, b=b):
                            return [e.dma_start(
                                out=WB[b][:, 0:1024].rearrange(
                                    "p (c f) -> p c f", c=8),
                                in_=dt["l3_Wout" + str(ex)][
                                    half * 1024:(half + 1) * 1024,
                                    dchunk * 128:(dchunk + 1) * 128]
                                .rearrange("(c p) f -> p c f", p=128))]
                        t_wo = p.dma("gp", f_wo, deps=h_toks[-1:] +
                                     ([t_wp[sname]] if t_wp[sname] else []),
                                     sem=sname)

                        def f_ymm(e, b=b):
                            r = None
                            for tb4 in range(4):
                                for hc8 in range(8):
                                    r = e.matmul(PS[tb4][:, :],
                                                 WB[b][:, hc8 * 128:
                                                       (hc8 + 1) * 128],
                                                 H33[:, hc8 * 2048 + tb4 * 512:
                                                     hc8 * 2048 +
                                                     (tb4 + 1) * 512],
                                                 start=(hc8 == 0),
                                                 stop=(hc8 == 7))
                            return r
                        dym = [t_wo] + h_toks + ([t_adl] if t_adl else [])
                        t_ym = p.emit("pe", f_ymm, deps=dym)
                        t_wp[sname] = t_ym

                        def f_ad(e, dchunk=dchunk,
                                 first=(ex == 0 and half == 0)):
                            r = None
                            for tb4 in range(4):
                                sl = ACC2[:, dchunk * 2048 + tb4 * 512:
                                          dchunk * 2048 + (tb4 + 1) * 512]
                                if first:
                                    r = e.tensor_copy(out=sl, in_=PS[tb4][:, :])
                                else:
                                    r = e.tensor_tensor(out=sl, in0=sl,
                                                        in1=PS[tb4][:, :],
                                                        op=OP.add)
                            return r
                        t_ad = p.emit("dve", f_ad, deps=[t_ym, t_az])
                        t_adl = t_ad
                        t_hp = t_ad
            t_spp = None
            for dchunk in range(NCH):
                for tb4 in range(4):
                    t_gs = p.emit("act", lambda e, dchunk=dchunk, tb4=tb4:
                                  e.activation(
                                      out=FS[:, 0:512],
                                      in_=ACC2[:, dchunk * 2048 + tb4 * 512:
                                               dchunk * 2048 + (tb4 + 1) * 512],
                                      func=AF.Silu),
                                  deps=[t_adl] + ([t_spp] if t_spp else []))
                    t_sp = p.dma("gp", lambda e, dchunk=dchunk, tb4=tb4: [
                        e.dma_start(out=gscr[:, dchunk * T + tb4 * 512:
                                             dchunk * T + (tb4 + 1) * 512],
                                    in_=FS[:, 0:512])], deps=[t_gs], sem="S_G")
                    g3_toks.append(t_sp)
                    t_spp = t_sp
            BOC = lambda dchunk: BO3[:, dchunk:dchunk + 1]
            return emit_mha([t_w0] + g3_toks[-1:], xn_toks, g3_toks[-1],
                            dt["l3_wq"], dt["l3_wk"], None, dt["l3_wo"],
                            1.0 / 32.0, v_from_k=True, bo_col=BOC)

        if nl >= 4:
            comb = emit_l3(comb)
            LAYER_MARKS.append(dict(p.cnt))

        t_st = p.dma("sp", lambda e: [e.dma_start(
            out=xout.rearrange("(c p) t -> p c t", p=128),
            in_=XF[:].rearrange("p (c t) -> p c t", c=8))],
            deps=comb, sem="S_O")
        p.emit("sp", lambda e: None, deps=[t_st], inc=0)

        block = E(nc.Block())

        @block.tensor
        def _(e):
            for f in p.th["pe"]:
                f(e)

        @block.scalar
        def _(e):
            ep = VProxy(e)
            for f in p.th["act"]:
                f(ep)

        @block.vector
        def _(e):
            ep = VProxy(e)
            for f in p.th["dve"]:
                f(ep)

        @block.sync
        def _(e):
            for f in p.th["sp"]:
                f(e)

        @block.gpsimd
        def _(e):
            for f in p.th["gp"]:
                f(e)

    return nc


def prep_inputs(inp, nl=4):
    f32 = np.float32
    base = {}
    base["ident"] = np.eye(128, dtype=BFDT)
    base["tri"] = (np.arange(128)[:, None] <= np.arange(128)[None, :]).astype(BFDT)
    if nl >= 2:
        w = np.transpose(inp["L1_attn_w_qkv"], (2, 1, 0, 3)).reshape(D, 3 * D)
        base["l1_wq_a"] = np.ascontiguousarray(w[:, 0:D], f32)
        base["l1_wk_a"] = np.ascontiguousarray(w[:, D:2 * D], f32)
        base["l1_wv_a"] = np.ascontiguousarray(w[:, 2 * D:3 * D], f32)
        base["l1_wo"] = np.ascontiguousarray(inp["L1_attn_w_o"], f32)
        base["l1_ln_g"] = np.ascontiguousarray(
            np.asarray(inp["L1_ffn_ln_g"], f32).reshape(8, 128).T)
        base["l1_ln_b"] = np.ascontiguousarray(
            np.asarray(inp["L1_ffn_ln_b"], f32).reshape(8, 128).T)
        base["l1_wqkv_l"] = np.ascontiguousarray(inp["L1_ffn_w_qkv_l"], f32)
        base["l1_wout_l"] = np.ascontiguousarray(inp["L1_ffn_w_out_l"], f32)
        base["l1_null_q"] = np.ascontiguousarray(
            np.asarray(inp["L1_ffn_null_q"], f32).reshape(8, 128).T)
        base["l1_rt_q"] = np.ascontiguousarray(
            np.asarray(inp["L1_ffn_rt_q"], f32).reshape(8, 128).T)
        base["l1_rms_g"] = np.ascontiguousarray(
            np.asarray(inp["L1_ffn_rms_g"], f32).reshape(8, 128).T)
        base["l1_wq_h"] = np.ascontiguousarray(inp["L1_ffn_w_q_h"], f32)
        base["l1_wkv_h"] = np.ascontiguousarray(
            np.asarray(inp["L1_ffn_w_kv_h"], f32).reshape(D, 1024))
        nk = np.asarray(inp["L1_ffn_null_kv"], f32)[0]
        nv = np.asarray(inp["L1_ffn_null_kv"], f32)[1]
        nkd = np.zeros((128, 8), f32)
        nkd[0:64, :] = nk.T
        nkd[64:128, :] = nk.T
        base["l1_nkd"] = nkd.astype(BFDT)
        nva = np.zeros((1, 520), f32)
        for h in range(8):
            nva[0, h * 65:h * 65 + 64] = nv[h]
            nva[0, h * 65 + 64] = 1.0
        base["l1_nvaug"] = nva.astype(BFDT)
        base["l1_wout_h"] = np.ascontiguousarray(inp["L1_ffn_w_out_h"], f32)
        kk = np.arange(128)[:, None]
        qq = np.arange(128)[None, :]
        m_prev = ((kk >= 64) & (qq < 64)).astype(BFDT)
        m_next = ((kk < 64) & (qq >= 64)).astype(BFDT)
        base["lmask"] = np.concatenate([m_prev, m_next], axis=1)
    if nl >= 3:
        w = np.asarray(inp["L2_attn_w_qkv"], f32)
        base["l2_wq_a"] = np.ascontiguousarray(w[:, 0:D])
        base["l2_wk_a"] = np.ascontiguousarray(w[:, D:2 * D])
        base["l2_wv_a"] = np.ascontiguousarray(w[:, 2 * D:3 * D])
        base["l2_wproj"] = np.ascontiguousarray(inp["L2_attn_w_proj"], f32)
        bq = np.asarray(inp["L2_attn_b_qkv"], f32).reshape(3, 16, 64)
        arr = np.zeros((128, 24), f32)
        for a_ in range(3):
            for g2 in range(8):
                arr[0:64, a_ * 8 + g2] = bq[a_, 2 * g2]
                arr[64:128, a_ * 8 + g2] = bq[a_, 2 * g2 + 1]
        base["l2_bqkv"] = arr
        base["l2_bproj"] = np.ascontiguousarray(
            np.asarray(inp["L2_attn_b_proj"], f32).reshape(8, 128).T)
        base["l2_gwT"] = np.ascontiguousarray(
            np.asarray(inp["L2_ffn_gate_w"], f32).T.reshape(8, 128, 8)
            .transpose(1, 0, 2).reshape(128, 64))
        base["iott"] = np.ascontiguousarray(
            np.broadcast_to(np.tile(8.0 - np.arange(8, dtype=f32), 16),
                            (128, 128)))
        for ex in range(8):
            base["l2_wgT" + str(ex)] = np.ascontiguousarray(
                np.asarray(inp["L2_ffn_w_gate"][ex], f32).T).astype(BFDT)
            base["l2_wuT" + str(ex)] = np.ascontiguousarray(
                np.asarray(inp["L2_ffn_w_up"][ex], f32).T).astype(BFDT)
            base["l2_wdT" + str(ex)] = np.ascontiguousarray(
                np.asarray(inp["L2_ffn_w_down"][ex], f32).T).astype(BFDT)
    if nl >= 4:
        base["l3_wq"] = np.ascontiguousarray(inp["L3_attn_Wq"], f32)
        base["l3_wk"] = np.ascontiguousarray(inp["L3_attn_Wk"], f32)
        base["l3_wo"] = np.ascontiguousarray(inp["L3_attn_Wo"], f32)
        base["l3_bo"] = np.ascontiguousarray(
            np.asarray(inp["L3_attn_bo"], f32).reshape(8, 128).T)
        base["l3_rwT"] = np.ascontiguousarray(
            np.asarray(inp["L3_ffn_router_w"], f32).T.reshape(8, 128, 8)
            .transpose(1, 0, 2).reshape(128, 64))
        base["l3_rbt"] = np.ascontiguousarray(
            np.tile(np.asarray(inp["L3_ffn_router_b"], f32), 16)[None, :])
        for ex in range(8):
            base["l3_W" + str(ex)] = np.ascontiguousarray(
                np.asarray(inp["L3_ffn_W"][ex], f32)).astype(BFDT)
            base["l3_V" + str(ex)] = np.ascontiguousarray(
                np.asarray(inp["L3_ffn_V"][ex], f32)).astype(BFDT)
            base["l3_Wout" + str(ex)] = np.ascontiguousarray(
                np.asarray(inp["L3_ffn_W_out"][ex], f32)).astype(BFDT)
    base["l0_wq"] = np.ascontiguousarray(inp["L0_attn_wq"], f32)
    base["l0_wk"] = np.ascontiguousarray(inp["L0_attn_wk"], f32)
    base["l0_wv"] = np.ascontiguousarray(inp["L0_attn_wv"], f32)
    base["l0_wo"] = np.ascontiguousarray(inp["L0_attn_wo"], f32)
    base["l0_wgT"] = np.ascontiguousarray(inp["L0_ffn_w_gate"].T, f32)
    base["l0_wuT"] = np.ascontiguousarray(inp["L0_ffn_w_up"].T, f32)
    base["l0_wdT"] = np.ascontiguousarray(inp["L0_ffn_w_down"].T, f32)
    maps = []
    for c in range(8):
        m = dict(base)
        m["xT"] = np.ascontiguousarray(inp["L0_attn_x"][c].T, f32)
        if nl >= 4:
            m["prior_mask"] = np.ascontiguousarray(
                (np.arange(8) < c).astype(f32)[:, None])
        maps.append(m)
    return maps


def kernel(**inputs):
    nl = 4
    nc = build(nl=nl)
    maps = prep_inputs(inputs, nl=nl)
    res = run_bass_kernel_spmd(nc, maps, core_ids=list(range(8)))
    out = np.stack([np.asarray(r["xout"]).T for r in res.results], axis=0)
    return np.ascontiguousarray(out, np.float32)
